# version 11
# baseline (speedup 1.0000x reference)
import os
import numpy as np
import ml_dtypes
from contextlib import ExitStack

os.environ.setdefault("GAUGE_NTFF_IGNORE_DMA", "1")
import concourse.bass as bass
import concourse.mybir as mybir
from concourse.bass_utils import run_bass_kernel_spmd

F32 = mybir.dt.float32
BF16 = mybir.dt.bfloat16
AF = mybir.ActivationFunctionType
ALU = mybir.AluOpType
AX = mybir.AxisListType
BF = ml_dtypes.bfloat16

T, D, CH = 8192, 1024, 512
DBG_UPTO = None
DBG_CHUNKS = None
NCH = T // CH
H, DH = 16, 64
LN_EPS = 1e-5


class Sched:

    def __init__(self):
        self.stages = []
        self.cur = {}

    def c(self, eng, fn):
        self.cur.setdefault(eng, []).append((fn, "c", 0))

    def d(self, eng, fn, slack=0):
        self.cur.setdefault(eng, []).append((fn, "d", slack))

    def bar(self):
        if self.cur:
            self.stages.append(self.cur)
            self.cur = {}

    def emit(self, block, sem, sem2):
        self.bar()
        ns = len(self.stages)
        bucket = [0] * ns
        bucket2 = [0] * ns
        for s, stg in enumerate(self.stages):
            for _, ops in stg.items():
                ncomp = 0
                for _, k, slack in ops:
                    if k == "d":
                        if slack:
                            bucket2[min(s + slack, ns - 1)] += 16
                        else:
                            bucket[s] += 16
                    else:
                        ncomp = 1
                bucket[s] += ncomp
        eff = [0] * (ns + 1)
        eff2 = [0] * (ns + 1)
        for s in range(ns):
            eff[s + 1] = eff[s] + bucket[s]
            eff2[s + 1] = eff2[s] + bucket2[s]
        total = eff[ns]
        total2 = eff2[ns]

        def emit_for(name, e):
            last2 = 0
            for s, stg in enumerate(self.stages):
                ops = stg.get(name)
                if not ops:
                    continue
                if eff[s]:
                    e.wait_ge(sem, eff[s])
                if eff2[s] > last2:
                    e.wait_ge(sem2, eff2[s])
                    last2 = eff2[s]
                lastc = max(
                    (i for i, (_, k, _sl) in enumerate(ops) if k == "c"),
                    default=None,
                )
                for i, (fn, k, sl) in enumerate(ops):
                    inst = fn(e)
                    if k == "d":
                        inst.then_inc(sem2 if sl else sem, 16)
                    elif i == lastc:
                        inst.then_inc(sem, 1)

        def make(nm):
            def body(e):
                emit_for(nm, e)
                if nm == "sync":
                    e.wait_ge(sem, total)
                    if total2:
                        e.wait_ge(sem2, total2)
            body.__name__ = "_"
            return body

        for name in ["sync", "tensor", "vector", "scalar", "gpsimd"]:
            getattr(block, name)(make(name))
        return total


def pipe(st, n, phases, stride=1):
    moff = max(off for off, _ in phases)
    for s in range((n - 1) * stride + moff + 1):
        for off, fn in phases:
            q, r = divmod(s - off, stride)
            if r == 0 and 0 <= q < n:
                fn(q)
        st.bar()


def emit_l0a(nc, st, stk, pre=None):
    pre = pre or {}
    def dt(nm, shape, dtype, kind=None):
        if nm in pre:
            return pre[nm]
        if kind is None:
            return nc.dram_tensor("a_" + nm, shape, dtype).ap()
        return nc.dram_tensor("a_" + nm, shape, dtype, kind=kind).ap()
    sb = lambda n_, s_, d_=F32: stk.enter_context(nc.sbuf_tensor("a_" + n_, s_, d_))
    ps = lambda n_, s_, d_=F32: stk.enter_context(nc.psum_tensor("a_" + n_, s_, d_))

    x0_d = dt("x0", [T, D], F32, kind="ExternalInput")
    wq_d = dt("wq", [128, 8, 1024], F32, kind="ExternalInput")
    wk_d = dt("wk", [128, 8, 1024], F32, kind="ExternalInput")
    wv_d = dt("wv", [128, 8, 1024], F32, kind="ExternalInput")
    wo_d = dt("wo", [128, 8, 1024], F32, kind="ExternalInput")
    bqc_d = dt("bqc", [128, 8], F32, kind="ExternalInput")
    bkc_d = dt("bkc", [128, 8], F32, kind="ExternalInput")
    boc_d = dt("boc", [128, 8], F32, kind="ExternalInput")
    bkb_d = dt("bkb", [128, 1024], F32, kind="ExternalInput")
    bvb_d = dt("bvb", [128, 1024], F32, kind="ExternalInput")
    cosb_d = dt("cosb", [128, T], F32, kind="ExternalInput")
    sinb_d = dt("sinb", [128, T], F32, kind="ExternalInput")
    ccol_d = dt("ccol", [128, 64], F32, kind="ExternalInput")
    scol_d = dt("scol", [128, 64], F32, kind="ExternalInput")
    mask_d = dt("mask", [128, 4, 512], BF16, kind="ExternalInput")
    ident_d = dt("ident", [128, 128], F32, kind="ExternalInput")
    ones64_d = dt("ones64", [1, 64], F32, kind="ExternalInput")
    onesr_d = dt("onesr", [1, 128], F32, kind="ExternalInput")
    onesc_d = dt("onesc", [128, 1], F32, kind="ExternalInput")
    cvec_d = dt("cvec", [128, 8], F32, kind="ExternalInput")
    x1f_d = dt("x1f", [D, T], F32, kind="ExternalOutput")
    x1b_d = dt("x1b", [D, T], BF16, kind="ExternalOutput")

    w_q = sb("w_q", [128, 8, 1024], BF16)
    w_k = sb("w_k", [128, 8, 1024], BF16)
    w_v = sb("w_v", [128, 8, 1024], BF16)
    w_o = sb("w_o", [128, 8, 1024], BF16)
    bqc = sb("bqc_s", [128, 8]); bkc = sb("bkc_s", [128, 8]); boc = sb("boc_s", [128, 8])
    bkb = sb("bkb_s", [128, 1024]); bvb = sb("bvb_s", [128, 1024])
    ccol = sb("ccol_s", [128, 64]); scol = sb("scol_s", [128, 64])
    mask = sb("mask_s", [128, 4, 512], BF16)
    ident = sb("ident_s", [128, 128])
    ones64 = sb("ones64_s", [1, 64]); onesr = sb("onesr_s", [1, 128]); onesc = sb("onesc_s", [128, 1])
    cosb = sb("cosb_s", [128, 512]); sinb = sb("sinb_s", [128, 512])
    cvec = sb("cvec_s", [128, 8])
    big0 = sb("big0", [128, 4096])
    x_tm = big0[:].rearrange("p (a b) -> p a b", b=1024)
    r_f = big0[:].rearrange("p (a b) -> p a b", b=512)
    bufA = sb("bufA", [128, 8, 512])
    x_fm = bufA[:]
    x1f_t = bufA[:]
    bufB = sb("bufB", [128, 8, 512], BF16)
    x_fmb = bufB[:]
    y_b = bufB[:]
    x1b_t = bufB[:]
    qc = sb("qc", [128, 8, 512], BF16); qs = sb("qs", [128, 8, 512], BF16)
    kcf = sb("kcf", [128, 8, 512], BF16); ksf = sb("ksf", [128, 8, 512], BF16)
    kct = sb("kct", [128, 4, 1024], BF16); kst = sb("kst", [128, 4, 1024], BF16)
    v_tm = sb("v_tm", [128, 4, 16, 65], BF16)
    r2 = sb("r2", [128, 2, 512])
    tmp2 = r2
    am2 = big0[:, 0:2048].bitcast(BF16).rearrange("p (a u b) -> p a u b", a=2, b=512)
    den_r = sb("den_r", [1, 512]); rec_r = sb("rec_r", [1, 512])
    rowm = sb("rowm", [1, 512]); rowv = sb("rowv", [1, 512])
    rowmm = den_r
    rowrs = sb("rowrs", [1, 512]); rowmrs = sb("rowmrs", [1, 512])
    Sf = sb("Sf", [128, 8, 2, 65]); Sb = sb("Sb", [128, 8, 2, 65], BF16)

    pp = [ps(f"pp{i}", [128, 512]) for i in range(8)]

    st.d("gpsimd", lambda e: e.dma_start(out=w_q[:], in_=wq_d))
    st.d("gpsimd", lambda e: e.dma_start(out=w_k[:], in_=wk_d))
    st.d("gpsimd", lambda e: e.dma_start(out=w_v[:], in_=wv_d))
    st.d("gpsimd", lambda e: e.dma_start(out=w_o[:], in_=wo_d))
    for s_, d_ in [(bqc, bqc_d), (bkc, bkc_d), (boc, boc_d), (bkb, bkb_d),
                   (bvb, bvb_d), (ccol, ccol_d), (scol, scol_d), (mask, mask_d),
                   (ident, ident_d), (ones64, ones64_d), (onesr, onesr_d),
                   (onesc, onesc_d), (cvec, cvec_d)]:
        st.d("sync", lambda e, s=s_, d=d_: e.dma_start(out=s[:], in_=d))
    if DBG_UPTO != 'nomemset':
        st.c("vector", lambda e: e.memset(Sf[:], 0.0))
        st.c("vector", lambda e: e.memset(Sb[:], 0.0))
        st.c("vector", lambda e: e.memset(v_tm[:, :, :, 64:65], 1.0))
    st.bar()
    if DBG_UPTO in ('stage0', 'nomemset'):
        return

    def do_chunk(c):
        c0 = c * CH
        for tt in range(4):
            st.d("sync", lambda e, tt=tt: e.dma_start(
                out=x_tm[:, tt, :], in_=x0_d[c0 + 128 * tt: c0 + 128 * (tt + 1), :]))
        st.d("sync", lambda e: e.dma_start(out=cosb[:], in_=cosb_d[:, c0:c0 + CH]))
        st.d("sync", lambda e: e.dma_start(out=sinb[:], in_=sinb_d[:, c0:c0 + CH]))
        st.bar()
        if DBG_UPTO == 'ingdma':
            return

        def ing_mm(j):
            def f(e):
                last = None
                for tt in range(4):
                    last = e.transpose(pp[j % 2][:, 128 * tt:128 * (tt + 1)],
                                       x_tm[:, tt, 128 * j:128 * (j + 1)], ident[:])
                return last
            st.c("tensor", f)

        def ing_ev(j):
            st.c("vector", lambda e: e.tensor_copy(x_fm[:, j, :], pp[j % 2][:]))

        def ing_cp(j):
            st.c("scalar", lambda e: e.copy(x_fmb[:, j, :], x_fm[:, j, :]))

        if DBG_UPTO == 'ingmm':
            pipe(st, 8, [(0, ing_mm)])
            return
        if DBG_UPTO == 'ingev1':
            def ing_ev1(j):
                st.c("vector", lambda e: e.tensor_copy(x_fm[:, j, :], pp[j % 2][:]))
            pipe(st, 8, [(0, ing_mm), (1, ing_ev1)])
            return
        pipe(st, 8, [(0, ing_mm), (1, ing_ev), (2, ing_cp)])
        if DBG_UPTO == 'ingest':
            return

        for (w_, bc_, co_, so_) in [(w_q, bqc, qc, qs), (w_k, bkc, kcf, ksf)]:
            def p_mm(m, w_=w_):
                def f(e):
                    for k in range(8):
                        last = e.matmul(pp[2 + m % 2][:], w_[:, k, 128 * m:128 * (m + 1)],
                                 x_fmb[:, k, :], start=(k == 0), stop=(k == 7))
                    return last
                st.c("tensor", f)

            def p_act(m, bc_=bc_):
                st.c("scalar", lambda e: e.activation(
                    tmp2[:, m % 2, :], pp[2 + m % 2][:], AF.Relu, bias=bc_[:, m:m + 1]))

            def p_cs(m, co_=co_, so_=so_):
                st.c("vector", lambda e: e.tensor_tensor(
                    co_[:, m, :], tmp2[:, m % 2, :], cosb[:], ALU.mult))
                st.c("gpsimd", lambda e: e.tensor_tensor(
                    so_[:, m, :], tmp2[:, m % 2, :], sinb[:], ALU.mult))

            pipe(st, 8, [(0, p_mm), (1, p_act), (2, p_cs)])
        if DBG_UPTO == 'proj':
            return

        items = [(tt, hf) for tt in range(4) for hf in range(2)]

        def kt_mm(i):
            tt, hf = items[i]
            def f(e):
                for k in range(8):
                    last = e.matmul(pp[2 + i % 2][:], x_fmb[:, k, 128 * tt:128 * (tt + 1)],
                             w_k[:, k, 512 * hf:512 * (hf + 1)],
                             start=(k == 0), stop=(k == 7))
                return last
            st.c("tensor", f)

        def kt_add(i):
            tt, hf = items[i]
            st.c("vector", lambda e: e.tensor_tensor(
                tmp2[:, i % 2, :], pp[2 + i % 2][:], bkb[:, 512 * hf:512 * (hf + 1)], ALU.add))

        def kt_cs(i):
            tt, hf = items[i]
            cc = ccol[:, 4 * c + tt: 4 * c + tt + 1]
            sc = scol[:, 4 * c + tt: 4 * c + tt + 1]
            st.c("vector", lambda e: e.tensor_scalar(
                kct[:, tt, 512 * hf:512 * (hf + 1)], tmp2[:, i % 2, :], 0.0, cc,
                ALU.max, ALU.mult))
            st.c("gpsimd", lambda e: e.tensor_scalar(
                kst[:, tt, 512 * hf:512 * (hf + 1)], tmp2[:, i % 2, :], 0.0, sc,
                ALU.max, ALU.mult))

        pipe(st, 8, [(0, kt_mm), (1, kt_add), (2, kt_cs)])

        def vt_mm(i):
            tt, hf = items[i]
            def f(e):
                for k in range(8):
                    last = e.matmul(pp[2 + i % 2][:], x_fmb[:, k, 128 * tt:128 * (tt + 1)],
                             w_v[:, k, 512 * hf:512 * (hf + 1)],
                             start=(k == 0), stop=(k == 7))
                return last
            st.c("tensor", f)

        def vt_ev(i):
            tt, hf = items[i]
            st.c("vector", lambda e: e.tensor_tensor(
                v_tm[:, tt, 8 * hf:8 * (hf + 1), 0:64],
                pp[2 + i % 2][:].rearrange("p (h x) -> p h x", x=64),
                bvb[:, 512 * hf:512 * (hf + 1)].rearrange("p (h x) -> p h x", x=64),
                ALU.add))

        pipe(st, 8, [(0, vt_mm), (1, vt_ev)])
        if DBG_UPTO == 'ktvt':
            return

        def hA(h):
            ht, hr, he = h // 2, 64 * (h % 2), 64 * (h % 2) + 64
            q = 4 * (h % 2)
            def f(e):
                last = None
                for ub in range(4):
                    e.matmul(pp[q + ub][:], kcf[hr:he, ht, 128 * ub:128 * (ub + 1)],
                             qc[hr:he, ht, :], start=True, stop=False)
                    last = e.matmul(pp[q + ub][:],
                                    ksf[hr:he, ht, 128 * ub:128 * (ub + 1)],
                                    qs[hr:he, ht, :], start=False, stop=True)
                return last
            st.c("tensor", f)

        def hB(h):
            q = 4 * (h % 2)
            def f(e):
                last = None
                for ub in range(4):
                    last = e.tensor_tensor(am2[:, h % 2, ub, :], pp[q + ub][:],
                                           mask[:, ub, :], ALU.mult)
                return last
            st.c("vector", f)

        def hC(h):
            ht, hr, he = h // 2, 64 * (h % 2), 64 * (h % 2) + 64
            q = 4 * (h % 2)
            def f(e):
                e.matmul(pp[q][hr:he, :], Sb[hr:he, ht, 0, 0:64],
                         qc[hr:he, ht, :], start=True, stop=False)
                e.matmul(pp[q][hr:he, :], Sb[hr:he, ht, 1, 0:64],
                         qs[hr:he, ht, :], start=False, stop=False)
                e.matmul(pp[q + 1][0:1, :], Sb[hr:he, ht, 0, 64:65],
                         qc[hr:he, ht, :], start=True, stop=False)
                e.matmul(pp[q + 1][0:1, :], Sb[hr:he, ht, 1, 64:65],
                         qs[hr:he, ht, :], start=False, stop=False)
                for ub in range(4):
                    e.matmul(pp[q][hr:he, :], v_tm[:, ub, h, 0:64],
                             am2[:, h % 2, ub, :], start=False, stop=(ub == 3))
                for ub in range(4):
                    e.matmul(pp[q + 1][0:1, :], v_tm[:, ub, h, 64:65],
                             am2[:, h % 2, ub, :], start=False, stop=(ub == 3))
                for ub in range(4):
                    e.matmul(pp[q + 2][hr:he, 0:65], kct[:, ub, 64 * h:64 * (h + 1)],
                             v_tm[:, ub, h, :], start=(ub == 0), stop=(ub == 3))
                last = None
                for ub in range(4):
                    last = e.matmul(pp[q + 3][hr:he, 0:65],
                                    kst[:, ub, 64 * h:64 * (h + 1)],
                                    v_tm[:, ub, h, :], start=(ub == 0), stop=(ub == 3))
                return last
            st.c("tensor", f)

        def hD(h):
            q = 4 * (h % 2)
            st.c("vector", lambda e, q=q: e.tensor_scalar(
                den_r[:], pp[q + 1][0:1, :], 1e-6, None, ALU.max))

        def hE(h):
            st.c("vector", lambda e: e.reciprocal(rec_r[:], den_r[:]))

        def hF(h):
            ht, hr, he = h // 2, 64 * (h % 2), 64 * (h % 2) + 64
            q = 4 * (h % 2)
            numf_h = big0[hr:he, 2048:2560]
            st.c("tensor", lambda e, q=q, hr=hr, he=he: e.matmul(
                pp[q + 1][hr:he, :], ones64[:], rec_r[:], start=True, stop=True))
            def f(e):
                e.tensor_copy(numf_h[:], pp[q][hr:he, :])
                e.tensor_tensor(Sf[hr:he, ht, 0, :], Sf[hr:he, ht, 0, :],
                                pp[q + 2][hr:he, 0:65], ALU.add)
                return e.tensor_tensor(Sf[hr:he, ht, 1, :], Sf[hr:he, ht, 1, :],
                                       pp[q + 3][hr:he, 0:65], ALU.add)
            st.c("vector", f)

        def hG(h):
            ht, hr, he = h // 2, 64 * (h % 2), 64 * (h % 2) + 64
            q = 4 * (h % 2)
            numf_h = big0[hr:he, 2048:2560]
            st.c("vector", lambda e: e.tensor_tensor(
                y_b[hr:he, ht, :], numf_h[:], pp[q + 1][hr:he, :], ALU.mult))
            st.c("scalar", lambda e: e.copy(Sb[hr:he, ht, 0, :], Sf[hr:he, ht, 0, :]))
            st.c("scalar", lambda e: e.copy(Sb[hr:he, ht, 1, :], Sf[hr:he, ht, 1, :]))

        pipe(st, 16, [(0, hA), (1, hB), (2, hC), (3, hD), (4, hE),
                      (5, hF), (6, hG)], stride=4)
        def o_mm(m):
            def f(e):
                for k in range(8):
                    last = e.matmul(pp[2 + m % 2][:], w_o[:, k, 128 * m:128 * (m + 1)],
                             y_b[:, k, :], start=(k == 0), stop=(k == 7))
                return last
            st.c("tensor", f)

        def o_act(m):
            st.c("scalar", lambda e: e.activation(
                tmp2[:, m % 2, :], pp[2 + m % 2][:], AF.Identity, bias=boc[:, m:m + 1]))

        def o_res(m):
            st.c("vector", lambda e: e.tensor_tensor(
                r_f[:, m, :], tmp2[:, m % 2, :], x_fm[:, m, :], ALU.add))

        pipe(st, 8, [(0, o_mm), (1, o_act), (2, o_res)])
        if DBG_UPTO == 'omm':
            return

        emit_ln(nc, st, pp, (pp[6][0:1, :], pp[7][0:1, :]), r_f, r2, rowm, rowv, rowmm, rowrs, rowmrs,
                onesc, onesr, x1f_t, x1b_t, cvec)
        for m in range(8):
            st.d("sync", lambda e, m=m: e.dma_start(
                out=x1f_d[128 * m:128 * (m + 1), c0:c0 + CH], in_=x1f_t[:, m, :]))
            st.d("scalar", lambda e, m=m: e.dma_start(
                out=x1b_d[128 * m:128 * (m + 1), c0:c0 + CH], in_=x1b_t[:, m, :]))
        st.bar()

    for c in range(DBG_CHUNKS or NCH):
        do_chunk(c)


def emit_ln(nc, st, pp, ps_stat, r_f, r2, rowm, rowv, rowmm, rowrs, rowmrs,
            onesc, onesr, outf, outb, cvec, nt=8, outb_off=False):
    stat0, stat1 = ps_stat
    """LayerNorm over the partition(feature) axis of nt [128,512] fp32 tiles.
    Writes fp32 result to outf tiles and bf16 copy to outb."""
    def l_sq(m):
        st.c("gpsimd", lambda e: e.tensor_tensor(
            r2[:, m % 2, :], r_f[:, m, :], r_f[:, m, :], ALU.mult))

    def l_mm(m):
        def f(e):
            last = e.matmul(stat0, onesc[:], r_f[:, m, :],
                            start=(m == 0), stop=(m == nt - 1))
            last = e.matmul(stat1, onesc[:], r2[:, m % 2, :],
                            start=(m == 0), stop=(m == nt - 1))
            return last
        st.c("tensor", f)

    pipe(st, nt, [(0, l_sq), (1, l_mm)])
    inv = 1.0 / (128 * nt)

    def l_rows(e):
        e.tensor_scalar(rowm[:], stat0, inv, None, ALU.mult)
        return e.tensor_scalar(rowv[:], stat1, inv, None, ALU.mult)
    st.c("vector", l_rows)
    st.bar()
    st.c("vector", lambda e: e.tensor_tensor(rowmm[:], rowm[:], rowm[:], ALU.mult))
    st.bar()
    st.c("vector", lambda e: e.tensor_tensor(rowv[:], rowv[:], rowmm[:], ALU.subtract))
    st.bar()
    st.c("scalar", lambda e: e.activation(rowrs[:], rowv[:], AF.Ln, bias=cvec[0:1, 0:1]))
    st.bar()
    st.c("scalar", lambda e: e.activation(rowrs[:], rowrs[:], AF.Exp, scale=cvec[0:1, 1:2]))
    st.bar()
    st.c("vector", lambda e: e.tensor_tensor(rowmrs[:], rowm[:], rowrs[:], ALU.mult))
    st.bar()

    def l_bc(e):
        last = e.matmul(pp[0][:], onesr[:], rowrs[:], start=True, stop=True)
        last = e.matmul(pp[1][:], onesr[:], rowmrs[:], start=True, stop=True)
        return last
    st.c("tensor", l_bc)
    st.bar()

    def l_n1(m):
        st.c("vector", lambda e: e.tensor_tensor(
            r2[:, m % 2, :], r_f[:, m, :], pp[0][:], ALU.mult))

    def l_n2(m):
        st.c("vector", lambda e: e.tensor_tensor(
            outf[:, m, :], r2[:, m % 2, :], pp[1][:], ALU.subtract))

    def l_cp(m):
        st.c("scalar", lambda e: e.copy(outb[:, m, :], outf[:, m, :]))

    if outb_off:
        pipe(st, nt, [(0, l_n1), (1, l_n2)])
    else:
        pipe(st, nt, [(0, l_n1), (1, l_n2), (2, l_cp)])


NG = 1024
NGR = T // NG


def emit_l0f(nc, st, stk, pre=None):
    pre = pre or {}
    def dt(nm, shape, dtype, kind=None):
        if nm in pre:
            return pre[nm]
        if kind is None:
            return nc.dram_tensor("f_" + nm, shape, dtype).ap()
        return nc.dram_tensor("f_" + nm, shape, dtype, kind=kind).ap()
    sb = lambda n_, s_, d_=F32: stk.enter_context(nc.sbuf_tensor("f_" + n_, s_, d_))
    ps = lambda n_, s_, d_=F32: stk.enter_context(nc.psum_tensor("f_" + n_, s_, d_))

    x1f_d = dt("x1f", [D, T], F32, kind="ExternalInput")
    x1b_d = dt("x1b", [D, T], BF16, kind="ExternalInput")
    w1r_d = dt("w1r", [8, 128, 8, 4096], F32, kind="ExternalInput")
    w3r_d = dt("w3r", [8, 128, 8, 4096], F32, kind="ExternalInput")
    w2r_d = dt("w2r", [8, 128, 16, 1024], F32, kind="ExternalInput")
    sw1r_d = dt("sw1r", [128, 8, 4096], F32, kind="ExternalInput")
    sw3r_d = dt("sw3r", [128, 8, 4096], F32, kind="ExternalInput")
    sw2r_d = dt("sw2r", [128, 16, 1024], F32, kind="ExternalInput")
    b1c_d = dt("b1c", [128, 9, 32], F32, kind="ExternalInput")
    b3c_d = dt("b3c", [128, 9, 32], F32, kind="ExternalInput")
    b2c_d = dt("b2c", [128, 9, 8], F32, kind="ExternalInput")
    gwr_d = dt("gwr", [128, 8, 8], F32, kind="ExternalInput")
    gbb_d = dt("gbb", [128, 8], F32, kind="ExternalInput")
    esel_d = dt("esel", [8, 8, 128], F32, kind="ExternalInput")
    ident_d = dt("ident", [128, 128], F32, kind="ExternalInput")
    onesr_d = dt("onesr", [1, 128], F32, kind="ExternalInput")
    onesc_d = dt("onesc", [128, 1], F32, kind="ExternalInput")
    cvec_d = dt("cvec", [128, 8], F32, kind="ExternalInput")
    x2f_d = dt("x2f", [D, T], F32, kind="ExternalOutput")
    x2b_d = dt("x2b", [D, T], BF16, kind="ExternalOutput")

    w1c = dt("w1c", [8, 128, 8, 4096], BF16)
    w3c = dt("w3c", [8, 128, 8, 4096], BF16)
    w2c = dt("w2c", [8, 128, 16, 1024], BF16)
    sw1c = dt("sw1c", [128, 8, 4096], BF16)
    sw3c = dt("sw3c", [128, 8, 4096], BF16)
    sw2c = dt("sw2c", [128, 16, 1024], BF16)

    x1bg = sb("x1bg", [128, 8, NG], BF16)
    acc = sb("acc", [128, 8, NG])
    act = sb("act", [128, 16, NG], BF16)
    wbuf = sb("wbuf", [128, 2, 2, 2, 8, 128], BF16)
    w2s = sb("w2s", [128, 16, 1024], BF16)
    combT = sb("combT", [8, NG])
    comb_b = sb("comb_b", [128, NG], BF16)
    ld = sb("ld", [128, 8, 512])
    h1s = sb("h1s", [128, 2, 2, 512], BF16)
    swsc = sb("swsc", [128, 17, 512], BF16)
    tmp2 = sb("tmpf", [128, 2, 512])
    rt_s = sb("rt_s", [128, 8, 56])
    b1c = sb("b1c_s", [128, 9, 32]); b3c = sb("b3c_s", [128, 9, 32])
    b2c = sb("b2c_s", [128, 9, 8])
    gwr = sb("gwr_s", [128, 8, 8], BF16); gbb = sb("gbb_s", [128, 8])
    esel = sb("esel_s", [8, 8, 128])
    ident = sb("ident_s", [128, 128])
    onesr = sb("onesr_s", [1, 128]); onesc = sb("onesc_s", [128, 1])
    cvec = sb("cvec_s", [128, 8])
    rcol = sb("rcol", [128, 48])
    rowm = sb("rowm", [1, 512]); rowv = sb("rowv", [1, 512])
    rowmm = sb("rowmm", [1, 512]); rowrs = sb("rowrs", [1, 512])
    rowmrs = sb("rowmrs", [1, 512])
    r2 = sb("r2", [128, 2, 512])
    x2b_t = act[:, 0:8, 0:512]

    pp = [ps(f"pp{i}", [128, 512]) for i in range(8)]

    for s_, d_ in [(b1c, b1c_d), (b3c, b3c_d), (b2c, b2c_d),
                   (gbb, gbb_d), (esel, esel_d), (ident, ident_d),
                   (onesr, onesr_d), (onesc, onesc_d), (cvec, cvec_d)]:
        st.d("sync", lambda e, s=s_, d=d_: e.dma_start(out=s[:], in_=d))
    st.d("gpsimd", lambda e: e.dma_start(out=gwr[:], in_=gwr_d))
    st.bar()

    stg_8x2048 = [acc[:].bitcast(BF16),
                  act[:].rearrange("p (a x) b -> p a (x b)", x=2)]
    stg_16x1024 = [acc[:].bitcast(BF16).rearrange("p a (x b) -> p (a x) b", b=1024),
                   act[:]]
    pieces = []
    for e in range(8):
        for p in range(2):
            pieces.append((w1r_d[e, :, :, 2048 * p:2048 * (p + 1)],
                           w1c[e, :, :, 2048 * p:2048 * (p + 1)], (8, 2048)))
            pieces.append((w3r_d[e, :, :, 2048 * p:2048 * (p + 1)],
                           w3c[e, :, :, 2048 * p:2048 * (p + 1)], (8, 2048)))
        pieces.append((w2r_d[e], w2c[e], (16, 1024)))
    for p in range(2):
        pieces.append((sw1r_d[:, :, 2048 * p:2048 * (p + 1)],
                       sw1c[:, :, 2048 * p:2048 * (p + 1)], (8, 2048)))
        pieces.append((sw3r_d[:, :, 2048 * p:2048 * (p + 1)],
                       sw3c[:, :, 2048 * p:2048 * (p + 1)], (8, 2048)))
    pieces.append((sw2r_d, sw2c, (16, 1024)))

    def pc_buf(i):
        a_, b_ = pieces[i][2]
        return (stg_8x2048 if a_ == 8 else stg_16x1024)[i % 2]

    def pc_ld(i):
        a_, b_ = pieces[i][2]
        buf = pc_buf(i)
        for a in range(a_):
            st.d("gpsimd", lambda e, a=a, buf=buf: e.dma_start(
                out=buf[:, a, :], in_=pieces[i][0][:, a, :],
                max_dma_last_dim=4096))

    def pc_stn(i):
        a_, b_ = pieces[i][2]
        buf = pc_buf(i)
        for a in range(a_):
            st.d("sync", lambda e, a=a, buf=buf: e.dma_start(
                out=pieces[i][1][:, a, :], in_=buf[:, a, :]))

    pipe(st, len(pieces), [(0, pc_ld), (1, pc_stn)])

    def do_expert(g, e_i):
        shared = (e_i == 8)
        w1src = sw1c if shared else w1c[e_i]
        w3src = sw3c if shared else w3c[e_i]
        w2src = sw2c if shared else w2c[e_i]
        NI = NG // 512

        if not shared:
            def cb_mm(q):
                st.c("tensor", lambda e, q=q: e.matmul(
                    pp[q % 2][:], esel[:, e_i, :], combT[:, 512 * q:512 * (q + 1)],
                    start=True, stop=True))

            def cb_ev(q):
                st.c("vector", lambda e, q=q: e.tensor_copy(
                    comb_b[:, 512 * q:512 * (q + 1)], pp[q % 2][:]))

            pipe(st, NI, [(0, cb_mm), (1, cb_ev)])

        st.d("sync", lambda e: e.dma_start(out=w2s[:], in_=w2src), slack=0)
        for w_ in range(2):
            for gl in range(2):
                src = (w1src if w_ == 0 else w3src)
                st.d("scalar", lambda e, w_=w_, gl=gl, src=src: e.dma_start(
                    out=wbuf[:, 0, w_, gl, :, :],
                    in_=src[:, :, 128 * (16 * gl):128 * (16 * gl) + 128]))
        st.bar()

        items = [(j, n4) for j in range(16) for n4 in range(NI)]

        def ph_ld(i):
            j, n4 = items[i]
            if n4 == 0 and j + 1 < 16:
                jn = j + 1
                for w_ in range(2):
                    for gl in range(2):
                        src = (w1src if w_ == 0 else w3src)
                        st.d("sync", lambda e, w_=w_, gl=gl, src=src, jn=jn:
                             e.dma_start(
                                 out=wbuf[:, jn % 2, w_, gl, :, :],
                                 in_=src[:, :, 128 * (jn + 16 * gl):
                                         128 * (jn + 16 * gl) + 128]),
                             slack=0)

        def ph_mm(i):
            j, n4 = items[i]
            q = 4 * (i % 2)
            def f(e):
                last = None
                for w_ in range(2):
                    for gl in range(2):
                        for k in range(8):
                            last = e.matmul(pp[q + 2 * w_ + gl][:],
                                            wbuf[:, j % 2, w_, gl, k, :],
                                            x1bg[:, k, 512 * n4:512 * (n4 + 1)],
                                            start=(k == 0), stop=(k == 7))
                return last
            st.c("tensor", f)

        def ph_ev(i):
            j, n4 = items[i]
            q = 4 * (i % 2)
            st.c("scalar", lambda e: e.activation(
                h1s[:, i % 2, 0, :], pp[q][:], AF.Identity,
                bias=b1c[:, e_i, j:j + 1]))
            st.c("scalar", lambda e: e.activation(
                h1s[:, i % 2, 1, :], pp[q + 1][:], AF.Identity,
                bias=b1c[:, e_i, j + 16:j + 17]))
            st.c("vector", lambda e: e.tensor_scalar(
                swsc[:, i % 2, :], pp[q + 2][:],
                b3c[:, e_i, j:j + 1], None, ALU.add))
            st.c("vector", lambda e: e.tensor_scalar(
                swsc[:, 2 + i % 2, :], pp[q + 3][:],
                b3c[:, e_i, j + 16:j + 17], None, ALU.add))

        def ph_prod(i):
            def f(e):
                e.tensor_tensor(swsc[:, 4 + i % 4, :], h1s[:, i % 2, 0, :],
                                swsc[:, i % 2, :], ALU.mult)
                return e.tensor_tensor(swsc[:, 8 + i % 5, :], h1s[:, i % 2, 1, :],
                                       swsc[:, 2 + i % 2, :], ALU.mult)
            st.c("vector", f)

        def ph_min(i):
            def f(e):
                e.tensor_scalar(swsc[:, 4 + i % 4, :], swsc[:, 4 + i % 4, :],
                                7.0, None, ALU.min)
                return e.tensor_scalar(swsc[:, 8 + i % 5, :], swsc[:, 8 + i % 5, :],
                                       7.0, None, ALU.min)
            st.c("vector", f)

        def ph_lcm(i):
            st.c("vector", lambda e: e.tensor_scalar(
                swsc[:, 8 + i % 5, :], swsc[:, 8 + i % 5, :], -7.0, 1.0,
                ALU.max, ALU.add))
            st.c("scalar", lambda e: e.activation(
                swsc[:, 13 + i % 2, :], swsc[:, 4 + i % 4, :], AF.Sigmoid,
                scale=cvec[:, 4:5]))

        def ph_cmb(i):
            j, n4 = items[i]
            if not shared:
                st.c("gpsimd", lambda e, n4=n4: e.tensor_tensor(
                    swsc[:, 8 + i % 5, :], swsc[:, 8 + i % 5, :],
                    comb_b[:, 512 * n4:512 * (n4 + 1)], ALU.mult))
            st.c("vector", lambda e: e.tensor_tensor(
                swsc[:, 15 + i % 2, :], swsc[:, 4 + i % 4, :],
                swsc[:, 13 + i % 2, :], ALU.mult))

        def ph_fin(i):
            j, n4 = items[i]
            st.c("vector", lambda e, j=j, n4=n4: e.tensor_tensor(
                act[:, j, 512 * n4:512 * (n4 + 1)], swsc[:, 15 + i % 2, :],
                swsc[:, 8 + i % 5, :], ALU.mult))

        pipe(st, len(items), [(0, ph_ld), (0, ph_mm), (1, ph_ev),
                              (2, ph_prod), (3, ph_min), (4, ph_lcm),
                              (5, ph_cmb), (6, ph_fin)])

        w2items = [(mo, n4) for mo in range(8) for n4 in range(NI)]

        def w2_mm(i):
            mo, n4 = w2items[i]
            def f(e):
                last = None
                for k in range(16):
                    last = e.matmul(pp[i % 6][:], w2s[:, k, 128 * mo:128 * (mo + 1)],
                                    act[:, k, 512 * n4:512 * (n4 + 1)],
                                    start=(k == 0), stop=(k == 15))
                return last
            st.c("tensor", f)

        def w2_ev(i):
            mo, n4 = w2items[i]
            if shared:
                st.c("vector", lambda e: e.tensor_scalar(
                    tmp2[:, i % 2, :], pp[i % 6][:], b2c[:, e_i, mo:mo + 1],
                    None, ALU.add))
            else:
                st.c("vector", lambda e: e.scalar_tensor_tensor(
                    tmp2[:, i % 2, :], comb_b[:, 512 * n4:512 * (n4 + 1)],
                    b2c[:, e_i, mo:mo + 1], pp[i % 6][:], ALU.mult, ALU.add))

        def w2_acc(i):
            mo, n4 = w2items[i]
            st.c("gpsimd", lambda e: e.tensor_tensor(
                acc[:, mo, 512 * n4:512 * (n4 + 1)],
                acc[:, mo, 512 * n4:512 * (n4 + 1)], tmp2[:, i % 2, :], ALU.add))

        pipe(st, len(w2items), [(0, w2_mm), (1, w2_ev), (2, w2_acc)])

    def do_group(g):
        g0 = g * NG

        for k in range(8):
            st.d("sync", lambda e, k=k: e.dma_start(
                out=x1bg[:, k, :], in_=x1b_d[128 * k:128 * (k + 1), g0:g0 + NG]))
        st.c("vector", lambda e: e.memset(acc[:], 0.0))
        st.bar()

        def rt_mm(i):
            def f(e):
                last = None
                for k in range(8):
                    last = e.matmul(pp[i % 3][:, 0:8],
                                    x1bg[:, k, 128 * i:128 * (i + 1)],
                                    gwr[:, k, :], start=(k == 0), stop=(k == 7))
                return last
            st.c("tensor", f)

        def _sl(i, k):
            return rt_s[:, i % 8, 8 * k:8 * k + 8]

        def _rc(i, k):
            return rcol[:, 6 * (i % 8) + k:6 * (i % 8) + k + 1]

        def rt_max(i):
            st.c("vector", lambda e: e.tensor_reduce(
                _rc(i, 0), pp[i % 3][:, 0:8], AX.X, ALU.max, negate=True))

        def rt_exp(i):
            st.c("scalar", lambda e: e.activation(
                _sl(i, 0), pp[i % 3][:, 0:8], AF.Exp, bias=_rc(i, 0)))

        def rt_z(i):
            st.c("vector", lambda e: e.tensor_reduce(_rc(i, 1), _sl(i, 0), AX.X, ALU.add))

        def rt_rz(i):
            st.c("vector", lambda e: e.reciprocal(_rc(i, 2), _rc(i, 1)))

        def rt_pb(i):
            def f(e):
                e.tensor_scalar(_sl(i, 1), _sl(i, 0), _rc(i, 2), None, ALU.mult)
                return e.scalar_tensor_tensor(
                    _sl(i, 2), _sl(i, 0), _rc(i, 2), gbb[:], ALU.mult, ALU.add)
            st.c("vector", f)

        def rt_mx1(i):
            st.c("vector", lambda e: e.tensor_reduce(_rc(i, 3), _sl(i, 2), AX.X, ALU.max))

        def rt_m1(i):
            st.c("vector", lambda e: e.tensor_scalar(
                _sl(i, 4), _sl(i, 2), _rc(i, 3), None, ALU.is_equal))

        def rt_b2(i):
            st.c("vector", lambda e: e.scalar_tensor_tensor(
                _sl(i, 3), _sl(i, 4), -1e30, _sl(i, 2), ALU.mult, ALU.add))

        def rt_mx2(i):
            st.c("vector", lambda e: e.tensor_reduce(_rc(i, 4), _sl(i, 3), AX.X, ALU.max))

        def rt_m2(i):
            st.c("vector", lambda e: e.tensor_scalar(
                _sl(i, 5), _sl(i, 3), _rc(i, 4), None, ALU.is_equal))

        def rt_m12(i):
            st.c("vector", lambda e: e.tensor_tensor(
                _sl(i, 5), _sl(i, 4), _sl(i, 5), ALU.add))

        def rt_cmb(i):
            st.c("vector", lambda e: e.tensor_tensor(
                _sl(i, 6), _sl(i, 1), _sl(i, 5), ALU.mult))

        def rt_tp(i):
            st.c("tensor", lambda e: e.transpose(
                pp[3 + (i // 4) % 2][0:8, 128 * (i % 4):128 * (i % 4 + 1)],
                _sl(i, 6), ident[:]))

        def rt_ev(i):
            if i % 4 == 3:
                q = i // 4
                st.c("vector", lambda e, q=q: e.tensor_copy(
                    combT[:, 512 * q:512 * (q + 1)], pp[3 + q % 2][0:8, :]))

        pipe(st, NG // 128, [(0, rt_mm), (1, rt_max), (2, rt_exp), (3, rt_z),
                             (4, rt_rz), (5, rt_pb), (6, rt_mx1), (7, rt_m1),
                             (8, rt_b2), (9, rt_mx2), (10, rt_m2), (11, rt_m12),
                             (12, rt_cmb), (13, rt_tp), (14, rt_ev)])



        for e_i in range(9):
            do_expert(g, e_i)

        for cc in range(NG // CH):
            t0 = g0 + cc * CH
            for m in range(8):
                st.d("sync", lambda e, m=m, t0=t0: e.dma_start(
                    out=ld[:, m, :], in_=x1f_d[128 * m:128 * (m + 1), t0:t0 + CH]))
            st.bar()
            rr = acc[:, :, cc * CH:(cc + 1) * CH]
            for m in range(8):
                st.c("vector", lambda e, m=m, rr=rr: e.tensor_tensor(
                    rr[:, m, :], rr[:, m, :], ld[:, m, :], ALU.add))
            st.bar()
            emit_ln(nc, st, pp, (pp[4][0:1, :], pp[5][0:1, :]), rr, r2, rowm, rowv, rowmm, rowrs,
                    rowmrs, onesc, onesr, ld, x2b_t, cvec)
            for m in range(8):
                st.d("sync", lambda e, m=m, t0=t0: e.dma_start(
                    out=x2f_d[128 * m:128 * (m + 1), t0:t0 + CH], in_=ld[:, m, :]))
                st.d("scalar", lambda e, m=m, t0=t0: e.dma_start(
                    out=x2b_d[128 * m:128 * (m + 1), t0:t0 + CH], in_=x2b_t[:, m, :]))
            st.bar()

    for g in range(NGR):
        do_group(g)


def build_l0f():
    nc = bass.Bass("TRN2", target_bir_lowering=False)
    with ExitStack() as stk:
        S = stk.enter_context(nc.semaphore("S"))
        S2 = stk.enter_context(nc.semaphore("S2"))
        block = stk.enter_context(nc.Block())
        st = Sched()
        emit_l0f(nc, st, stk)
        st.emit(block, S, S2)
    return nc


def emit_l1a(nc, st, stk, pre=None):
    pre = pre or {}
    def dt(nm, shape, dtype, kind=None):
        if nm in pre:
            return pre[nm]
        if kind is None:
            return nc.dram_tensor("m_" + nm, shape, dtype).ap()
        return nc.dram_tensor("m_" + nm, shape, dtype, kind=kind).ap()
    sb = lambda n_, s_, d_=F32: stk.enter_context(nc.sbuf_tensor("m_" + n_, s_, d_))
    ps = lambda n_, s_, d_=F32: stk.enter_context(nc.psum_tensor("m_" + n_, s_, d_))

    x2f_d = dt("x2f", [D, T], F32, kind="ExternalInput")
    x2b_d = dt("x2b", [D, T], BF16, kind="ExternalInput")
    winr_d = dt("winr", [128, 8, 4096], F32, kind="ExternalInput")
    woutr_d = dt("woutr", [128, 16, 1024], F32, kind="ExternalInput")
    wxp_d = dt("wxp", [128, 16, 80], F32, kind="ExternalInput")
    cwc_d = dt("cwc", [128, 16, 4], F32, kind="ExternalInput")
    cbc_d = dt("cbc", [128, 16], F32, kind="ExternalInput")
    arlog_d = dt("arlog", [128, 256], F32, kind="ExternalInput")
    dcol_d = dt("dcol", [128, 16], F32, kind="ExternalInput")
    repm_d = dt("repm", [128, 128], F32, kind="ExternalInput")
    sumsel_d = dt("sumsel", [128, 64], F32, kind="ExternalInput")
    selp_d = dt("selp", [48, 8, 128], F32, kind="ExternalInput")
    onesr_d = dt("onesr", [1, 128], F32, kind="ExternalInput")
    onesc_d = dt("onesc", [128, 1], F32, kind="ExternalInput")
    cvec_d = dt("cvec", [128, 8], F32, kind="ExternalInput")
    x3f_d = dt("x3f", [D, T], F32, kind="ExternalOutput")
    x3b_d = dt("x3b", [D, T], BF16, kind="ExternalOutput")

    winc = dt("winc", [128, 8, 4096], BF16)
    woutc = dt("woutc", [128, 16, 1024], BF16)

    xin = sb("xin", [128, 16, 515])
    tails = sb("tails", [128, 16, 3])
    xcb = sb("xcb", [128, 16, 512], BF16)
    szb = sb("szb", [128, 16, 512], BF16)
    yb = xin[:].rearrange("p a b -> p (a b)")[:, 4096:8192].bitcast(BF16).rearrange(
        "p (a b) -> p a b", b=512)
    yzb = yb
    x2bg = sb("x2bg", [128, 8, 512], BF16)
    ld = sb("ld", [128, 8, 512])
    r_f = xin[:].rearrange("p a b -> p (a b)")[:, 0:4096].rearrange(
        "p (a b) -> p a b", b=512)
    wbuf = sb("wbuf", [128, 3, 8, 128], BF16)
    wbuf2 = sb("wbuf2", [128, 2, 16, 128], BF16)
    wxs = sb("wxs", [128, 16, 80], BF16)
    cwc = sb("cwc_s", [128, 16, 4]); cbc = sb("cbc_s", [128, 16])
    ar = sb("ar_s", [128, 256]); arlog = sb("arlog_s", [128, 256])
    dcol = sb("dcol_s", [128, 16])
    repm = sb("repm_s", [128, 128], BF16)
    sumsel = sb("sumsel_s", [128, 64], BF16)
    selp = sb("selp_s", [48, 8, 128], BF16)
    onesr = sb("onesr_s", [1, 128]); onesc = sb("onesc_s", [128, 1])
    cvec = sb("cvec_s", [128, 8])
    ssm = sb("ssm_s", [80, 512])
    bpr = sb("bpr", [16, 512], BF16); cprt = sb("cprt", [48, 512], BF16)
    dtb = sb("dtb", [128, 512])
    Bb = sb("Bb", [128, 8, 512], BF16); Cb = sb("Cb", [128, 8, 512], BF16)
    dA = sb("dA", [128, 8, 512], BF16)
    bt = sb("bt", [128, 4, 512], BF16)
    ht = sb("ht", [128, 4, 512], BF16)
    hct = sb("hct", [128, 4, 512], BF16)
    xcrb = sb("xcrb", [128, 2, 512], BF16)
    stcol = sb("stcol", [128, 256])
    cv1 = sb("cv1", [128, 2, 1024]); cv2 = sb("cv2", [128, 2, 1024])
    cv3 = sb("cv3", [128, 2, 512])
    srow = sb("srow", [1, 6, 512])
    rowm = sb("rowm", [1, 512]); rowv = sb("rowv", [1, 512])
    rowmm = sb("rowmm", [1, 512]); rowrs = sb("rowrs", [1, 512])
    rowmrs = sb("rowmrs", [1, 512])
    r2 = sb("r2", [128, 2, 512])

    pp = [ps(f"pp{i}", [128, 512]) for i in range(4)]
    ps_y = ps("ps_y", [128, 2, 512])
    ps_stat = ps("ps_stat", [1, 2, 512])

    for s_, d_ in [(cwc, cwc_d), (cbc, cbc_d), (arlog, arlog_d), (dcol, dcol_d),
                   (onesr, onesr_d), (onesc, onesc_d), (cvec, cvec_d)]:
        st.d("sync", lambda e, s=s_, d=d_: e.dma_start(out=s[:], in_=d))
    for s_, d_ in [(wxs, wxp_d), (repm, repm_d), (sumsel, sumsel_d),
                   (selp, selp_d)]:
        st.d("gpsimd", lambda e, s=s_, d=d_: e.dma_start(out=s[:], in_=d))
    st.c("vector", lambda e: e.memset(stcol[:], 0.0))
    st.bar()
    st.c("scalar", lambda e: e.activation(ar[:], arlog[:], AF.Exp))
    st.bar()
    st.c("vector", lambda e: e.tensor_scalar(ar[:], ar[:], -1.0, None, ALU.mult))
    st.bar()

    stg = [xin[:].rearrange("p a b -> p (a b)")[:, 0:4096].bitcast(BF16)
           .rearrange("p (a b) -> p a b", b=1024),
           xin[:].rearrange("p a b -> p (a b)")[:, 4096:8192].bitcast(BF16)
           .rearrange("p (a b) -> p a b", b=1024)]
    pieces = []
    for q in range(4):
        pieces.append((winr_d[:, :, 1024 * q:1024 * (q + 1)],
                       winc[:, :, 1024 * q:1024 * (q + 1)], 8))
    pieces.append((woutr_d[:, 0:8, :], woutc[:, 0:8, :], 8))
    pieces.append((woutr_d[:, 8:16, :], woutc[:, 8:16, :], 8))

    def pc_ld(i):
        src, dst, na = pieces[i]
        for a in range(na):
            st.d("gpsimd", lambda e, a=a, src=src, i=i: e.dma_start(
                out=stg[i % 2][:, a, :], in_=src[:, a, :], max_dma_last_dim=4096))

    def pc_stn(i):
        src, dst, na = pieces[i]
        for a in range(na):
            st.d("sync", lambda e, a=a, dst=dst, i=i: e.dma_start(
                out=dst[:, a, :], in_=stg[i % 2][:, a, :]))

    pipe(st, len(pieces), [(0, pc_ld), (1, pc_stn)])
    st.bar()

    def do_chunk(c):
        c0 = c * CH
        for k in range(8):
            st.d("sync", lambda e, k=k: e.dma_start(
                out=x2bg[:, k, :], in_=x2b_d[128 * k:128 * (k + 1), c0:c0 + CH]))
        if c > 0:
            for j in range(16):
                st.c("gpsimd", lambda e, j=j: e.tensor_copy(
                    xin[:, j, 0:3], tails[:, j, :]))
        else:
            st.c("gpsimd", lambda e: e.memset(xin[:, :, 0:3], 0.0))
        for w_ in range(2):
            st.d("scalar", lambda e, w_=w_: e.dma_start(
                out=wbuf[:, w_, :, :],
                in_=winc[:, :, 128 * w_:128 * (w_ + 1)]))
        st.bar()

        def wi_ld(m):
            if m + 2 < 32:
                st.d("sync", lambda e, m=m: e.dma_start(
                    out=wbuf[:, (m + 2) % 3, :, :],
                    in_=winc[:, :, 128 * (m + 2):128 * (m + 3)]), slack=0)

        def wi_mm(m):
            def f(e):
                last = None
                for k in range(8):
                    last = e.matmul(pp[m % 2][:], wbuf[:, m % 3, k, :],
                                    x2bg[:, k, :], start=(k == 0), stop=(k == 7))
                return last
            st.c("tensor", f)

        def wi_ev(m):
            if m < 16:
                st.c("vector", lambda e, m=m: e.tensor_copy(
                    xin[:, m, 3:515], pp[m % 2][:]))
            else:
                st.c("scalar", lambda e, m=m: e.activation(
                    szb[:, m - 16, :], pp[m % 2][:], AF.Silu))

        pipe(st, 32, [(0, wi_ld), (0, wi_mm), (1, wi_ev)])

        def cv_a(j):
            def f(e):
                e.tensor_scalar(cv1[:, j % 2, 0:512], xin[:, j, 0:512],
                                cwc[:, j, 0:1], None, ALU.mult)
                e.tensor_scalar(cv1[:, j % 2, 512:1024], xin[:, j, 1:513],
                                cwc[:, j, 1:2], None, ALU.mult)
                e.tensor_scalar(cv2[:, j % 2, 0:512], xin[:, j, 2:514],
                                cwc[:, j, 2:3], None, ALU.mult)
                return e.tensor_scalar(cv2[:, j % 2, 512:1024], xin[:, j, 3:515],
                                       cwc[:, j, 3:4], None, ALU.mult)
            st.c("vector", f)

        def cv_b(j):
            def f(e):
                e.tensor_tensor(cv1[:, j % 2, 0:512], cv1[:, j % 2, 0:512],
                                cv1[:, j % 2, 512:1024], ALU.add)
                return e.tensor_tensor(cv2[:, j % 2, 0:512], cv2[:, j % 2, 0:512],
                                       cv2[:, j % 2, 512:1024], ALU.add)
            st.c("gpsimd", f)

        def cv_c(j):
            st.c("vector", lambda e, j=j: e.tensor_tensor(
                cv3[:, j % 2, :], cv1[:, j % 2, 0:512], cv2[:, j % 2, 0:512], ALU.add))

        def cv_d(j):
            st.c("scalar", lambda e, j=j: e.activation(
                xcb[:, j, :], cv3[:, j % 2, :], AF.Silu, bias=cbc[:, j:j + 1]))

        pipe(st, 16, [(0, cv_a), (1, cv_b), (2, cv_c), (3, cv_d)])
        for j in range(16):
            st.c("vector", lambda e, j=j: e.tensor_copy(
                tails[:, j, :], xin[:, j, 512:515]))
        st.bar()

        def wx_mm(e):
            last = None
            for k in range(16):
                last = e.matmul(pp[2][0:80, :], wxs[:, k, :], xcb[:, k, :],
                                start=(k == 0), stop=(k == 15))
            return last
        st.c("tensor", wx_mm)
        st.bar()
        st.c("vector", lambda e: e.tensor_copy(ssm[:], pp[2][0:80, :]))
        st.bar()

        st.d("scalar", lambda e: e.dma_start(out=srow[:, 0, :], in_=ssm[64:65, :]))
        st.bar()
        dtr = srow[:, 0, :]
        st.c("vector", lambda e: e.tensor_scalar(
            srow[:, 1, :], dtr, -1.0, None, ALU.mult))
        st.bar()
        st.c("vector", lambda e: e.tensor_tensor(
            srow[:, 1, :], srow[:, 1, :], dtr, ALU.max))
        st.bar()
        st.c("scalar", lambda e: e.activation(
            srow[:, 2, :], srow[:, 1, :], AF.Exp, scale=cvec[0:1, 2:3]))
        st.bar()
        st.c("scalar", lambda e: e.activation(
            srow[:, 3, :], srow[:, 2, :], AF.Ln, bias=cvec[0:1, 3:4]))
        st.c("vector", lambda e: e.tensor_scalar(
            srow[:, 4, :], dtr, 0.0, None, ALU.max))
        st.bar()
        st.c("vector", lambda e: e.tensor_tensor(
            srow[:, 5, :], srow[:, 3, :], srow[:, 4, :], ALU.add))
        st.bar()
        st.c("tensor", lambda e: e.matmul(pp[3][:], onesr[:], srow[:, 5, :],
                                          start=True, stop=True))
        st.bar()
        st.c("vector", lambda e: e.tensor_copy(dtb[:], pp[3][:]))
        st.bar()
        st.c("vector", lambda e: e.tensor_tensor(
            bpr[:], ssm[0:16, :], pp[3][0:16, :], ALU.mult))
        st.c("scalar", lambda e: e.copy(cprt[32:48, :], ssm[32:48, :]))
        st.bar()

        def bc_mm(i):
            sp, which = i // 2, i % 2
            if which == 0:
                st.c("tensor", lambda e, sp=sp: e.matmul(
                    pp[i % 2][:], selp[0:16, sp, :], bpr[:], start=True, stop=True))
            else:
                st.c("tensor", lambda e, sp=sp: e.matmul(
                    pp[i % 2][:], selp[32:48, sp, :], cprt[32:48, :],
                    start=True, stop=True))

        def bc_ev(i):
            sp, which = i // 2, i % 2
            dst = Bb if which == 0 else Cb
            st.c("vector", lambda e: e.tensor_copy(dst[:, sp, :], pp[i % 2][:]))

        pipe(st, 16, [(0, bc_mm), (1, bc_ev)])

        units = [(dg, sp) for dg in range(32) for sp in range(8)]

        def u_pe(i):
            dg, sp = units[i]
            if sp == 0:
                hb = 64 * (dg % 2)
                st.c("tensor", lambda e, dg=dg, hb=hb: e.matmul(
                    pp[2 + (dg % 2)][:], repm[hb:hb + 64, :],
                    xcb[hb:hb + 64, dg // 2, :], start=True, stop=True))

        def u_xev(i):
            dg, sp = units[i]
            if sp == 0:
                st.c("scalar", lambda e, dg=dg: e.copy(
                    xcrb[:, dg % 2, :], pp[2 + (dg % 2)][:]))

        def u_dA(i):
            dg, sp = units[i]
            st.c("scalar", lambda e, i=i: e.activation(
                dA[:, i % 8, :], dtb[:], AF.Exp, scale=ar[:, i:i + 1]))

        def u_b(i):
            dg, sp = units[i]
            st.c("vector", lambda e, i=i, dg=dg, sp=sp: e.tensor_tensor(
                bt[:, i % 4, :], xcrb[:, dg % 2, :], Bb[:, sp, :], ALU.mult))

        def u_scan(i):
            dg, sp = units[i]
            st.c("vector", lambda e, i=i: e.tensor_tensor_scan(
                ht[:, i % 4, :], dA[:, i % 8, :], bt[:, i % 4, :],
                stcol[:, i:i + 1], ALU.mult, ALU.add))

        def u_st(i):
            st.c("scalar", lambda e, i=i: e.copy(
                stcol[:, i:i + 1], ht[:, i % 4, 511:512]))

        def u_hc(i):
            dg, sp = units[i]
            st.c("gpsimd", lambda e, i=i, sp=sp: e.tensor_tensor(
                hct[:, i % 4, :], ht[:, i % 4, :], Cb[:, sp, :], ALU.mult))

        def u_ymm(i):
            dg, sp = units[i]
            hb = 64 * (dg % 2)
            st.c("tensor", lambda e, i=i, dg=dg, sp=sp, hb=hb: e.matmul(
                ps_y[hb:hb + 64, dg % 2, :], sumsel[:, :], hct[:, i % 4, :],
                start=(sp == 0), stop=(sp == 7)))

        def u_yev(i):
            dg, sp = units[i]
            if sp != 7:
                return
            hb = 64 * (dg % 2)
            st.c("vector", lambda e, dg=dg, hb=hb: e.scalar_tensor_tensor(
                yb[hb:hb + 64, dg // 2, :], xcb[hb:hb + 64, dg // 2, :],
                dcol[hb:hb + 64, dg // 2:dg // 2 + 1],
                ps_y[hb:hb + 64, dg % 2, :], ALU.mult, ALU.add))

        ph = [(0, u_pe), (0, u_dA), (1, u_xev), (2, u_b), (3, u_scan),
              (4, u_st), (4, u_hc), (5, u_ymm), (6, u_yev)]
        moff = 6
        nu = len(units)
        for s in range(nu // 2 + moff):
            for off, fn in ph:
                for half in range(2):
                    i = (s - off) * 2 + half
                    if 0 <= i < nu and (s - off) >= 0 and (s - off) < nu // 2:
                        fn(i)
            st.bar()

        def yz_f(j):
            st.c("vector", lambda e, j=j: e.tensor_tensor(
                yzb[:, j, :], yb[:, j, :], szb[:, j, :], ALU.mult))

        pipe(st, 16, [(0, yz_f)])

        for m in range(8):
            st.d("sync", lambda e, m=m: e.dma_start(
                out=ld[:, m, :], in_=x2f_d[128 * m:128 * (m + 1), c0:c0 + CH]))
        st.d("scalar", lambda e: e.dma_start(out=wbuf2[:, 0, :, :],
                                             in_=woutc[:, :, 0:128]))
        st.bar()

        def wo_ld(m):
            if m + 1 < 8:
                st.d("scalar", lambda e, m=m: e.dma_start(
                    out=wbuf2[:, (m + 1) % 2, :, :],
                    in_=woutc[:, :, 128 * (m + 1):128 * (m + 2)]))

        def wo_mm(m):
            def f(e):
                last = None
                for k in range(16):
                    last = e.matmul(pp[m % 2][:], wbuf2[:, m % 2, k, :],
                                    yzb[:, k, :], start=(k == 0), stop=(k == 15))
                return last
            st.c("tensor", f)

        def wo_ev(m):
            st.c("vector", lambda e, m=m: e.tensor_tensor(
                r_f[:, m, :], pp[m % 2][:], ld[:, m, :], ALU.add))

        pipe(st, 8, [(0, wo_ld), (0, wo_mm), (1, wo_ev)])

        emit_ln(nc, st, pp, (ps_stat[:, 0, :], ps_stat[:, 1, :]), r_f, r2, rowm, rowv, rowmm, rowrs,
                rowmrs, onesc, onesr, ld, xcb[:, 0:8, :], cvec)
        for m in range(8):
            st.d("sync", lambda e, m=m: e.dma_start(
                out=x3f_d[128 * m:128 * (m + 1), c0:c0 + CH], in_=ld[:, m, :]))
            st.d("scalar", lambda e, m=m: e.dma_start(
                out=x3b_d[128 * m:128 * (m + 1), c0:c0 + CH], in_=xcb[:, m, :]))
        st.bar()

    for c in range(NCH):
        do_chunk(c)


def build_l1a():
    nc = bass.Bass("TRN2", target_bir_lowering=False)
    with ExitStack() as stk:
        S = stk.enter_context(nc.semaphore("S"))
        S2 = stk.enter_context(nc.semaphore("S2"))
        block = stk.enter_context(nc.Block())
        st = Sched()
        emit_l1a(nc, st, stk)
        st.emit(block, S, S2)
    return nc


def l1a_host_inputs(g):
    out = common_consts()
    winT = g["L1_attn_W_in"].T
    out["winr"] = np.ascontiguousarray(
        winT.reshape(8, 128, 4096).transpose(1, 0, 2))
    woutT = g["L1_attn_W_out"].T
    out["woutr"] = np.ascontiguousarray(
        woutT.reshape(16, 128, 1024).transpose(1, 0, 2))
    wx = g["L1_attn_W_x"]
    wxp = np.zeros((2048, 80), np.float32)
    wxp[:, 0:16] = wx[0:16].T
    wxp[:, 32:48] = wx[16:32].T
    wxp[:, 64] = wx[32]
    out["wxp"] = np.ascontiguousarray(
        wxp.reshape(16, 128, 80).transpose(1, 0, 2))
    cw = g["L1_attn_conv_w"][:, 0, :]
    out["cwc"] = np.ascontiguousarray(cw.reshape(16, 128, 4).transpose(1, 0, 2))
    out["cbc"] = np.ascontiguousarray(
        g["L1_attn_conv_b"].reshape(16, 128).T)
    A = g["L1_attn_A_log"]
    ar = np.zeros((128, 256), np.float32)
    for dg in range(32):
        for sp in range(8):
            u = dg * 8 + sp
            for sl in range(2):
                ar[64 * sl:64 * (sl + 1), u] = A[64 * dg:64 * (dg + 1), 2 * sp + sl]
    out["arlog"] = ar
    out["dcol"] = np.ascontiguousarray(
        g["L1_attn_D"].reshape(16, 128).T)
    repm = np.zeros((128, 128), np.float32)
    for a in range(2):
        for k in range(64):
            repm[64 * a + k, :][np.arange(128) % 64 == k] = 1.0
    out["repm"] = repm
    sumsel = np.zeros((128, 64), np.float32)
    for p in range(128):
        sumsel[p, p % 64] = 1.0
    out["sumsel"] = sumsel
    selp = np.zeros((48, 8, 128), np.float32)
    for sp in range(8):
        for p in range(128):
            selp[2 * sp + p // 64, sp, p] = 1.0
            selp[32 + 2 * sp + p // 64, sp, p] = 1.0
    out["selp"] = selp
    return out


def emit_l1f_a(nc, st, stk, pre=None):
    pre = pre or {}
    def dt(nm, shape, dtype, kind=None):
        if nm in pre:
            return pre[nm]
        if kind is None:
            return nc.dram_tensor("p_" + nm, shape, dtype).ap()
        return nc.dram_tensor("p_" + nm, shape, dtype, kind=kind).ap()
    sb = lambda n_, s_, d_=F32: stk.enter_context(nc.sbuf_tensor("p_" + n_, s_, d_))
    ps = lambda n_, s_, d_=F32: stk.enter_context(nc.psum_tensor("p_" + n_, s_, d_))

    x3b_d = dt("x3b", [D, T], BF16, kind="ExternalInput")
    gwr2_d = dt("gwr2", [128, 8, 8], F32, kind="ExternalInput")
    gbb2_d = dt("gbb2", [128, 8], F32, kind="ExternalInput")
    w1r_d = dt("w1r", [8, 128, 8, 1024], F32, kind="ExternalInput")
    w2r_d = dt("w2r", [8, 128, 8, 1024], F32, kind="ExternalInput")
    w3r_d = dt("w3r", [8, 128, 8, 1024], F32, kind="ExternalInput")
    b1c_d = dt("b1c", [128, 8, 8], F32, kind="ExternalInput")
    b2c_d = dt("b2c", [128, 8, 8], F32, kind="ExternalInput")
    b3c_d = dt("b3c", [128, 8, 8], F32, kind="ExternalInput")
    esel_d = dt("esel", [8, 8, 128], F32, kind="ExternalInput")
    ident_d = dt("ident", [128, 128], F32, kind="ExternalInput")
    onesr_d = dt("onesr", [1, 128], F32, kind="ExternalInput")
    vpart_d = dt("vpart", [128, 8, 8], F32, kind="ExternalOutput")
    indT_d = dt("indT", [8, T], BF16, kind="ExternalOutput")

    w1c = dt("w1c", [8, 128, 8, 1024], BF16)
    w2c = dt("w2c", [8, 128, 8, 1024], BF16)
    w3c = dt("w3c", [8, 128, 8, 1024], BF16)
    rowsc_d = dt("rowsc", [16], F32)

    lgT = sb("lgT", [8, T])
    lg_tm = sb("lg_tm", [128, 64, 8])
    indT_s = sb("indT_s", [8, T], BF16)
    wtT = sb("wtT", [8, T])
    w1s = sb("w1s", [128, 8, 1024], BF16)
    w2s = sb("w2s", [128, 8, 1024], BF16)
    w3s = sb("w3s", [128, 8, 1024], BF16)
    x3bc = sb("x3bc", [128, 2, 8, 512], BF16)
    h1b = sb("h1b", [128, 8, 512], BF16)
    h2b = sb("h2b", [128, 8, 512], BF16)
    ytb = sb("ytb", [128, 4, 512], BF16)
    ysc = sb("ysc", [128, 2, 512], BF16)
    vsc = sb("vsc", [128, 8, 8, 16]); vred = sb("vred", [128, 8, 8])
    tks = sb("tks", [128, 8, 64])
    tkc = sb("tkc", [128, 48])
    mzc = sb("mzc", [8, 2])
    mzr = sb("mzr", [1, 16])
    mzb = sb("mzb", [128, 16])
    gwr2 = sb("gwr2_s", [128, 8, 8], BF16)
    gbb2 = sb("gbb2_s", [128, 8])
    b1c = sb("b1c_s", [128, 8, 8]); b2c = sb("b2c_s", [128, 8, 8])
    b3c = sb("b3c_s", [128, 8, 8])
    esel = sb("esel_s", [8, 8, 128])
    ident = sb("ident_s", [128, 128])
    onesr = sb("onesr_s", [1, 128])

    pp = [ps(f"pp{i}", [128, 512]) for i in range(6)]
    ps_w = ps("ps_w", [128, 512])

    for s_, d_ in [(gbb2, gbb2_d), (b1c, b1c_d), (b2c, b2c_d), (b3c, b3c_d),
                   (esel, esel_d), (ident, ident_d), (onesr, onesr_d)]:
        st.d("sync", lambda e, s=s_, d=d_: e.dma_start(out=s[:], in_=d))
    st.d("gpsimd", lambda e: e.dma_start(out=gwr2[:], in_=gwr2_d))
    st.bar()

    pieces = []
    for e in range(8):
        pieces.append((w1r_d[e], w1c[e]))
        pieces.append((w2r_d[e], w2c[e]))
        pieces.append((w3r_d[e], w3c[e]))
    stg = [w1s[:], w2s[:]]

    def pc_ld(i):
        for a in range(8):
            st.d("gpsimd", lambda e, a=a, i=i: e.dma_start(
                out=stg[i % 2][:, a, :], in_=pieces[i][0][:, a, :],
                max_dma_last_dim=4096))

    def pc_stn(i):
        for a in range(8):
            st.d("sync", lambda e, a=a, i=i: e.dma_start(
                out=pieces[i][1][:, a, :], in_=stg[i % 2][:, a, :]))

    pipe(st, len(pieces), [(0, pc_ld), (1, pc_stn)])

    def lgdma(i):
        if i % 4 == 0:
            c = i // 4
            for k in range(8):
                st.d("sync", lambda e, k=k, c=c: e.dma_start(
                    out=x3bc[:, c % 2, k, :],
                    in_=x3b_d[128 * k:128 * (k + 1), 512 * c:512 * (c + 1)]),
                     slack=0)

    def lg_mm(i):
        def f(e):
            last = None
            for k in range(8):
                last = e.matmul(pp[i % 3][:, 0:8],
                                x3bc[:, (i // 4) % 2, k, 128 * (i % 4):128 * (i % 4 + 1)],
                                gwr2[:, k, :], start=(k == 0), stop=(k == 7))
            return last
        st.c("tensor", f)

    def lg_add(i):
        st.c("vector", lambda e, i=i: e.tensor_tensor(
            lg_tm[:, i, :], pp[i % 3][:, 0:8], gbb2[:], ALU.add))

    def lg_tp(i):
        st.c("tensor", lambda e, i=i: e.transpose(
            pp[3 + (i // 4) % 2][0:8, 128 * (i % 4):128 * (i % 4 + 1)],
            lg_tm[:, i, :], ident[:]))

    def lg_ev(i):
        if i % 4 == 3:
            q = i // 4
            st.c("vector", lambda e, q=q: e.tensor_copy(
                lgT[:, 512 * q:512 * (q + 1)], pp[3 + q % 2][0:8, :]))

    pipe(st, 64, [(0, lgdma), (2, lg_mm), (3, lg_add), (4, lg_tp), (5, lg_ev)])

    st.c("vector", lambda e: e.tensor_reduce(mzc[:, 0:1], lgT[:], AX.X, ALU.max,
                                             negate=True))
    st.bar()
    st.c("scalar", lambda e: e.activation(lgT[:], lgT[:], AF.Exp,
                                          bias=mzc[:, 0:1],
                                          accum_out=mzc[:, 1:2]))
    st.bar()
    st.d("scalar", lambda e: e.dma_start(out=rowsc_d[0:8], in_=mzc[:, 0]))
    st.d("scalar", lambda e: e.dma_start(out=rowsc_d[8:16], in_=mzc[:, 1]))
    st.bar()
    st.d("scalar", lambda e: e.dma_start(out=mzr[:], in_=rowsc_d))
    st.bar()
    st.c("vector", lambda e: e.reciprocal(mzr[:, 8:16], mzr[:, 8:16]))
    st.bar()
    st.c("tensor", lambda e: e.matmul(pp[0][:, 0:16], onesr[:], mzr[:],
                                      start=True, stop=True))
    st.bar()
    st.c("vector", lambda e: e.tensor_copy(mzb[:], pp[0][:, 0:16]))
    st.bar()

    def _tk(i, k):
        return tks[:, i % 8, 8 * k:8 * k + 8]

    def _tkc(i, k):
        return tkc[:, 6 * (i % 8) + k:6 * (i % 8) + k + 1]

    def tk_sub(i):
        st.c("vector", lambda e, i=i: e.tensor_tensor(
            _tk(i, 0), lg_tm[:, i, :], mzb[:, 0:8], ALU.add))

    def tk_exp(i):
        st.c("scalar", lambda e, i=i: e.activation(_tk(i, 1), _tk(i, 0), AF.Exp))

    def tk_pr(i):
        st.c("vector", lambda e, i=i: e.tensor_tensor(
            _tk(i, 2), _tk(i, 1), mzb[:, 8:16], ALU.mult))

    def tk_mx1(i):
        st.c("vector", lambda e, i=i: e.tensor_reduce(
            _tkc(i, 0), _tk(i, 2), AX.X, ALU.max))

    def tk_m1(i):
        st.c("vector", lambda e, i=i: e.tensor_scalar(
            _tk(i, 3), _tk(i, 2), _tkc(i, 0), None, ALU.is_equal))

    def tk_b2(i):
        st.c("vector", lambda e, i=i: e.scalar_tensor_tensor(
            _tk(i, 4), _tk(i, 3), -1e30, _tk(i, 2), ALU.mult, ALU.add))

    def tk_mx2(i):
        st.c("vector", lambda e, i=i: e.tensor_reduce(
            _tkc(i, 1), _tk(i, 4), AX.X, ALU.max))

    def tk_m2(i):
        st.c("vector", lambda e, i=i: e.tensor_scalar(
            _tk(i, 5), _tk(i, 4), _tkc(i, 1), None, ALU.is_equal))

    def tk_m12(i):
        st.c("vector", lambda e, i=i: e.tensor_tensor(
            _tk(i, 5), _tk(i, 3), _tk(i, 5), ALU.add))

    def tk_wt(i):
        st.c("vector", lambda e, i=i: e.tensor_tensor(
            _tk(i, 6), _tk(i, 2), _tk(i, 5), ALU.mult))

    def tk_tp(i):
        def f(e):
            e.transpose(pp[3 + (i // 4) % 2][0:8, 128 * (i % 4):128 * (i % 4 + 1)],
                        _tk(i, 5), ident[:])
            return e.transpose(
                pp[(i // 4) % 2][0:8, 128 * (i % 4):128 * (i % 4 + 1)],
                _tk(i, 6), ident[:])
        st.c("tensor", f)

    def tk_ev(i):
        if i % 4 == 3:
            q = i // 4
            st.c("vector", lambda e, q=q: e.tensor_copy(
                indT_s[:, 512 * q:512 * (q + 1)], pp[3 + q % 2][0:8, :]))
            st.c("scalar", lambda e, q=q: e.copy(
                wtT[:, 512 * q:512 * (q + 1)], pp[q % 2][0:8, :]))

    pipe(st, 64, [(0, tk_sub), (1, tk_exp), (2, tk_pr), (3, tk_mx1),
                  (4, tk_m1), (5, tk_b2), (6, tk_mx2), (7, tk_m2),
                  (8, tk_m12), (9, tk_wt), (10, tk_tp), (11, tk_ev)])
    for q in range(16):
        st.d("scalar", lambda e, q=q: e.dma_start(
            out=indT_d[:, 512 * q:512 * (q + 1)],
            in_=indT_s[:, 512 * q:512 * (q + 1)]))
    st.c("vector", lambda e: e.memset(vsc[:], 0.0))
    st.bar()

    def do_expert(e_i):
        st.d("scalar", lambda e: e.dma_start(out=w1s[:], in_=w1c[e_i]))
        st.d("scalar", lambda e: e.dma_start(out=w2s[:], in_=w2c[e_i]))
        st.d("scalar", lambda e: e.dma_start(out=w3s[:], in_=w3c[e_i]))
        st.bar()

        def do_chunk(c):
            c0 = c * CH
            for k in range(8):
                st.d("sync", lambda e, k=k: e.dma_start(
                    out=x3bc[:, 0, k, :], in_=x3b_d[128 * k:128 * (k + 1), c0:c0 + CH]))
            st.c("tensor", lambda e: e.matmul(
                ps_w[:], esel[:, e_i, :], wtT[:, c0:c0 + CH], start=True, stop=True))
            st.bar()

            def h_mm(e, src, wsrc, m, i):
                last = None
                for k in range(8):
                    last = e.matmul(pp[i % 6][:], wsrc[:, k, 128 * m:128 * (m + 1)],
                                    src[:, k, :], start=(k == 0), stop=(k == 7))
                return last

            def h1_mm(s):
                def f(e):
                    h_mm(e, x3bc[:, 0], w1s, 2 * s, 2 * s)
                    return h_mm(e, x3bc[:, 0], w1s, 2 * s + 1, 2 * s + 1)
                st.c("tensor", f)

            def h1_ev(s):
                for m in (2 * s, 2 * s + 1):
                    st.c("scalar", lambda e, m=m: e.activation(
                        h1b[:, m, :], pp[m % 6][:], AF.Relu, bias=b1c[:, e_i, m:m + 1]))

            pipe(st, 4, [(0, h1_mm), (1, h1_ev)])

            def h2_mm(s):
                def f(e):
                    h_mm(e, h1b, w2s, 2 * s, 2 * s)
                    return h_mm(e, h1b, w2s, 2 * s + 1, 2 * s + 1)
                st.c("tensor", f)

            def h2_ev(s):
                for m in (2 * s, 2 * s + 1):
                    st.c("scalar", lambda e, m=m: e.activation(
                        h2b[:, m, :], pp[m % 6][:], AF.Relu, bias=b2c[:, e_i, m:m + 1]))

            pipe(st, 4, [(0, h2_mm), (1, h2_ev)])

            def y_mm(s):
                def f(e):
                    h_mm(e, h2b, w3s, 2 * s, 2 * s)
                    return h_mm(e, h2b, w3s, 2 * s + 1, 2 * s + 1)
                st.c("tensor", f)

            def y_ev(s):
                for m in (2 * s, 2 * s + 1):
                    st.c("scalar", lambda e, m=m: e.activation(
                        ytb[:, m % 4, :], pp[m % 6][:], AF.Relu,
                        bias=b3c[:, e_i, m:m + 1]))

            def y_acc(s):
                for m in (2 * s, 2 * s + 1):
                    st.c("vector", lambda e, m=m: e.scalar_tensor_tensor(
                        ysc[:, m % 2, :], ytb[:, m % 4, :], 1.0, ps_w[:],
                        ALU.mult, ALU.mult,
                        accum_out=vsc[:, e_i, m, c:c + 1]))

            pipe(st, 4, [(0, y_mm), (1, y_ev), (2, y_acc)])

        for c in range(NCH):
            do_chunk(c)

    for e_i in range(8):
        do_expert(e_i)

    st.c("vector", lambda e: e.tensor_reduce(vred[:], vsc[:], AX.X, ALU.add))
    st.bar()
    st.d("sync", lambda e: e.dma_start(out=vpart_d, in_=vred[:]))
    st.bar()


def emit_l1f_b(nc, st, stk, pre=None, nvp=8):
    pre = pre or {}
    def dt(nm, shape, dtype, kind=None):
        if nm in pre:
            return pre[nm]
        if kind is None:
            return nc.dram_tensor("b_" + nm, shape, dtype).ap()
        return nc.dram_tensor("b_" + nm, shape, dtype, kind=kind).ap()
    sb = lambda n_, s_, d_=F32: stk.enter_context(nc.sbuf_tensor("b_" + n_, s_, d_))
    ps = lambda n_, s_, d_=F32: stk.enter_context(nc.psum_tensor("b_" + n_, s_, d_))

    x3f_d = dt("x3f", [D, T], F32, kind="ExternalInput")
    indT_d = dt("indT", [8, T], BF16, kind="ExternalInput")
    vp_d = [dt(f"vp{i}", [128, 8, 8], F32, kind="ExternalInput")
            for i in range(nvp)]
    ident_d = dt("ident", [128, 128], F32, kind="ExternalInput")
    onesr_d = dt("onesr", [1, 128], F32, kind="ExternalInput")
    onesc_d = dt("onesc", [128, 1], F32, kind="ExternalInput")
    cvec_d = dt("cvec", [128, 8], F32, kind="ExternalInput")
    x4_d = dt("x4", [T, D], F32, kind="ExternalOutput")

    vps = sb("vps", [128, nvp, 8, 8])
    vsum = sb("vsum", [128, 8, 8])
    vT = sb("vT", [8, 8, 128], BF16)
    indT_s = sb("indT_s", [8, T], BF16)
    ld = sb("ld", [128, 8, 512])
    r_f = sb("r_f", [128, 8, 512])
    o_tm = sb("o_tm", [128, 4, 1024])
    ident = sb("ident_s", [128, 128])
    onesr = sb("onesr_s", [1, 128]); onesc = sb("onesc_s", [128, 1])
    cvec = sb("cvec_s", [128, 8])
    rowm = sb("rowm", [1, 512]); rowv = sb("rowv", [1, 512])
    rowmm = sb("rowmm", [1, 512]); rowrs = sb("rowrs", [1, 512])
    rowmrs = sb("rowmrs", [1, 512])
    r2 = sb("r2", [128, 2, 512])

    pp = [ps(f"pp{i}", [128, 512]) for i in range(6)]
    ps_stat = ps("ps_stat", [1, 2, 512])

    for s_, d_ in [(ident, ident_d), (onesr, onesr_d), (onesc, onesc_d),
                   (cvec, cvec_d)]:
        st.d("sync", lambda e, s=s_, d=d_: e.dma_start(out=s[:], in_=d))
    for i in range(nvp):
        st.d("sync", lambda e, i=i: e.dma_start(out=vps[:, i, :, :], in_=vp_d[i]))
    st.d("scalar", lambda e: e.dma_start(out=indT_s[:], in_=indT_d))
    st.bar()
    stp = nvp // 2
    while stp >= 1:
        for i in range(stp):
            st.c("vector", lambda e, i=i, stp=stp: e.tensor_tensor(
                vps[:, i, :, :], vps[:, i, :, :], vps[:, i + stp, :, :], ALU.add))
        st.bar()
        stp //= 2
    st.c("vector", lambda e: e.tensor_copy(vsum[:], vps[:, 0, :, :]))
    st.bar()
    for m in range(8):
        st.c("tensor", lambda e, m=m: e.transpose(
            pp[m % 2][0:8, 0:128], vsum[:, :, m], ident[:]))
        st.bar()
        st.c("scalar", lambda e, m=m: e.copy(vT[:, m, :], pp[m % 2][0:8, 0:128]))
        st.bar()

    def do_chunk(c):
        c0 = c * CH
        for m in range(8):
            st.d("sync", lambda e, m=m: e.dma_start(
                out=ld[:, m, :], in_=x3f_d[128 * m:128 * (m + 1), c0:c0 + CH]))
        st.bar()

        def o_mm(m):
            st.c("tensor", lambda e, m=m: e.matmul(
                pp[m % 2][:], vT[:, m, :], indT_s[:, c0:c0 + CH],
                start=True, stop=True))

        def o_ev(m):
            st.c("vector", lambda e, m=m: e.tensor_tensor(
                r_f[:, m, :], pp[m % 2][:], ld[:, m, :], ALU.add))

        pipe(st, 8, [(0, o_mm), (1, o_ev)])
        emit_ln(nc, st, pp, (ps_stat[:, 0, :], ps_stat[:, 1, :]), r_f, r2, rowm, rowv, rowmm, rowrs,
                rowmrs, onesc, onesr, ld, None, cvec, outb_off=True)

        def t_mm(i):
            tt_, m = i // 8, i % 8
            st.c("tensor", lambda e, tt_=tt_, m=m, i=i: e.transpose(
                pp[2 + i % 2][:, 0:128], ld[:, m, 128 * tt_:128 * (tt_ + 1)],
                ident[:]))

        def t_ev(i):
            tt_, m = i // 8, i % 8
            st.c("vector", lambda e, tt_=tt_, m=m, i=i: e.tensor_copy(
                o_tm[:, tt_, 128 * m:128 * (m + 1)], pp[2 + i % 2][:, 0:128]))

        def t_st(i):
            tt_, m = i // 8, i % 8
            if m == 7:
                st.d("sync", lambda e, tt_=tt_: e.dma_start(
                    out=x4_d[c0 + 128 * tt_:c0 + 128 * (tt_ + 1), :],
                    in_=o_tm[:, tt_, :]))

        pipe(st, 32, [(0, t_mm), (1, t_ev), (2, t_st)])

    for c in range(NCH):
        do_chunk(c)


def build_l1f_a():
    nc = bass.Bass("TRN2", target_bir_lowering=False)
    with ExitStack() as stk:
        S = stk.enter_context(nc.semaphore("S"))
        S2 = stk.enter_context(nc.semaphore("S2"))
        block = stk.enter_context(nc.Block())
        st = Sched()
        emit_l1f_a(nc, st, stk)
        st.emit(block, S, S2)
    return nc


def build_l1f_b():
    nc = bass.Bass("TRN2", target_bir_lowering=False)
    with ExitStack() as stk:
        S = stk.enter_context(nc.semaphore("S"))
        S2 = stk.enter_context(nc.semaphore("S2"))
        block = stk.enter_context(nc.Block())
        st = Sched()
        emit_l1f_b(nc, st, stk)
        st.emit(block, S, S2)
    return nc


def l1f_host_inputs(g):
    out = common_consts()
    out["gwr2"] = np.ascontiguousarray(
        g["L1_ffn_gate_w"].reshape(8, 128, 8).transpose(1, 0, 2))
    out["gbb2"] = np.ascontiguousarray(
        np.broadcast_to(g["L1_ffn_gate_b"], (128, 8)))
    re3 = lambda w: np.ascontiguousarray(
        w.reshape(8, 8, 128, 1024).transpose(0, 2, 1, 3))
    out["w1r"] = re3(g["L1_ffn_w1"]); out["w2r"] = re3(g["L1_ffn_w2"])
    out["w3r"] = re3(g["L1_ffn_w3"])
    colc = lambda b: np.ascontiguousarray(
        b.reshape(8, 8, 128).transpose(2, 0, 1))
    out["b1c"] = colc(g["L1_ffn_b1"]); out["b2c"] = colc(g["L1_ffn_b2"])
    out["b3c"] = colc(g["L1_ffn_b3"])
    es = np.zeros((8, 8, 128), np.float32)
    for e in range(8):
        es[e, e, :] = 1.0
    out["esel"] = es
    return out


def build_l0a():
    nc = bass.Bass("TRN2", target_bir_lowering=False)
    with ExitStack() as stk:
        S = stk.enter_context(nc.semaphore("S"))
        S2 = stk.enter_context(nc.semaphore("S2"))
        block = stk.enter_context(nc.Block())
        st = Sched()
        emit_l0a(nc, st, stk)
        st.emit(block, S, S2)
    return nc


def l0a_host_inputs(common):
    g = common
    out = {}
    re_w = lambda w: np.ascontiguousarray(
        w.reshape(8, 128, 1024).transpose(1, 0, 2))
    out["wq"] = re_w(g["L0_attn_Wq"]); out["wk"] = re_w(g["L0_attn_Wk"])
    out["wv"] = re_w(g["L0_attn_Wv"]); out["wo"] = re_w(g["L0_attn_Wo"])
    col = lambda b: np.ascontiguousarray(b.reshape(8, 128).T)
    out["bqc"] = col(g["L0_attn_bq"]); out["bkc"] = col(g["L0_attn_bk"])
    out["boc"] = col(g["L0_attn_bo"])
    out["bkb"] = np.ascontiguousarray(np.broadcast_to(g["L0_attn_bk"], (128, 1024)))
    out["bvb"] = np.ascontiguousarray(np.broadcast_to(g["L0_attn_bv"], (128, 1024)))
    ang = (np.pi / (2.0 * T)) * np.arange(T, dtype=np.float32)
    cosw = np.cos(ang).astype(np.float32); sinw = np.sin(ang).astype(np.float32)
    out["cosb"] = np.ascontiguousarray(np.broadcast_to(cosw, (128, T)))
    out["sinb"] = np.ascontiguousarray(np.broadcast_to(sinw, (128, T)))
    out["ccol"] = np.ascontiguousarray(cosw.reshape(64, 128).T)
    out["scol"] = np.ascontiguousarray(sinw.reshape(64, 128).T)
    u = np.arange(128)[:, None]
    tloc = np.arange(512)[None, :]
    msk = np.stack([(128 * ub + u <= tloc).astype(np.float32) for ub in range(4)], 1)
    out["mask"] = np.ascontiguousarray(msk.astype(BF))
    out["ident"] = np.eye(128, dtype=np.float32)
    out["ones64"] = np.ones((1, 64), np.float32)
    out["onesr"] = np.ones((1, 128), np.float32)
    out["onesc"] = np.ones((128, 1), np.float32)
    cv = np.zeros((128, 8), np.float32)
    cv[:, 0] = LN_EPS; cv[:, 1] = -0.5; cv[:, 2] = -1.0; cv[:, 3] = 1.0
    cv[:, 4] = 1.702
    out["cvec"] = cv
    return out


def common_consts():
    out = {}
    out["ident"] = np.eye(128, dtype=np.float32)
    out["onesr"] = np.ones((1, 128), np.float32)
    out["onesc"] = np.ones((128, 1), np.float32)
    cv = np.zeros((128, 8), np.float32)
    cv[:, 0] = LN_EPS; cv[:, 1] = -0.5; cv[:, 2] = -1.0; cv[:, 3] = 1.0
    cv[:, 4] = 1.702
    out["cvec"] = cv
    return out


_PERM = np.concatenate([np.arange(0, 4096, 2), np.arange(1, 4096, 2)])


def l0f_host_inputs(g):
    out = common_consts()
    w1p = g["L0_ffn_w1"][:, :, _PERM]
    w3p = g["L0_ffn_w3"][:, :, _PERM]
    out["w1r"] = np.ascontiguousarray(
        w1p.reshape(8, 8, 128, 4096).transpose(0, 2, 1, 3))
    out["w3r"] = np.ascontiguousarray(
        w3p.reshape(8, 8, 128, 4096).transpose(0, 2, 1, 3))
    out["w2r"] = np.ascontiguousarray(
        g["L0_ffn_w2"].reshape(8, 16, 128, 1024).transpose(0, 2, 1, 3))
    sw1p = g["L0_ffn_sw1"][:, _PERM]; sw3p = g["L0_ffn_sw3"][:, _PERM]
    out["sw1r"] = np.ascontiguousarray(sw1p.reshape(8, 128, 4096).transpose(1, 0, 2))
    out["sw3r"] = np.ascontiguousarray(sw3p.reshape(8, 128, 4096).transpose(1, 0, 2))
    out["sw2r"] = np.ascontiguousarray(
        g["L0_ffn_sw2"].reshape(16, 128, 1024).transpose(1, 0, 2))
    b1p = g["L0_ffn_b1"][:, _PERM]; b3p = g["L0_ffn_b3"][:, _PERM]
    b1c = b1p.reshape(8, 32, 128).transpose(2, 0, 1)
    b3c = b3p.reshape(8, 32, 128).transpose(2, 0, 1)
    sb1 = g["L0_ffn_sb1"][_PERM].reshape(32, 128).T[:, None, :]
    sb3 = g["L0_ffn_sb3"][_PERM].reshape(32, 128).T[:, None, :]
    out["b1c"] = np.ascontiguousarray(np.concatenate([b1c, sb1], 1))
    out["b3c"] = np.ascontiguousarray(np.concatenate([b3c, sb3], 1))
    b2c = g["L0_ffn_b2"].reshape(8, 8, 128).transpose(2, 0, 1)
    sb2 = g["L0_ffn_sb2"].reshape(8, 128).T[:, None, :]
    out["b2c"] = np.ascontiguousarray(np.concatenate([b2c, sb2], 1))
    out["gwr"] = np.ascontiguousarray(
        g["L0_ffn_gate_w"].T.reshape(8, 128, 8).transpose(1, 0, 2))
    out["gbb"] = np.ascontiguousarray(
        np.broadcast_to(g["L0_ffn_gate_b"], (128, 8)))
    es = np.zeros((8, 8, 128), np.float32)
    for e in range(8):
        es[e, e, :] = 1.0
    out["esel"] = es
    return out


def build_fused():
    nc = bass.Bass("TRN2", target_bir_lowering=False)
    x1f = nc.dram_tensor("x1f", [D, T], F32).ap()
    x1b = nc.dram_tensor("x1b", [D, T], BF16).ap()
    x2f = nc.dram_tensor("x2f", [D, T], F32).ap()
    x2b = nc.dram_tensor("x2b", [D, T], BF16).ap()
    x3f = nc.dram_tensor("x3f", [D, T], F32).ap()
    x3b = nc.dram_tensor("x3b", [D, T], BF16).ap()
    vpart = nc.dram_tensor("vpart", [128, 8, 8], F32).ap()
    vsum = nc.dram_tensor("vsum", [128, 64], F32, addr_space="Shared").ap()
    indT = nc.dram_tensor("indT", [8, T], BF16).ap()
    with nc.semaphore("S") as S, nc.semaphore("S2") as S2, nc.Block() as block:
        st = Sched()
        with ExitStack() as s1:
            emit_l0a(nc, st, s1, pre={"x1f": x1f, "x1b": x1b})
            st.bar()
        with ExitStack() as s2:
            emit_l0f(nc, st, s2, pre={"x1f": x1f, "x1b": x1b,
                                      "x2f": x2f, "x2b": x2b})
            st.bar()
        with ExitStack() as s3:
            emit_l1a(nc, st, s3, pre={"x2f": x2f, "x2b": x2b,
                                      "x3f": x3f, "x3b": x3b})
            st.bar()
        with ExitStack() as s4:
            emit_l1f_a(nc, st, s4, pre={"x3b": x3b, "vpart": vpart,
                                        "indT": indT})
            st.bar()
        st.c("gpsimd", lambda e: e.collective_compute(
            "AllReduce", ALU.add, replica_groups=[list(range(8))],
            ins=[vpart.rearrange("p a b -> p (a b)")], outs=[vsum]))
        st.bar()
        with ExitStack() as s5:
            emit_l1f_b(nc, st, s5, nvp=1,
                       pre={"x3f": x3f, "indT": indT,
                            "vp0": vsum.rearrange("p (a b) -> p a b", b=8)})
            st.bar()
        st.emit(block, S, S2)
    return nc


def kernel(**inputs):
    n = 8
    host_a = l0a_host_inputs(inputs)
    host_f = l0f_host_inputs(inputs)
    host_m = l1a_host_inputs(inputs)
    host_m.pop("ident", None)
    host_p = l1f_host_inputs(inputs)
    hp_a = {k: host_p[k] for k in ["gwr2", "gbb2", "w1r", "w2r", "w3r",
                                   "b1c", "b2c", "b3c", "esel", "ident", "onesr"]}
    hp_b = {k: host_p[k] for k in ["ident", "onesr", "onesc", "cvec"]}
    base = {}
    for pfx, h in [("a", host_a), ("f", host_f), ("m", host_m),
                   ("p", hp_a), ("b", hp_b)]:
        for k, v in h.items():
            base[f"{pfx}_{k}"] = v
    nc = build_fused()
    in_maps = []
    for b in range(n):
        im = dict(base)
        im["a_x0"] = np.ascontiguousarray(inputs["L0_attn_x"][b])
        in_maps.append(im)
    res = run_bass_kernel_spmd(nc, in_maps, core_ids=list(range(n))).results
    return np.stack([np.asarray(res[b]["b_x4"]) for b in range(n)], axis=0)


def kernel_multi(**inputs):
    n = 8
    cores = list(range(n))
    x0s = [np.ascontiguousarray(inputs["L0_attn_x"][b]) for b in range(n)]

    host_a = l0a_host_inputs(inputs)
    nc1 = build_l0a()
    r1 = run_bass_kernel_spmd(
        nc1, [dict(host_a, x0=x0s[b]) for b in range(n)], core_ids=cores).results

    host_f = l0f_host_inputs(inputs)
    nc2 = build_l0f()
    r2 = run_bass_kernel_spmd(
        nc2, [dict(host_f, x1f=np.asarray(r1[b]["x1f"]),
                   x1b=np.asarray(r1[b]["x1b"])) for b in range(n)],
        core_ids=cores).results
    del r1

    host_m = l1a_host_inputs(inputs)
    nc3 = build_l1a()
    r3 = run_bass_kernel_spmd(
        nc3, [dict(host_m, x2f=np.asarray(r2[b]["x2f"]),
                   x2b=np.asarray(r2[b]["x2b"])) for b in range(n)],
        core_ids=cores).results
    del r2

    host_p = l1f_host_inputs(inputs)
    hp_a = {k: host_p[k] for k in ["gwr2", "gbb2", "w1r", "w2r", "w3r",
                                   "b1c", "b2c", "b3c", "esel", "ident", "onesr"]}
    nc4 = build_l1f_a()
    r4 = run_bass_kernel_spmd(
        nc4, [dict(hp_a, x3b=np.asarray(r3[b]["x3b"])) for b in range(n)],
        core_ids=cores).results

    vparts = [np.asarray(r4[b]["vpart"]) for b in range(n)]
    hp_b = {k: host_p[k] for k in ["ident", "onesr", "onesc", "cvec"]}
    nc5 = build_l1f_b()
    in5 = []
    for b in range(n):
        m = dict(hp_b, x3f=np.asarray(r3[b]["x3f"]),
                 indT=np.asarray(r4[b]["indT"]))
        for i in range(n):
            m[f"vp{i}"] = vparts[i]
        in5.append(m)
    r5 = run_bass_kernel_spmd(nc5, in5, core_ids=cores).results
    return np.stack([np.asarray(r5[b]["x4"]) for b in range(n)], axis=0)


# version 12
# speedup vs baseline: 1.0036x; 1.0036x over previous
import os
import numpy as np
import ml_dtypes
from contextlib import ExitStack

os.environ.setdefault("GAUGE_NTFF_IGNORE_DMA", "1")
import concourse.bass as bass
import concourse.mybir as mybir
from concourse.bass_utils import run_bass_kernel_spmd

F32 = mybir.dt.float32
BF16 = mybir.dt.bfloat16
AF = mybir.ActivationFunctionType
ALU = mybir.AluOpType
AX = mybir.AxisListType
BF = ml_dtypes.bfloat16

T, D, CH = 8192, 1024, 512
DBG_UPTO = None
DBG_CHUNKS = None
NCH = T // CH
H, DH = 16, 64
LN_EPS = 1e-5


class Sched:

    def __init__(self):
        self.stages = []
        self.cur = {}

    def c(self, eng, fn):
        self.cur.setdefault(eng, []).append((fn, "c", 0))

    def d(self, eng, fn, slack=0):
        self.cur.setdefault(eng, []).append((fn, "d", slack))

    def bar(self):
        if self.cur:
            self.stages.append(self.cur)
            self.cur = {}

    def emit(self, block, sem, sem2, sem3):
        self.bar()
        ns = len(self.stages)
        bucket = [0] * ns
        bucket2 = [0] * ns
        bucket3 = [0] * ns
        for s, stg in enumerate(self.stages):
            for eng, ops in stg.items():
                ncomp = 0
                for _, k, slack in ops:
                    if k == "d":
                        if slack:
                            bucket2[min(s + slack, ns - 1)] += 16
                        elif eng == "gpsimd":
                            bucket3[s] += 16
                        else:
                            bucket[s] += 16
                    else:
                        ncomp = 1
                bucket[s] += ncomp
        eff = [0] * (ns + 1)
        eff2 = [0] * (ns + 1)
        eff3 = [0] * (ns + 1)
        for s in range(ns):
            eff[s + 1] = eff[s] + bucket[s]
            eff2[s + 1] = eff2[s] + bucket2[s]
            eff3[s + 1] = eff3[s] + bucket3[s]
        total, total2, total3 = eff[ns], eff2[ns], eff3[ns]

        def emit_for(name, e):
            last2 = 0
            last3 = 0
            for s, stg in enumerate(self.stages):
                ops = stg.get(name)
                if not ops:
                    continue
                if eff[s]:
                    e.wait_ge(sem, eff[s])
                if eff2[s] > last2:
                    e.wait_ge(sem2, eff2[s])
                    last2 = eff2[s]
                if eff3[s] > last3:
                    e.wait_ge(sem3, eff3[s])
                    last3 = eff3[s]
                lastc = max(
                    (i for i, (_, k, _sl) in enumerate(ops) if k == "c"),
                    default=None,
                )
                for i, (fn, k, sl) in enumerate(ops):
                    inst = fn(e)
                    if k == "d":
                        inst.then_inc(
                            sem2 if sl else (sem3 if name == "gpsimd" else sem), 16)
                    elif i == lastc:
                        inst.then_inc(sem, 1)

        def make(nm):
            def body(e):
                emit_for(nm, e)
                if nm == "sync":
                    e.wait_ge(sem, total)
                    if total2:
                        e.wait_ge(sem2, total2)
                    if total3:
                        e.wait_ge(sem3, total3)
            body.__name__ = "_"
            return body

        for name in ["sync", "tensor", "vector", "scalar", "gpsimd"]:
            getattr(block, name)(make(name))
        return total


def pipe(st, n, phases, stride=1):
    moff = max(off for off, _ in phases)
    for s in range((n - 1) * stride + moff + 1):
        for off, fn in phases:
            q, r = divmod(s - off, stride)
            if r == 0 and 0 <= q < n:
                fn(q)
        st.bar()


def emit_l0a(nc, st, stk, pre=None):
    pre = pre or {}
    def dt(nm, shape, dtype, kind=None):
        if nm in pre:
            return pre[nm]
        if kind is None:
            return nc.dram_tensor("a_" + nm, shape, dtype).ap()
        return nc.dram_tensor("a_" + nm, shape, dtype, kind=kind).ap()
    sb = lambda n_, s_, d_=F32: stk.enter_context(nc.sbuf_tensor("a_" + n_, s_, d_))
    ps = lambda n_, s_, d_=F32: stk.enter_context(nc.psum_tensor("a_" + n_, s_, d_))

    x0_d = dt("x0", [T, D], F32, kind="ExternalInput")
    wq_d = dt("wq", [128, 8, 1024], F32, kind="ExternalInput")
    wk_d = dt("wk", [128, 8, 1024], F32, kind="ExternalInput")
    wv_d = dt("wv", [128, 8, 1024], F32, kind="ExternalInput")
    wo_d = dt("wo", [128, 8, 1024], F32, kind="ExternalInput")
    bqc_d = dt("bqc", [128, 8], F32, kind="ExternalInput")
    bkc_d = dt("bkc", [128, 8], F32, kind="ExternalInput")
    boc_d = dt("boc", [128, 8], F32, kind="ExternalInput")
    bkb_d = dt("bkb", [128, 1024], F32, kind="ExternalInput")
    bvb_d = dt("bvb", [128, 1024], F32, kind="ExternalInput")
    cosb_d = dt("cosb", [128, T], F32, kind="ExternalInput")
    sinb_d = dt("sinb", [128, T], F32, kind="ExternalInput")
    ccol_d = dt("ccol", [128, 64], F32, kind="ExternalInput")
    scol_d = dt("scol", [128, 64], F32, kind="ExternalInput")
    mask_d = dt("mask", [128, 4, 512], BF16, kind="ExternalInput")
    ident_d = dt("ident", [128, 128], F32, kind="ExternalInput")
    ones64_d = dt("ones64", [1, 64], F32, kind="ExternalInput")
    onesr_d = dt("onesr", [1, 128], F32, kind="ExternalInput")
    onesc_d = dt("onesc", [128, 1], F32, kind="ExternalInput")
    cvec_d = dt("cvec", [128, 8], F32, kind="ExternalInput")
    x1f_d = dt("x1f", [D, T], F32, kind="ExternalOutput")
    x1b_d = dt("x1b", [D, T], BF16, kind="ExternalOutput")

    w_q = sb("w_q", [128, 8, 1024], BF16)
    w_k = sb("w_k", [128, 8, 1024], BF16)
    w_v = sb("w_v", [128, 8, 1024], BF16)
    w_o = sb("w_o", [128, 8, 1024], BF16)
    bqc = sb("bqc_s", [128, 8]); bkc = sb("bkc_s", [128, 8]); boc = sb("boc_s", [128, 8])
    bkb = sb("bkb_s", [128, 1024]); bvb = sb("bvb_s", [128, 1024])
    ccol = sb("ccol_s", [128, 64]); scol = sb("scol_s", [128, 64])
    mask = sb("mask_s", [128, 4, 512], BF16)
    ident = sb("ident_s", [128, 128])
    ones64 = sb("ones64_s", [1, 64]); onesr = sb("onesr_s", [1, 128]); onesc = sb("onesc_s", [128, 1])
    cosb = sb("cosb_s", [128, 512]); sinb = sb("sinb_s", [128, 512])
    cvec = sb("cvec_s", [128, 8])
    big0 = sb("big0", [128, 4096])
    x_tm = big0[:].rearrange("p (a b) -> p a b", b=1024)
    r_f = big0[:].rearrange("p (a b) -> p a b", b=512)
    bufA = sb("bufA", [128, 8, 512])
    x_fm = bufA[:]
    x1f_t = bufA[:]
    bufB = sb("bufB", [128, 8, 512], BF16)
    x_fmb = bufB[:]
    y_b = bufB[:]
    x1b_t = bufB[:]
    qc = sb("qc", [128, 8, 512], BF16); qs = sb("qs", [128, 8, 512], BF16)
    kcf = sb("kcf", [128, 8, 512], BF16); ksf = sb("ksf", [128, 8, 512], BF16)
    kct = sb("kct", [128, 4, 1024], BF16); kst = sb("kst", [128, 4, 1024], BF16)
    v_tm = sb("v_tm", [128, 4, 16, 65], BF16)
    r2 = sb("r2", [128, 2, 512])
    tmp2 = r2
    am2 = big0[:, 0:2048].bitcast(BF16).rearrange("p (a u b) -> p a u b", a=2, b=512)
    den_r = sb("den_r", [1, 512]); rec_r = sb("rec_r", [1, 512])
    rowm = sb("rowm", [1, 512]); rowv = sb("rowv", [1, 512])
    rowmm = den_r
    rowrs = sb("rowrs", [1, 512]); rowmrs = sb("rowmrs", [1, 512])
    Sf = sb("Sf", [128, 8, 2, 65]); Sb = sb("Sb", [128, 8, 2, 65], BF16)

    pp = [ps(f"pp{i}", [128, 512]) for i in range(8)]

    st.d("gpsimd", lambda e: e.dma_start(out=w_q[:], in_=wq_d))
    st.d("gpsimd", lambda e: e.dma_start(out=w_k[:], in_=wk_d))
    st.d("gpsimd", lambda e: e.dma_start(out=w_v[:], in_=wv_d))
    st.d("gpsimd", lambda e: e.dma_start(out=w_o[:], in_=wo_d))
    for s_, d_ in [(bqc, bqc_d), (bkc, bkc_d), (boc, boc_d), (bkb, bkb_d),
                   (bvb, bvb_d), (ccol, ccol_d), (scol, scol_d), (mask, mask_d),
                   (ident, ident_d), (ones64, ones64_d), (onesr, onesr_d),
                   (onesc, onesc_d), (cvec, cvec_d)]:
        st.d("sync", lambda e, s=s_, d=d_: e.dma_start(out=s[:], in_=d))
    if DBG_UPTO != 'nomemset':
        st.c("vector", lambda e: e.memset(Sf[:], 0.0))
        st.c("vector", lambda e: e.memset(Sb[:], 0.0))
        st.c("vector", lambda e: e.memset(v_tm[:, :, :, 64:65], 1.0))
    st.bar()
    if DBG_UPTO in ('stage0', 'nomemset'):
        return

    def do_chunk(c):
        c0 = c * CH
        for tt in range(4):
            st.d("sync", lambda e, tt=tt: e.dma_start(
                out=x_tm[:, tt, :], in_=x0_d[c0 + 128 * tt: c0 + 128 * (tt + 1), :]))
        st.d("sync", lambda e: e.dma_start(out=cosb[:], in_=cosb_d[:, c0:c0 + CH]))
        st.d("sync", lambda e: e.dma_start(out=sinb[:], in_=sinb_d[:, c0:c0 + CH]))
        st.bar()
        if DBG_UPTO == 'ingdma':
            return

        def ing_mm(j):
            def f(e):
                last = None
                for tt in range(4):
                    last = e.transpose(pp[j % 2][:, 128 * tt:128 * (tt + 1)],
                                       x_tm[:, tt, 128 * j:128 * (j + 1)], ident[:])
                return last
            st.c("tensor", f)

        def ing_ev(j):
            st.c("vector", lambda e: e.tensor_copy(x_fm[:, j, :], pp[j % 2][:]))

        def ing_cp(j):
            st.c("scalar", lambda e: e.copy(x_fmb[:, j, :], x_fm[:, j, :]))

        if DBG_UPTO == 'ingmm':
            pipe(st, 8, [(0, ing_mm)])
            return
        if DBG_UPTO == 'ingev1':
            def ing_ev1(j):
                st.c("vector", lambda e: e.tensor_copy(x_fm[:, j, :], pp[j % 2][:]))
            pipe(st, 8, [(0, ing_mm), (1, ing_ev1)])
            return
        pipe(st, 8, [(0, ing_mm), (1, ing_ev), (2, ing_cp)])
        if DBG_UPTO == 'ingest':
            return

        for (w_, bc_, co_, so_) in [(w_q, bqc, qc, qs), (w_k, bkc, kcf, ksf)]:
            def p_mm(m, w_=w_):
                def f(e):
                    for k in range(8):
                        last = e.matmul(pp[2 + m % 2][:], w_[:, k, 128 * m:128 * (m + 1)],
                                 x_fmb[:, k, :], start=(k == 0), stop=(k == 7))
                    return last
                st.c("tensor", f)

            def p_act(m, bc_=bc_):
                st.c("scalar", lambda e: e.activation(
                    tmp2[:, m % 2, :], pp[2 + m % 2][:], AF.Relu, bias=bc_[:, m:m + 1]))

            def p_cs(m, co_=co_, so_=so_):
                st.c("vector", lambda e: e.tensor_tensor(
                    co_[:, m, :], tmp2[:, m % 2, :], cosb[:], ALU.mult))
                st.c("gpsimd", lambda e: e.tensor_tensor(
                    so_[:, m, :], tmp2[:, m % 2, :], sinb[:], ALU.mult))

            pipe(st, 8, [(0, p_mm), (1, p_act), (2, p_cs)])
        if DBG_UPTO == 'proj':
            return

        items = [(tt, hf) for tt in range(4) for hf in range(2)]

        def kt_mm(i):
            tt, hf = items[i]
            def f(e):
                for k in range(8):
                    last = e.matmul(pp[2 + i % 2][:], x_fmb[:, k, 128 * tt:128 * (tt + 1)],
                             w_k[:, k, 512 * hf:512 * (hf + 1)],
                             start=(k == 0), stop=(k == 7))
                return last
            st.c("tensor", f)

        def kt_add(i):
            tt, hf = items[i]
            st.c("vector", lambda e: e.tensor_tensor(
                tmp2[:, i % 2, :], pp[2 + i % 2][:], bkb[:, 512 * hf:512 * (hf + 1)], ALU.add))

        def kt_cs(i):
            tt, hf = items[i]
            cc = ccol[:, 4 * c + tt: 4 * c + tt + 1]
            sc = scol[:, 4 * c + tt: 4 * c + tt + 1]
            st.c("vector", lambda e: e.tensor_scalar(
                kct[:, tt, 512 * hf:512 * (hf + 1)], tmp2[:, i % 2, :], 0.0, cc,
                ALU.max, ALU.mult))
            st.c("gpsimd", lambda e: e.tensor_scalar(
                kst[:, tt, 512 * hf:512 * (hf + 1)], tmp2[:, i % 2, :], 0.0, sc,
                ALU.max, ALU.mult))

        pipe(st, 8, [(0, kt_mm), (1, kt_add), (2, kt_cs)])

        def vt_mm(i):
            tt, hf = items[i]
            def f(e):
                for k in range(8):
                    last = e.matmul(pp[2 + i % 2][:], x_fmb[:, k, 128 * tt:128 * (tt + 1)],
                             w_v[:, k, 512 * hf:512 * (hf + 1)],
                             start=(k == 0), stop=(k == 7))
                return last
            st.c("tensor", f)

        def vt_ev(i):
            tt, hf = items[i]
            st.c("vector", lambda e: e.tensor_tensor(
                v_tm[:, tt, 8 * hf:8 * (hf + 1), 0:64],
                pp[2 + i % 2][:].rearrange("p (h x) -> p h x", x=64),
                bvb[:, 512 * hf:512 * (hf + 1)].rearrange("p (h x) -> p h x", x=64),
                ALU.add))

        pipe(st, 8, [(0, vt_mm), (1, vt_ev)])
        if DBG_UPTO == 'ktvt':
            return

        def hA(h):
            ht, hr, he = h // 2, 64 * (h % 2), 64 * (h % 2) + 64
            q = 4 * (h % 2)
            def f(e):
                last = None
                for ub in range(4):
                    e.matmul(pp[q + ub][:], kcf[hr:he, ht, 128 * ub:128 * (ub + 1)],
                             qc[hr:he, ht, :], start=True, stop=False)
                    last = e.matmul(pp[q + ub][:],
                                    ksf[hr:he, ht, 128 * ub:128 * (ub + 1)],
                                    qs[hr:he, ht, :], start=False, stop=True)
                return last
            st.c("tensor", f)

        def hB(h):
            q = 4 * (h % 2)
            def f(e):
                last = None
                for ub in range(4):
                    last = e.tensor_tensor(am2[:, h % 2, ub, :], pp[q + ub][:],
                                           mask[:, ub, :], ALU.mult)
                return last
            st.c("vector", f)

        def hC(h):
            ht, hr, he = h // 2, 64 * (h % 2), 64 * (h % 2) + 64
            q = 4 * (h % 2)
            def f(e):
                e.matmul(pp[q][hr:he, :], Sb[hr:he, ht, 0, 0:64],
                         qc[hr:he, ht, :], start=True, stop=False)
                e.matmul(pp[q][hr:he, :], Sb[hr:he, ht, 1, 0:64],
                         qs[hr:he, ht, :], start=False, stop=False)
                e.matmul(pp[q + 1][0:1, :], Sb[hr:he, ht, 0, 64:65],
                         qc[hr:he, ht, :], start=True, stop=False)
                e.matmul(pp[q + 1][0:1, :], Sb[hr:he, ht, 1, 64:65],
                         qs[hr:he, ht, :], start=False, stop=False)
                for ub in range(4):
                    e.matmul(pp[q][hr:he, :], v_tm[:, ub, h, 0:64],
                             am2[:, h % 2, ub, :], start=False, stop=(ub == 3))
                for ub in range(4):
                    e.matmul(pp[q + 1][0:1, :], v_tm[:, ub, h, 64:65],
                             am2[:, h % 2, ub, :], start=False, stop=(ub == 3))
                for ub in range(4):
                    e.matmul(pp[q + 2][hr:he, 0:65], kct[:, ub, 64 * h:64 * (h + 1)],
                             v_tm[:, ub, h, :], start=(ub == 0), stop=(ub == 3))
                last = None
                for ub in range(4):
                    last = e.matmul(pp[q + 3][hr:he, 0:65],
                                    kst[:, ub, 64 * h:64 * (h + 1)],
                                    v_tm[:, ub, h, :], start=(ub == 0), stop=(ub == 3))
                return last
            st.c("tensor", f)

        def hD(h):
            q = 4 * (h % 2)
            st.c("vector", lambda e, q=q: e.tensor_scalar(
                den_r[:], pp[q + 1][0:1, :], 1e-6, None, ALU.max))

        def hE(h):
            st.c("vector", lambda e: e.reciprocal(rec_r[:], den_r[:]))

        def hF(h):
            ht, hr, he = h // 2, 64 * (h % 2), 64 * (h % 2) + 64
            q = 4 * (h % 2)
            numf_h = big0[hr:he, 2048:2560]
            st.c("tensor", lambda e, q=q, hr=hr, he=he: e.matmul(
                pp[q + 1][hr:he, :], ones64[:], rec_r[:], start=True, stop=True))
            def f(e):
                e.tensor_copy(numf_h[:], pp[q][hr:he, :])
                e.tensor_tensor(Sf[hr:he, ht, 0, :], Sf[hr:he, ht, 0, :],
                                pp[q + 2][hr:he, 0:65], ALU.add)
                return e.tensor_tensor(Sf[hr:he, ht, 1, :], Sf[hr:he, ht, 1, :],
                                       pp[q + 3][hr:he, 0:65], ALU.add)
            st.c("vector", f)

        def hG(h):
            ht, hr, he = h // 2, 64 * (h % 2), 64 * (h % 2) + 64
            q = 4 * (h % 2)
            numf_h = big0[hr:he, 2048:2560]
            st.c("vector", lambda e: e.tensor_tensor(
                y_b[hr:he, ht, :], numf_h[:], pp[q + 1][hr:he, :], ALU.mult))
            st.c("scalar", lambda e: e.copy(Sb[hr:he, ht, 0, :], Sf[hr:he, ht, 0, :]))
            st.c("scalar", lambda e: e.copy(Sb[hr:he, ht, 1, :], Sf[hr:he, ht, 1, :]))

        pipe(st, 16, [(0, hA), (1, hB), (2, hC), (3, hD), (4, hE),
                      (5, hF), (6, hG)], stride=4)
        def o_mm(m):
            def f(e):
                for k in range(8):
                    last = e.matmul(pp[2 + m % 2][:], w_o[:, k, 128 * m:128 * (m + 1)],
                             y_b[:, k, :], start=(k == 0), stop=(k == 7))
                return last
            st.c("tensor", f)

        def o_act(m):
            st.c("scalar", lambda e: e.activation(
                tmp2[:, m % 2, :], pp[2 + m % 2][:], AF.Identity, bias=boc[:, m:m + 1]))

        def o_res(m):
            st.c("vector", lambda e: e.tensor_tensor(
                r_f[:, m, :], tmp2[:, m % 2, :], x_fm[:, m, :], ALU.add))

        pipe(st, 8, [(0, o_mm), (1, o_act), (2, o_res)])
        if DBG_UPTO == 'omm':
            return

        emit_ln(nc, st, pp, (pp[6][0:1, :], pp[7][0:1, :]), r_f, r2, rowm, rowv, rowmm, rowrs, rowmrs,
                onesc, onesr, x1f_t, x1b_t, cvec)
        for m in range(8):
            st.d("sync", lambda e, m=m: e.dma_start(
                out=x1f_d[128 * m:128 * (m + 1), c0:c0 + CH], in_=x1f_t[:, m, :]))
            st.d("scalar", lambda e, m=m: e.dma_start(
                out=x1b_d[128 * m:128 * (m + 1), c0:c0 + CH], in_=x1b_t[:, m, :]))
        st.bar()

    for c in range(DBG_CHUNKS or NCH):
        do_chunk(c)


def emit_ln(nc, st, pp, ps_stat, r_f, r2, rowm, rowv, rowmm, rowrs, rowmrs,
            onesc, onesr, outf, outb, cvec, nt=8, outb_off=False):
    stat0, stat1 = ps_stat
    """LayerNorm over the partition(feature) axis of nt [128,512] fp32 tiles.
    Writes fp32 result to outf tiles and bf16 copy to outb."""
    def l_sq(m):
        st.c("gpsimd", lambda e: e.tensor_tensor(
            r2[:, m % 2, :], r_f[:, m, :], r_f[:, m, :], ALU.mult))

    def l_mm(m):
        def f(e):
            last = e.matmul(stat0, onesc[:], r_f[:, m, :],
                            start=(m == 0), stop=(m == nt - 1))
            last = e.matmul(stat1, onesc[:], r2[:, m % 2, :],
                            start=(m == 0), stop=(m == nt - 1))
            return last
        st.c("tensor", f)

    pipe(st, nt, [(0, l_sq), (1, l_mm)])
    inv = 1.0 / (128 * nt)

    def l_rows(e):
        e.tensor_scalar(rowm[:], stat0, inv, None, ALU.mult)
        return e.tensor_scalar(rowv[:], stat1, inv, None, ALU.mult)
    st.c("vector", l_rows)
    st.bar()
    st.c("vector", lambda e: e.tensor_tensor(rowmm[:], rowm[:], rowm[:], ALU.mult))
    st.bar()
    st.c("vector", lambda e: e.tensor_tensor(rowv[:], rowv[:], rowmm[:], ALU.subtract))
    st.bar()
    st.c("scalar", lambda e: e.activation(rowrs[:], rowv[:], AF.Ln, bias=cvec[0:1, 0:1]))
    st.bar()
    st.c("scalar", lambda e: e.activation(rowrs[:], rowrs[:], AF.Exp, scale=cvec[0:1, 1:2]))
    st.bar()
    st.c("vector", lambda e: e.tensor_tensor(rowmrs[:], rowm[:], rowrs[:], ALU.mult))
    st.bar()

    def l_bc(e):
        last = e.matmul(pp[0][:], onesr[:], rowrs[:], start=True, stop=True)
        last = e.matmul(pp[1][:], onesr[:], rowmrs[:], start=True, stop=True)
        return last
    st.c("tensor", l_bc)
    st.bar()

    def l_n1(m):
        st.c("vector", lambda e: e.tensor_tensor(
            r2[:, m % 2, :], r_f[:, m, :], pp[0][:], ALU.mult))

    def l_n2(m):
        st.c("vector", lambda e: e.tensor_tensor(
            outf[:, m, :], r2[:, m % 2, :], pp[1][:], ALU.subtract))

    def l_cp(m):
        st.c("scalar", lambda e: e.copy(outb[:, m, :], outf[:, m, :]))

    if outb_off:
        pipe(st, nt, [(0, l_n1), (1, l_n2)])
    else:
        pipe(st, nt, [(0, l_n1), (1, l_n2), (2, l_cp)])


NG = 1024
NGR = T // NG


def emit_l0f(nc, st, stk, pre=None):
    pre = pre or {}
    def dt(nm, shape, dtype, kind=None):
        if nm in pre:
            return pre[nm]
        if kind is None:
            return nc.dram_tensor("f_" + nm, shape, dtype).ap()
        return nc.dram_tensor("f_" + nm, shape, dtype, kind=kind).ap()
    sb = lambda n_, s_, d_=F32: stk.enter_context(nc.sbuf_tensor("f_" + n_, s_, d_))
    ps = lambda n_, s_, d_=F32: stk.enter_context(nc.psum_tensor("f_" + n_, s_, d_))

    x1f_d = dt("x1f", [D, T], F32, kind="ExternalInput")
    x1b_d = dt("x1b", [D, T], BF16, kind="ExternalInput")
    w1r_d = dt("w1r", [8, 128, 8, 4096], F32, kind="ExternalInput")
    w3r_d = dt("w3r", [8, 128, 8, 4096], F32, kind="ExternalInput")
    w2r_d = dt("w2r", [8, 128, 16, 1024], F32, kind="ExternalInput")
    sw1r_d = dt("sw1r", [128, 8, 4096], F32, kind="ExternalInput")
    sw3r_d = dt("sw3r", [128, 8, 4096], F32, kind="ExternalInput")
    sw2r_d = dt("sw2r", [128, 16, 1024], F32, kind="ExternalInput")
    b1c_d = dt("b1c", [128, 9, 32], F32, kind="ExternalInput")
    b3c_d = dt("b3c", [128, 9, 32], F32, kind="ExternalInput")
    b2c_d = dt("b2c", [128, 9, 8], F32, kind="ExternalInput")
    gwr_d = dt("gwr", [128, 8, 8], F32, kind="ExternalInput")
    gbb_d = dt("gbb", [128, 8], F32, kind="ExternalInput")
    esel_d = dt("esel", [8, 8, 128], F32, kind="ExternalInput")
    ident_d = dt("ident", [128, 128], F32, kind="ExternalInput")
    onesr_d = dt("onesr", [1, 128], F32, kind="ExternalInput")
    onesc_d = dt("onesc", [128, 1], F32, kind="ExternalInput")
    cvec_d = dt("cvec", [128, 8], F32, kind="ExternalInput")
    x2f_d = dt("x2f", [D, T], F32, kind="ExternalOutput")
    x2b_d = dt("x2b", [D, T], BF16, kind="ExternalOutput")

    w1c = dt("w1c", [8, 128, 8, 4096], BF16)
    w3c = dt("w3c", [8, 128, 8, 4096], BF16)
    w2c = dt("w2c", [8, 128, 16, 1024], BF16)
    sw1c = dt("sw1c", [128, 8, 4096], BF16)
    sw3c = dt("sw3c", [128, 8, 4096], BF16)
    sw2c = dt("sw2c", [128, 16, 1024], BF16)

    x1bg = sb("x1bg", [128, 8, NG], BF16)
    acc = sb("acc", [128, 8, NG])
    act = sb("act", [128, 16, NG], BF16)
    wbuf = sb("wbuf", [128, 2, 2, 2, 8, 128], BF16)
    w2s = sb("w2s", [128, 16, 1024], BF16)
    combT = sb("combT", [8, NG])
    comb_b = sb("comb_b", [128, NG], BF16)
    ld = sb("ld", [128, 8, 512])
    h1s = sb("h1s", [128, 2, 2, 512], BF16)
    swsc = sb("swsc", [128, 17, 512], BF16)
    tmp2 = sb("tmpf", [128, 2, 512])
    rt_s = sb("rt_s", [128, 8, 56])
    b1c = sb("b1c_s", [128, 9, 32]); b3c = sb("b3c_s", [128, 9, 32])
    b2c = sb("b2c_s", [128, 9, 8])
    gwr = sb("gwr_s", [128, 8, 8], BF16); gbb = sb("gbb_s", [128, 8])
    esel = sb("esel_s", [8, 8, 128])
    ident = sb("ident_s", [128, 128])
    onesr = sb("onesr_s", [1, 128]); onesc = sb("onesc_s", [128, 1])
    cvec = sb("cvec_s", [128, 8])
    rcol = sb("rcol", [128, 48])
    rowm = sb("rowm", [1, 512]); rowv = sb("rowv", [1, 512])
    rowmm = sb("rowmm", [1, 512]); rowrs = sb("rowrs", [1, 512])
    rowmrs = sb("rowmrs", [1, 512])
    r2 = sb("r2", [128, 2, 512])
    x2b_t = act[:, 0:8, 0:512]

    pp = [ps(f"pp{i}", [128, 512]) for i in range(8)]

    for s_, d_ in [(b1c, b1c_d), (b3c, b3c_d), (b2c, b2c_d),
                   (gbb, gbb_d), (esel, esel_d), (ident, ident_d),
                   (onesr, onesr_d), (onesc, onesc_d), (cvec, cvec_d)]:
        st.d("sync", lambda e, s=s_, d=d_: e.dma_start(out=s[:], in_=d))
    st.d("gpsimd", lambda e: e.dma_start(out=gwr[:], in_=gwr_d))
    st.bar()

    stg_8x2048 = [acc[:].bitcast(BF16),
                  act[:].rearrange("p (a x) b -> p a (x b)", x=2)]
    stg_16x1024 = [acc[:].bitcast(BF16).rearrange("p a (x b) -> p (a x) b", b=1024),
                   act[:]]
    pieces = []
    for e in range(8):
        for p in range(2):
            pieces.append((w1r_d[e, :, :, 2048 * p:2048 * (p + 1)],
                           w1c[e, :, :, 2048 * p:2048 * (p + 1)], (8, 2048)))
            pieces.append((w3r_d[e, :, :, 2048 * p:2048 * (p + 1)],
                           w3c[e, :, :, 2048 * p:2048 * (p + 1)], (8, 2048)))
        pieces.append((w2r_d[e], w2c[e], (16, 1024)))
    for p in range(2):
        pieces.append((sw1r_d[:, :, 2048 * p:2048 * (p + 1)],
                       sw1c[:, :, 2048 * p:2048 * (p + 1)], (8, 2048)))
        pieces.append((sw3r_d[:, :, 2048 * p:2048 * (p + 1)],
                       sw3c[:, :, 2048 * p:2048 * (p + 1)], (8, 2048)))
    pieces.append((sw2r_d, sw2c, (16, 1024)))

    def pc_buf(i):
        a_, b_ = pieces[i][2]
        return (stg_8x2048 if a_ == 8 else stg_16x1024)[i % 2]

    def pc_ld(i):
        a_, b_ = pieces[i][2]
        buf = pc_buf(i)
        for a in range(a_):
            st.d("gpsimd", lambda e, a=a, buf=buf: e.dma_start(
                out=buf[:, a, :], in_=pieces[i][0][:, a, :],
                max_dma_last_dim=4096))

    def pc_stn(i):
        a_, b_ = pieces[i][2]
        buf = pc_buf(i)
        for a in range(a_):
            st.d("sync", lambda e, a=a, buf=buf: e.dma_start(
                out=pieces[i][1][:, a, :], in_=buf[:, a, :]))

    pipe(st, len(pieces), [(0, pc_ld), (1, pc_stn)])

    def do_expert(g, e_i):
        shared = (e_i == 8)
        w1src = sw1c if shared else w1c[e_i]
        w3src = sw3c if shared else w3c[e_i]
        w2src = sw2c if shared else w2c[e_i]
        NI = NG // 512

        if not shared:
            def cb_mm(q):
                st.c("tensor", lambda e, q=q: e.matmul(
                    pp[q % 2][:], esel[:, e_i, :], combT[:, 512 * q:512 * (q + 1)],
                    start=True, stop=True))

            def cb_ev(q):
                st.c("vector", lambda e, q=q: e.tensor_copy(
                    comb_b[:, 512 * q:512 * (q + 1)], pp[q % 2][:]))

            pipe(st, NI, [(0, cb_mm), (1, cb_ev)])

        st.d("sync", lambda e: e.dma_start(out=w2s[:], in_=w2src), slack=0)
        for w_ in range(2):
            for gl in range(2):
                src = (w1src if w_ == 0 else w3src)
                st.d("scalar", lambda e, w_=w_, gl=gl, src=src: e.dma_start(
                    out=wbuf[:, 0, w_, gl, :, :],
                    in_=src[:, :, 128 * (16 * gl):128 * (16 * gl) + 128]))
        st.bar()

        items = [(j, n4) for j in range(16) for n4 in range(NI)]

        def ph_ld(i):
            j, n4 = items[i]
            if n4 == 0 and j + 1 < 16:
                jn = j + 1
                for w_ in range(2):
                    for gl in range(2):
                        src = (w1src if w_ == 0 else w3src)
                        st.d("sync", lambda e, w_=w_, gl=gl, src=src, jn=jn:
                             e.dma_start(
                                 out=wbuf[:, jn % 2, w_, gl, :, :],
                                 in_=src[:, :, 128 * (jn + 16 * gl):
                                         128 * (jn + 16 * gl) + 128]),
                             slack=0)

        def ph_mm(i):
            j, n4 = items[i]
            q = 4 * (i % 2)
            def f(e):
                last = None
                for w_ in range(2):
                    for gl in range(2):
                        for k in range(8):
                            last = e.matmul(pp[q + 2 * w_ + gl][:],
                                            wbuf[:, j % 2, w_, gl, k, :],
                                            x1bg[:, k, 512 * n4:512 * (n4 + 1)],
                                            start=(k == 0), stop=(k == 7))
                return last
            st.c("tensor", f)

        def ph_ev(i):
            j, n4 = items[i]
            q = 4 * (i % 2)
            st.c("scalar", lambda e: e.activation(
                h1s[:, i % 2, 0, :], pp[q][:], AF.Identity,
                bias=b1c[:, e_i, j:j + 1]))
            st.c("scalar", lambda e: e.activation(
                h1s[:, i % 2, 1, :], pp[q + 1][:], AF.Identity,
                bias=b1c[:, e_i, j + 16:j + 17]))
            st.c("vector", lambda e: e.tensor_scalar(
                swsc[:, i % 2, :], pp[q + 2][:],
                b3c[:, e_i, j:j + 1], None, ALU.add))
            st.c("vector", lambda e: e.tensor_scalar(
                swsc[:, 2 + i % 2, :], pp[q + 3][:],
                b3c[:, e_i, j + 16:j + 17], None, ALU.add))

        def ph_prod(i):
            def f(e):
                e.tensor_tensor(swsc[:, 4 + i % 4, :], h1s[:, i % 2, 0, :],
                                swsc[:, i % 2, :], ALU.mult)
                return e.tensor_tensor(swsc[:, 8 + i % 5, :], h1s[:, i % 2, 1, :],
                                       swsc[:, 2 + i % 2, :], ALU.mult)
            st.c("vector", f)

        def ph_min(i):
            def f(e):
                e.tensor_scalar(swsc[:, 4 + i % 4, :], swsc[:, 4 + i % 4, :],
                                7.0, None, ALU.min)
                return e.tensor_scalar(swsc[:, 8 + i % 5, :], swsc[:, 8 + i % 5, :],
                                       7.0, None, ALU.min)
            st.c("vector", f)

        def ph_lcm(i):
            st.c("vector", lambda e: e.tensor_scalar(
                swsc[:, 8 + i % 5, :], swsc[:, 8 + i % 5, :], -7.0, 1.0,
                ALU.max, ALU.add))
            st.c("scalar", lambda e: e.activation(
                swsc[:, 13 + i % 2, :], swsc[:, 4 + i % 4, :], AF.Sigmoid,
                scale=cvec[:, 4:5]))

        def ph_cmb(i):
            j, n4 = items[i]
            if not shared:
                st.c("gpsimd", lambda e, n4=n4: e.tensor_tensor(
                    swsc[:, 8 + i % 5, :], swsc[:, 8 + i % 5, :],
                    comb_b[:, 512 * n4:512 * (n4 + 1)], ALU.mult))
            st.c("vector", lambda e: e.tensor_tensor(
                swsc[:, 15 + i % 2, :], swsc[:, 4 + i % 4, :],
                swsc[:, 13 + i % 2, :], ALU.mult))

        def ph_fin(i):
            j, n4 = items[i]
            st.c("vector", lambda e, j=j, n4=n4: e.tensor_tensor(
                act[:, j, 512 * n4:512 * (n4 + 1)], swsc[:, 15 + i % 2, :],
                swsc[:, 8 + i % 5, :], ALU.mult))

        pipe(st, len(items), [(0, ph_ld), (0, ph_mm), (1, ph_ev),
                              (2, ph_prod), (3, ph_min), (4, ph_lcm),
                              (5, ph_cmb), (6, ph_fin)])

        w2items = [(mo, n4) for mo in range(8) for n4 in range(NI)]

        def w2_mm(i):
            mo, n4 = w2items[i]
            def f(e):
                last = None
                for k in range(16):
                    last = e.matmul(pp[i % 6][:], w2s[:, k, 128 * mo:128 * (mo + 1)],
                                    act[:, k, 512 * n4:512 * (n4 + 1)],
                                    start=(k == 0), stop=(k == 15))
                return last
            st.c("tensor", f)

        def w2_ev(i):
            mo, n4 = w2items[i]
            if shared:
                st.c("vector", lambda e: e.tensor_scalar(
                    tmp2[:, i % 2, :], pp[i % 6][:], b2c[:, e_i, mo:mo + 1],
                    None, ALU.add))
            else:
                st.c("vector", lambda e: e.scalar_tensor_tensor(
                    tmp2[:, i % 2, :], comb_b[:, 512 * n4:512 * (n4 + 1)],
                    b2c[:, e_i, mo:mo + 1], pp[i % 6][:], ALU.mult, ALU.add))

        def w2_acc(i):
            mo, n4 = w2items[i]
            st.c("gpsimd", lambda e: e.tensor_tensor(
                acc[:, mo, 512 * n4:512 * (n4 + 1)],
                acc[:, mo, 512 * n4:512 * (n4 + 1)], tmp2[:, i % 2, :], ALU.add))

        pipe(st, len(w2items), [(0, w2_mm), (1, w2_ev), (2, w2_acc)])

    def do_group(g):
        g0 = g * NG

        for k in range(8):
            st.d("sync", lambda e, k=k: e.dma_start(
                out=x1bg[:, k, :], in_=x1b_d[128 * k:128 * (k + 1), g0:g0 + NG]))
        st.c("vector", lambda e: e.memset(acc[:], 0.0))
        st.bar()

        def rt_mm(i):
            def f(e):
                last = None
                for k in range(8):
                    last = e.matmul(pp[i % 3][:, 0:8],
                                    x1bg[:, k, 128 * i:128 * (i + 1)],
                                    gwr[:, k, :], start=(k == 0), stop=(k == 7))
                return last
            st.c("tensor", f)

        def _sl(i, k):
            return rt_s[:, i % 8, 8 * k:8 * k + 8]

        def _rc(i, k):
            return rcol[:, 6 * (i % 8) + k:6 * (i % 8) + k + 1]

        def rt_max(i):
            st.c("vector", lambda e: e.tensor_reduce(
                _rc(i, 0), pp[i % 3][:, 0:8], AX.X, ALU.max, negate=True))

        def rt_exp(i):
            st.c("scalar", lambda e: e.activation(
                _sl(i, 0), pp[i % 3][:, 0:8], AF.Exp, bias=_rc(i, 0)))

        def rt_z(i):
            st.c("vector", lambda e: e.tensor_reduce(_rc(i, 1), _sl(i, 0), AX.X, ALU.add))

        def rt_rz(i):
            st.c("vector", lambda e: e.reciprocal(_rc(i, 2), _rc(i, 1)))

        def rt_pb(i):
            def f(e):
                e.tensor_scalar(_sl(i, 1), _sl(i, 0), _rc(i, 2), None, ALU.mult)
                return e.scalar_tensor_tensor(
                    _sl(i, 2), _sl(i, 0), _rc(i, 2), gbb[:], ALU.mult, ALU.add)
            st.c("vector", f)

        def rt_mx1(i):
            st.c("vector", lambda e: e.tensor_reduce(_rc(i, 3), _sl(i, 2), AX.X, ALU.max))

        def rt_m1(i):
            st.c("vector", lambda e: e.tensor_scalar(
                _sl(i, 4), _sl(i, 2), _rc(i, 3), None, ALU.is_equal))

        def rt_b2(i):
            st.c("vector", lambda e: e.scalar_tensor_tensor(
                _sl(i, 3), _sl(i, 4), -1e30, _sl(i, 2), ALU.mult, ALU.add))

        def rt_mx2(i):
            st.c("vector", lambda e: e.tensor_reduce(_rc(i, 4), _sl(i, 3), AX.X, ALU.max))

        def rt_m2(i):
            st.c("vector", lambda e: e.tensor_scalar(
                _sl(i, 5), _sl(i, 3), _rc(i, 4), None, ALU.is_equal))

        def rt_m12(i):
            st.c("vector", lambda e: e.tensor_tensor(
                _sl(i, 5), _sl(i, 4), _sl(i, 5), ALU.add))

        def rt_cmb(i):
            st.c("vector", lambda e: e.tensor_tensor(
                _sl(i, 6), _sl(i, 1), _sl(i, 5), ALU.mult))

        def rt_tp(i):
            st.c("tensor", lambda e: e.transpose(
                pp[3 + (i // 4) % 2][0:8, 128 * (i % 4):128 * (i % 4 + 1)],
                _sl(i, 6), ident[:]))

        def rt_ev(i):
            if i % 4 == 3:
                q = i // 4
                st.c("vector", lambda e, q=q: e.tensor_copy(
                    combT[:, 512 * q:512 * (q + 1)], pp[3 + q % 2][0:8, :]))

        pipe(st, NG // 128, [(0, rt_mm), (1, rt_max), (2, rt_exp), (3, rt_z),
                             (4, rt_rz), (5, rt_pb), (6, rt_mx1), (7, rt_m1),
                             (8, rt_b2), (9, rt_mx2), (10, rt_m2), (11, rt_m12),
                             (12, rt_cmb), (13, rt_tp), (14, rt_ev)])



        for e_i in range(9):
            do_expert(g, e_i)

        for cc in range(NG // CH):
            t0 = g0 + cc * CH
            for m in range(8):
                st.d("sync", lambda e, m=m, t0=t0: e.dma_start(
                    out=ld[:, m, :], in_=x1f_d[128 * m:128 * (m + 1), t0:t0 + CH]))
            st.bar()
            rr = acc[:, :, cc * CH:(cc + 1) * CH]
            for m in range(8):
                st.c("vector", lambda e, m=m, rr=rr: e.tensor_tensor(
                    rr[:, m, :], rr[:, m, :], ld[:, m, :], ALU.add))
            st.bar()
            emit_ln(nc, st, pp, (pp[4][0:1, :], pp[5][0:1, :]), rr, r2, rowm, rowv, rowmm, rowrs,
                    rowmrs, onesc, onesr, ld, x2b_t, cvec)
            for m in range(8):
                st.d("sync", lambda e, m=m, t0=t0: e.dma_start(
                    out=x2f_d[128 * m:128 * (m + 1), t0:t0 + CH], in_=ld[:, m, :]))
                st.d("scalar", lambda e, m=m, t0=t0: e.dma_start(
                    out=x2b_d[128 * m:128 * (m + 1), t0:t0 + CH], in_=x2b_t[:, m, :]))
            st.bar()

    for g in range(NGR):
        do_group(g)


def build_l0f():
    nc = bass.Bass("TRN2", target_bir_lowering=False)
    with ExitStack() as stk:
        S = stk.enter_context(nc.semaphore("S"))
        S2 = stk.enter_context(nc.semaphore("S2"))
        S3 = stk.enter_context(nc.semaphore("S3"))
        block = stk.enter_context(nc.Block())
        st = Sched()
        emit_l0f(nc, st, stk)
        st.emit(block, S, S2, S3)
    return nc


def emit_l1a(nc, st, stk, pre=None):
    pre = pre or {}
    def dt(nm, shape, dtype, kind=None):
        if nm in pre:
            return pre[nm]
        if kind is None:
            return nc.dram_tensor("m_" + nm, shape, dtype).ap()
        return nc.dram_tensor("m_" + nm, shape, dtype, kind=kind).ap()
    sb = lambda n_, s_, d_=F32: stk.enter_context(nc.sbuf_tensor("m_" + n_, s_, d_))
    ps = lambda n_, s_, d_=F32: stk.enter_context(nc.psum_tensor("m_" + n_, s_, d_))

    x2f_d = dt("x2f", [D, T], F32, kind="ExternalInput")
    x2b_d = dt("x2b", [D, T], BF16, kind="ExternalInput")
    winr_d = dt("winr", [128, 8, 4096], F32, kind="ExternalInput")
    woutr_d = dt("woutr", [128, 16, 1024], F32, kind="ExternalInput")
    wxp_d = dt("wxp", [128, 16, 80], F32, kind="ExternalInput")
    cwc_d = dt("cwc", [128, 16, 4], F32, kind="ExternalInput")
    cbc_d = dt("cbc", [128, 16], F32, kind="ExternalInput")
    arlog_d = dt("arlog", [128, 256], F32, kind="ExternalInput")
    dcol_d = dt("dcol", [128, 16], F32, kind="ExternalInput")
    repm_d = dt("repm", [128, 128], F32, kind="ExternalInput")
    sumsel_d = dt("sumsel", [128, 64], F32, kind="ExternalInput")
    selp_d = dt("selp", [48, 8, 128], F32, kind="ExternalInput")
    onesr_d = dt("onesr", [1, 128], F32, kind="ExternalInput")
    onesc_d = dt("onesc", [128, 1], F32, kind="ExternalInput")
    cvec_d = dt("cvec", [128, 8], F32, kind="ExternalInput")
    x3f_d = dt("x3f", [D, T], F32, kind="ExternalOutput")
    x3b_d = dt("x3b", [D, T], BF16, kind="ExternalOutput")

    winc = dt("winc", [128, 8, 4096], BF16)
    woutc = dt("woutc", [128, 16, 1024], BF16)

    xin = sb("xin", [128, 16, 515])
    tails = sb("tails", [128, 16, 3])
    xcb = sb("xcb", [128, 16, 512], BF16)
    szb = sb("szb", [128, 16, 512], BF16)
    yb = xin[:].rearrange("p a b -> p (a b)")[:, 4096:8192].bitcast(BF16).rearrange(
        "p (a b) -> p a b", b=512)
    yzb = yb
    x2bg = sb("x2bg", [128, 8, 512], BF16)
    ld = sb("ld", [128, 8, 512])
    r_f = xin[:].rearrange("p a b -> p (a b)")[:, 0:4096].rearrange(
        "p (a b) -> p a b", b=512)
    wbuf = sb("wbuf", [128, 3, 8, 128], BF16)
    wbuf2 = sb("wbuf2", [128, 2, 16, 128], BF16)
    wxs = sb("wxs", [128, 16, 80], BF16)
    cwc = sb("cwc_s", [128, 16, 4]); cbc = sb("cbc_s", [128, 16])
    ar = sb("ar_s", [128, 256]); arlog = sb("arlog_s", [128, 256])
    dcol = sb("dcol_s", [128, 16])
    repm = sb("repm_s", [128, 128], BF16)
    sumsel = sb("sumsel_s", [128, 64], BF16)
    selp = sb("selp_s", [48, 8, 128], BF16)
    onesr = sb("onesr_s", [1, 128]); onesc = sb("onesc_s", [128, 1])
    cvec = sb("cvec_s", [128, 8])
    ssm = sb("ssm_s", [80, 512])
    bpr = sb("bpr", [16, 512], BF16); cprt = sb("cprt", [48, 512], BF16)
    dtb = sb("dtb", [128, 512])
    Bb = sb("Bb", [128, 8, 512], BF16); Cb = sb("Cb", [128, 8, 512], BF16)
    dA = sb("dA", [128, 8, 512], BF16)
    bt = sb("bt", [128, 4, 512], BF16)
    ht = sb("ht", [128, 4, 512], BF16)
    hct = sb("hct", [128, 4, 512], BF16)
    xcrb = sb("xcrb", [128, 2, 512], BF16)
    stcol = sb("stcol", [128, 256])
    cv1 = sb("cv1", [128, 2, 1024]); cv2 = sb("cv2", [128, 2, 1024])
    cv3 = sb("cv3", [128, 2, 512])
    srow = sb("srow", [1, 6, 512])
    rowm = sb("rowm", [1, 512]); rowv = sb("rowv", [1, 512])
    rowmm = sb("rowmm", [1, 512]); rowrs = sb("rowrs", [1, 512])
    rowmrs = sb("rowmrs", [1, 512])
    r2 = sb("r2", [128, 2, 512])

    pp = [ps(f"pp{i}", [128, 512]) for i in range(4)]
    ps_y = ps("ps_y", [128, 2, 512])
    ps_stat = ps("ps_stat", [1, 2, 512])

    for s_, d_ in [(cwc, cwc_d), (cbc, cbc_d), (arlog, arlog_d), (dcol, dcol_d),
                   (onesr, onesr_d), (onesc, onesc_d), (cvec, cvec_d)]:
        st.d("sync", lambda e, s=s_, d=d_: e.dma_start(out=s[:], in_=d))
    for s_, d_ in [(wxs, wxp_d), (repm, repm_d), (sumsel, sumsel_d),
                   (selp, selp_d)]:
        st.d("gpsimd", lambda e, s=s_, d=d_: e.dma_start(out=s[:], in_=d))
    st.c("vector", lambda e: e.memset(stcol[:], 0.0))
    st.bar()
    st.c("scalar", lambda e: e.activation(ar[:], arlog[:], AF.Exp))
    st.bar()
    st.c("vector", lambda e: e.tensor_scalar(ar[:], ar[:], -1.0, None, ALU.mult))
    st.bar()

    stg = [xin[:].rearrange("p a b -> p (a b)")[:, 0:4096].bitcast(BF16)
           .rearrange("p (a b) -> p a b", b=1024),
           xin[:].rearrange("p a b -> p (a b)")[:, 4096:8192].bitcast(BF16)
           .rearrange("p (a b) -> p a b", b=1024)]
    pieces = []
    for q in range(4):
        pieces.append((winr_d[:, :, 1024 * q:1024 * (q + 1)],
                       winc[:, :, 1024 * q:1024 * (q + 1)], 8))
    pieces.append((woutr_d[:, 0:8, :], woutc[:, 0:8, :], 8))
    pieces.append((woutr_d[:, 8:16, :], woutc[:, 8:16, :], 8))

    def pc_ld(i):
        src, dst, na = pieces[i]
        for a in range(na):
            st.d("gpsimd", lambda e, a=a, src=src, i=i: e.dma_start(
                out=stg[i % 2][:, a, :], in_=src[:, a, :], max_dma_last_dim=4096))

    def pc_stn(i):
        src, dst, na = pieces[i]
        for a in range(na):
            st.d("sync", lambda e, a=a, dst=dst, i=i: e.dma_start(
                out=dst[:, a, :], in_=stg[i % 2][:, a, :]))

    pipe(st, len(pieces), [(0, pc_ld), (1, pc_stn)])
    st.bar()

    def do_chunk(c):
        c0 = c * CH
        for k in range(8):
            st.d("sync", lambda e, k=k: e.dma_start(
                out=x2bg[:, k, :], in_=x2b_d[128 * k:128 * (k + 1), c0:c0 + CH]))
        if c > 0:
            for j in range(16):
                st.c("gpsimd", lambda e, j=j: e.tensor_copy(
                    xin[:, j, 0:3], tails[:, j, :]))
        else:
            st.c("gpsimd", lambda e: e.memset(xin[:, :, 0:3], 0.0))
        for w_ in range(2):
            st.d("scalar", lambda e, w_=w_: e.dma_start(
                out=wbuf[:, w_, :, :],
                in_=winc[:, :, 128 * w_:128 * (w_ + 1)]))
        st.bar()

        def wi_ld(m):
            if m + 2 < 32:
                st.d("sync", lambda e, m=m: e.dma_start(
                    out=wbuf[:, (m + 2) % 3, :, :],
                    in_=winc[:, :, 128 * (m + 2):128 * (m + 3)]), slack=0)

        def wi_mm(m):
            def f(e):
                last = None
                for k in range(8):
                    last = e.matmul(pp[m % 2][:], wbuf[:, m % 3, k, :],
                                    x2bg[:, k, :], start=(k == 0), stop=(k == 7))
                return last
            st.c("tensor", f)

        def wi_ev(m):
            if m < 16:
                st.c("vector", lambda e, m=m: e.tensor_copy(
                    xin[:, m, 3:515], pp[m % 2][:]))
            else:
                st.c("scalar", lambda e, m=m: e.activation(
                    szb[:, m - 16, :], pp[m % 2][:], AF.Silu))

        pipe(st, 32, [(0, wi_ld), (0, wi_mm), (1, wi_ev)])

        def cv_a(j):
            def f(e):
                e.tensor_scalar(cv1[:, j % 2, 0:512], xin[:, j, 0:512],
                                cwc[:, j, 0:1], None, ALU.mult)
                e.tensor_scalar(cv1[:, j % 2, 512:1024], xin[:, j, 1:513],
                                cwc[:, j, 1:2], None, ALU.mult)
                e.tensor_scalar(cv2[:, j % 2, 0:512], xin[:, j, 2:514],
                                cwc[:, j, 2:3], None, ALU.mult)
                return e.tensor_scalar(cv2[:, j % 2, 512:1024], xin[:, j, 3:515],
                                       cwc[:, j, 3:4], None, ALU.mult)
            st.c("vector", f)

        def cv_b(j):
            def f(e):
                e.tensor_tensor(cv1[:, j % 2, 0:512], cv1[:, j % 2, 0:512],
                                cv1[:, j % 2, 512:1024], ALU.add)
                return e.tensor_tensor(cv2[:, j % 2, 0:512], cv2[:, j % 2, 0:512],
                                       cv2[:, j % 2, 512:1024], ALU.add)
            st.c("gpsimd", f)

        def cv_c(j):
            st.c("vector", lambda e, j=j: e.tensor_tensor(
                cv3[:, j % 2, :], cv1[:, j % 2, 0:512], cv2[:, j % 2, 0:512], ALU.add))

        def cv_d(j):
            st.c("scalar", lambda e, j=j: e.activation(
                xcb[:, j, :], cv3[:, j % 2, :], AF.Silu, bias=cbc[:, j:j + 1]))

        pipe(st, 16, [(0, cv_a), (1, cv_b), (2, cv_c), (3, cv_d)])
        for j in range(16):
            st.c("vector", lambda e, j=j: e.tensor_copy(
                tails[:, j, :], xin[:, j, 512:515]))
        st.bar()

        def wx_mm(e):
            last = None
            for k in range(16):
                last = e.matmul(pp[2][0:80, :], wxs[:, k, :], xcb[:, k, :],
                                start=(k == 0), stop=(k == 15))
            return last
        st.c("tensor", wx_mm)
        st.bar()
        st.c("vector", lambda e: e.tensor_copy(ssm[:], pp[2][0:80, :]))
        st.bar()

        st.d("scalar", lambda e: e.dma_start(out=srow[:, 0, :], in_=ssm[64:65, :]))
        st.bar()
        dtr = srow[:, 0, :]
        st.c("vector", lambda e: e.tensor_scalar(
            srow[:, 1, :], dtr, -1.0, None, ALU.mult))
        st.bar()
        st.c("vector", lambda e: e.tensor_tensor(
            srow[:, 1, :], srow[:, 1, :], dtr, ALU.max))
        st.bar()
        st.c("scalar", lambda e: e.activation(
            srow[:, 2, :], srow[:, 1, :], AF.Exp, scale=cvec[0:1, 2:3]))
        st.bar()
        st.c("scalar", lambda e: e.activation(
            srow[:, 3, :], srow[:, 2, :], AF.Ln, bias=cvec[0:1, 3:4]))
        st.c("vector", lambda e: e.tensor_scalar(
            srow[:, 4, :], dtr, 0.0, None, ALU.max))
        st.bar()
        st.c("vector", lambda e: e.tensor_tensor(
            srow[:, 5, :], srow[:, 3, :], srow[:, 4, :], ALU.add))
        st.bar()
        st.c("tensor", lambda e: e.matmul(pp[3][:], onesr[:], srow[:, 5, :],
                                          start=True, stop=True))
        st.bar()
        st.c("vector", lambda e: e.tensor_copy(dtb[:], pp[3][:]))
        st.bar()
        st.c("vector", lambda e: e.tensor_tensor(
            bpr[:], ssm[0:16, :], pp[3][0:16, :], ALU.mult))
        st.c("scalar", lambda e: e.copy(cprt[32:48, :], ssm[32:48, :]))
        st.bar()

        def bc_mm(i):
            sp, which = i // 2, i % 2
            if which == 0:
                st.c("tensor", lambda e, sp=sp: e.matmul(
                    pp[i % 2][:], selp[0:16, sp, :], bpr[:], start=True, stop=True))
            else:
                st.c("tensor", lambda e, sp=sp: e.matmul(
                    pp[i % 2][:], selp[32:48, sp, :], cprt[32:48, :],
                    start=True, stop=True))

        def bc_ev(i):
            sp, which = i // 2, i % 2
            dst = Bb if which == 0 else Cb
            st.c("vector", lambda e: e.tensor_copy(dst[:, sp, :], pp[i % 2][:]))

        pipe(st, 16, [(0, bc_mm), (1, bc_ev)])

        units = [(dg, sp) for dg in range(32) for sp in range(8)]

        def u_pe(i):
            dg, sp = units[i]
            if sp == 0:
                hb = 64 * (dg % 2)
                st.c("tensor", lambda e, dg=dg, hb=hb: e.matmul(
                    pp[2 + (dg % 2)][:], repm[hb:hb + 64, :],
                    xcb[hb:hb + 64, dg // 2, :], start=True, stop=True))

        def u_xev(i):
            dg, sp = units[i]
            if sp == 0:
                st.c("scalar", lambda e, dg=dg: e.copy(
                    xcrb[:, dg % 2, :], pp[2 + (dg % 2)][:]))

        def u_dA(i):
            dg, sp = units[i]
            st.c("scalar", lambda e, i=i: e.activation(
                dA[:, i % 8, :], dtb[:], AF.Exp, scale=ar[:, i:i + 1]))

        def u_b(i):
            dg, sp = units[i]
            st.c("vector", lambda e, i=i, dg=dg, sp=sp: e.tensor_tensor(
                bt[:, i % 4, :], xcrb[:, dg % 2, :], Bb[:, sp, :], ALU.mult))

        def u_scan(i):
            dg, sp = units[i]
            st.c("vector", lambda e, i=i: e.tensor_tensor_scan(
                ht[:, i % 4, :], dA[:, i % 8, :], bt[:, i % 4, :],
                stcol[:, i:i + 1], ALU.mult, ALU.add))

        def u_st(i):
            st.c("scalar", lambda e, i=i: e.copy(
                stcol[:, i:i + 1], ht[:, i % 4, 511:512]))

        def u_hc(i):
            dg, sp = units[i]
            st.c("gpsimd", lambda e, i=i, sp=sp: e.tensor_tensor(
                hct[:, i % 4, :], ht[:, i % 4, :], Cb[:, sp, :], ALU.mult))

        def u_ymm(i):
            dg, sp = units[i]
            hb = 64 * (dg % 2)
            st.c("tensor", lambda e, i=i, dg=dg, sp=sp, hb=hb: e.matmul(
                ps_y[hb:hb + 64, dg % 2, :], sumsel[:, :], hct[:, i % 4, :],
                start=(sp == 0), stop=(sp == 7)))

        def u_yev(i):
            dg, sp = units[i]
            if sp != 7:
                return
            hb = 64 * (dg % 2)
            st.c("vector", lambda e, dg=dg, hb=hb: e.scalar_tensor_tensor(
                yb[hb:hb + 64, dg // 2, :], xcb[hb:hb + 64, dg // 2, :],
                dcol[hb:hb + 64, dg // 2:dg // 2 + 1],
                ps_y[hb:hb + 64, dg % 2, :], ALU.mult, ALU.add))

        ph = [(0, u_pe), (0, u_dA), (1, u_xev), (2, u_b), (3, u_scan),
              (4, u_st), (4, u_hc), (5, u_ymm), (6, u_yev)]
        moff = 6
        nu = len(units)
        for s in range(nu // 2 + moff):
            for off, fn in ph:
                for half in range(2):
                    i = (s - off) * 2 + half
                    if 0 <= i < nu and (s - off) >= 0 and (s - off) < nu // 2:
                        fn(i)
            st.bar()

        def yz_f(j):
            st.c("vector", lambda e, j=j: e.tensor_tensor(
                yzb[:, j, :], yb[:, j, :], szb[:, j, :], ALU.mult))

        pipe(st, 16, [(0, yz_f)])

        for m in range(8):
            st.d("sync", lambda e, m=m: e.dma_start(
                out=ld[:, m, :], in_=x2f_d[128 * m:128 * (m + 1), c0:c0 + CH]))
        st.d("scalar", lambda e: e.dma_start(out=wbuf2[:, 0, :, :],
                                             in_=woutc[:, :, 0:128]))
        st.bar()

        def wo_ld(m):
            if m + 1 < 8:
                st.d("scalar", lambda e, m=m: e.dma_start(
                    out=wbuf2[:, (m + 1) % 2, :, :],
                    in_=woutc[:, :, 128 * (m + 1):128 * (m + 2)]))

        def wo_mm(m):
            def f(e):
                last = None
                for k in range(16):
                    last = e.matmul(pp[m % 2][:], wbuf2[:, m % 2, k, :],
                                    yzb[:, k, :], start=(k == 0), stop=(k == 15))
                return last
            st.c("tensor", f)

        def wo_ev(m):
            st.c("vector", lambda e, m=m: e.tensor_tensor(
                r_f[:, m, :], pp[m % 2][:], ld[:, m, :], ALU.add))

        pipe(st, 8, [(0, wo_ld), (0, wo_mm), (1, wo_ev)])

        emit_ln(nc, st, pp, (ps_stat[:, 0, :], ps_stat[:, 1, :]), r_f, r2, rowm, rowv, rowmm, rowrs,
                rowmrs, onesc, onesr, ld, xcb[:, 0:8, :], cvec)
        for m in range(8):
            st.d("sync", lambda e, m=m: e.dma_start(
                out=x3f_d[128 * m:128 * (m + 1), c0:c0 + CH], in_=ld[:, m, :]))
            st.d("scalar", lambda e, m=m: e.dma_start(
                out=x3b_d[128 * m:128 * (m + 1), c0:c0 + CH], in_=xcb[:, m, :]))
        st.bar()

    for c in range(NCH):
        do_chunk(c)


def build_l1a():
    nc = bass.Bass("TRN2", target_bir_lowering=False)
    with ExitStack() as stk:
        S = stk.enter_context(nc.semaphore("S"))
        S2 = stk.enter_context(nc.semaphore("S2"))
        S3 = stk.enter_context(nc.semaphore("S3"))
        block = stk.enter_context(nc.Block())
        st = Sched()
        emit_l1a(nc, st, stk)
        st.emit(block, S, S2, S3)
    return nc


def l1a_host_inputs(g):
    out = common_consts()
    winT = g["L1_attn_W_in"].T
    out["winr"] = np.ascontiguousarray(
        winT.reshape(8, 128, 4096).transpose(1, 0, 2))
    woutT = g["L1_attn_W_out"].T
    out["woutr"] = np.ascontiguousarray(
        woutT.reshape(16, 128, 1024).transpose(1, 0, 2))
    wx = g["L1_attn_W_x"]
    wxp = np.zeros((2048, 80), np.float32)
    wxp[:, 0:16] = wx[0:16].T
    wxp[:, 32:48] = wx[16:32].T
    wxp[:, 64] = wx[32]
    out["wxp"] = np.ascontiguousarray(
        wxp.reshape(16, 128, 80).transpose(1, 0, 2))
    cw = g["L1_attn_conv_w"][:, 0, :]
    out["cwc"] = np.ascontiguousarray(cw.reshape(16, 128, 4).transpose(1, 0, 2))
    out["cbc"] = np.ascontiguousarray(
        g["L1_attn_conv_b"].reshape(16, 128).T)
    A = g["L1_attn_A_log"]
    ar = np.zeros((128, 256), np.float32)
    for dg in range(32):
        for sp in range(8):
            u = dg * 8 + sp
            for sl in range(2):
                ar[64 * sl:64 * (sl + 1), u] = A[64 * dg:64 * (dg + 1), 2 * sp + sl]
    out["arlog"] = ar
    out["dcol"] = np.ascontiguousarray(
        g["L1_attn_D"].reshape(16, 128).T)
    repm = np.zeros((128, 128), np.float32)
    for a in range(2):
        for k in range(64):
            repm[64 * a + k, :][np.arange(128) % 64 == k] = 1.0
    out["repm"] = repm
    sumsel = np.zeros((128, 64), np.float32)
    for p in range(128):
        sumsel[p, p % 64] = 1.0
    out["sumsel"] = sumsel
    selp = np.zeros((48, 8, 128), np.float32)
    for sp in range(8):
        for p in range(128):
            selp[2 * sp + p // 64, sp, p] = 1.0
            selp[32 + 2 * sp + p // 64, sp, p] = 1.0
    out["selp"] = selp
    return out


def emit_l1f_a(nc, st, stk, pre=None):
    pre = pre or {}
    def dt(nm, shape, dtype, kind=None):
        if nm in pre:
            return pre[nm]
        if kind is None:
            return nc.dram_tensor("p_" + nm, shape, dtype).ap()
        return nc.dram_tensor("p_" + nm, shape, dtype, kind=kind).ap()
    sb = lambda n_, s_, d_=F32: stk.enter_context(nc.sbuf_tensor("p_" + n_, s_, d_))
    ps = lambda n_, s_, d_=F32: stk.enter_context(nc.psum_tensor("p_" + n_, s_, d_))

    x3b_d = dt("x3b", [D, T], BF16, kind="ExternalInput")
    gwr2_d = dt("gwr2", [128, 8, 8], F32, kind="ExternalInput")
    gbb2_d = dt("gbb2", [128, 8], F32, kind="ExternalInput")
    w1r_d = dt("w1r", [8, 128, 8, 1024], F32, kind="ExternalInput")
    w2r_d = dt("w2r", [8, 128, 8, 1024], F32, kind="ExternalInput")
    w3r_d = dt("w3r", [8, 128, 8, 1024], F32, kind="ExternalInput")
    b1c_d = dt("b1c", [128, 8, 8], F32, kind="ExternalInput")
    b2c_d = dt("b2c", [128, 8, 8], F32, kind="ExternalInput")
    b3c_d = dt("b3c", [128, 8, 8], F32, kind="ExternalInput")
    esel_d = dt("esel", [8, 8, 128], F32, kind="ExternalInput")
    ident_d = dt("ident", [128, 128], F32, kind="ExternalInput")
    onesr_d = dt("onesr", [1, 128], F32, kind="ExternalInput")
    vpart_d = dt("vpart", [128, 8, 8], F32, kind="ExternalOutput")
    indT_d = dt("indT", [8, T], BF16, kind="ExternalOutput")

    w1c = dt("w1c", [8, 128, 8, 1024], BF16)
    w2c = dt("w2c", [8, 128, 8, 1024], BF16)
    w3c = dt("w3c", [8, 128, 8, 1024], BF16)
    rowsc_d = dt("rowsc", [16], F32)

    lgT = sb("lgT", [8, T])
    lg_tm = sb("lg_tm", [128, 64, 8])
    indT_s = sb("indT_s", [8, T], BF16)
    wtT = sb("wtT", [8, T])
    w1s = sb("w1s", [128, 8, 1024], BF16)
    w2s = sb("w2s", [128, 8, 1024], BF16)
    w3s = sb("w3s", [128, 8, 1024], BF16)
    x3bc = sb("x3bc", [128, 2, 8, 512], BF16)
    h1b = sb("h1b", [128, 8, 512], BF16)
    h2b = sb("h2b", [128, 8, 512], BF16)
    ytb = sb("ytb", [128, 4, 512], BF16)
    ysc = sb("ysc", [128, 2, 512], BF16)
    vsc = sb("vsc", [128, 8, 8, 16]); vred = sb("vred", [128, 8, 8])
    tks = sb("tks", [128, 8, 64])
    tkc = sb("tkc", [128, 48])
    mzc = sb("mzc", [8, 2])
    mzr = sb("mzr", [1, 16])
    mzb = sb("mzb", [128, 16])
    gwr2 = sb("gwr2_s", [128, 8, 8], BF16)
    gbb2 = sb("gbb2_s", [128, 8])
    b1c = sb("b1c_s", [128, 8, 8]); b2c = sb("b2c_s", [128, 8, 8])
    b3c = sb("b3c_s", [128, 8, 8])
    esel = sb("esel_s", [8, 8, 128])
    ident = sb("ident_s", [128, 128])
    onesr = sb("onesr_s", [1, 128])

    pp = [ps(f"pp{i}", [128, 512]) for i in range(6)]
    ps_w = ps("ps_w", [128, 512])

    for s_, d_ in [(gbb2, gbb2_d), (b1c, b1c_d), (b2c, b2c_d), (b3c, b3c_d),
                   (esel, esel_d), (ident, ident_d), (onesr, onesr_d)]:
        st.d("sync", lambda e, s=s_, d=d_: e.dma_start(out=s[:], in_=d))
    st.d("gpsimd", lambda e: e.dma_start(out=gwr2[:], in_=gwr2_d))
    st.bar()

    pieces = []
    for e in range(8):
        pieces.append((w1r_d[e], w1c[e]))
        pieces.append((w2r_d[e], w2c[e]))
        pieces.append((w3r_d[e], w3c[e]))
    stg = [w1s[:], w2s[:]]

    def pc_ld(i):
        for a in range(8):
            st.d("gpsimd", lambda e, a=a, i=i: e.dma_start(
                out=stg[i % 2][:, a, :], in_=pieces[i][0][:, a, :],
                max_dma_last_dim=4096))

    def pc_stn(i):
        for a in range(8):
            st.d("sync", lambda e, a=a, i=i: e.dma_start(
                out=pieces[i][1][:, a, :], in_=stg[i % 2][:, a, :]))

    pipe(st, len(pieces), [(0, pc_ld), (1, pc_stn)])

    def lgdma(i):
        if i % 4 == 0:
            c = i // 4
            for k in range(8):
                st.d("sync", lambda e, k=k, c=c: e.dma_start(
                    out=x3bc[:, c % 2, k, :],
                    in_=x3b_d[128 * k:128 * (k + 1), 512 * c:512 * (c + 1)]),
                     slack=0)

    def lg_mm(i):
        def f(e):
            last = None
            for k in range(8):
                last = e.matmul(pp[i % 3][:, 0:8],
                                x3bc[:, (i // 4) % 2, k, 128 * (i % 4):128 * (i % 4 + 1)],
                                gwr2[:, k, :], start=(k == 0), stop=(k == 7))
            return last
        st.c("tensor", f)

    def lg_add(i):
        st.c("vector", lambda e, i=i: e.tensor_tensor(
            lg_tm[:, i, :], pp[i % 3][:, 0:8], gbb2[:], ALU.add))

    def lg_tp(i):
        st.c("tensor", lambda e, i=i: e.transpose(
            pp[3 + (i // 4) % 2][0:8, 128 * (i % 4):128 * (i % 4 + 1)],
            lg_tm[:, i, :], ident[:]))

    def lg_ev(i):
        if i % 4 == 3:
            q = i // 4
            st.c("vector", lambda e, q=q: e.tensor_copy(
                lgT[:, 512 * q:512 * (q + 1)], pp[3 + q % 2][0:8, :]))

    pipe(st, 64, [(0, lgdma), (2, lg_mm), (3, lg_add), (4, lg_tp), (5, lg_ev)])

    st.c("vector", lambda e: e.tensor_reduce(mzc[:, 0:1], lgT[:], AX.X, ALU.max,
                                             negate=True))
    st.bar()
    st.c("scalar", lambda e: e.activation(lgT[:], lgT[:], AF.Exp,
                                          bias=mzc[:, 0:1],
                                          accum_out=mzc[:, 1:2]))
    st.bar()
    st.d("scalar", lambda e: e.dma_start(out=rowsc_d[0:8], in_=mzc[:, 0]))
    st.d("scalar", lambda e: e.dma_start(out=rowsc_d[8:16], in_=mzc[:, 1]))
    st.bar()
    st.d("scalar", lambda e: e.dma_start(out=mzr[:], in_=rowsc_d))
    st.bar()
    st.c("vector", lambda e: e.reciprocal(mzr[:, 8:16], mzr[:, 8:16]))
    st.bar()
    st.c("tensor", lambda e: e.matmul(pp[0][:, 0:16], onesr[:], mzr[:],
                                      start=True, stop=True))
    st.bar()
    st.c("vector", lambda e: e.tensor_copy(mzb[:], pp[0][:, 0:16]))
    st.bar()

    def _tk(i, k):
        return tks[:, i % 8, 8 * k:8 * k + 8]

    def _tkc(i, k):
        return tkc[:, 6 * (i % 8) + k:6 * (i % 8) + k + 1]

    def tk_sub(i):
        st.c("vector", lambda e, i=i: e.tensor_tensor(
            _tk(i, 0), lg_tm[:, i, :], mzb[:, 0:8], ALU.add))

    def tk_exp(i):
        st.c("scalar", lambda e, i=i: e.activation(_tk(i, 1), _tk(i, 0), AF.Exp))

    def tk_pr(i):
        st.c("vector", lambda e, i=i: e.tensor_tensor(
            _tk(i, 2), _tk(i, 1), mzb[:, 8:16], ALU.mult))

    def tk_mx1(i):
        st.c("vector", lambda e, i=i: e.tensor_reduce(
            _tkc(i, 0), _tk(i, 2), AX.X, ALU.max))

    def tk_m1(i):
        st.c("vector", lambda e, i=i: e.tensor_scalar(
            _tk(i, 3), _tk(i, 2), _tkc(i, 0), None, ALU.is_equal))

    def tk_b2(i):
        st.c("vector", lambda e, i=i: e.scalar_tensor_tensor(
            _tk(i, 4), _tk(i, 3), -1e30, _tk(i, 2), ALU.mult, ALU.add))

    def tk_mx2(i):
        st.c("vector", lambda e, i=i: e.tensor_reduce(
            _tkc(i, 1), _tk(i, 4), AX.X, ALU.max))

    def tk_m2(i):
        st.c("vector", lambda e, i=i: e.tensor_scalar(
            _tk(i, 5), _tk(i, 4), _tkc(i, 1), None, ALU.is_equal))

    def tk_m12(i):
        st.c("vector", lambda e, i=i: e.tensor_tensor(
            _tk(i, 5), _tk(i, 3), _tk(i, 5), ALU.add))

    def tk_wt(i):
        st.c("vector", lambda e, i=i: e.tensor_tensor(
            _tk(i, 6), _tk(i, 2), _tk(i, 5), ALU.mult))

    def tk_tp(i):
        def f(e):
            e.transpose(pp[3 + (i // 4) % 2][0:8, 128 * (i % 4):128 * (i % 4 + 1)],
                        _tk(i, 5), ident[:])
            return e.transpose(
                pp[(i // 4) % 2][0:8, 128 * (i % 4):128 * (i % 4 + 1)],
                _tk(i, 6), ident[:])
        st.c("tensor", f)

    def tk_ev(i):
        if i % 4 == 3:
            q = i // 4
            st.c("vector", lambda e, q=q: e.tensor_copy(
                indT_s[:, 512 * q:512 * (q + 1)], pp[3 + q % 2][0:8, :]))
            st.c("scalar", lambda e, q=q: e.copy(
                wtT[:, 512 * q:512 * (q + 1)], pp[q % 2][0:8, :]))

    pipe(st, 64, [(0, tk_sub), (1, tk_exp), (2, tk_pr), (3, tk_mx1),
                  (4, tk_m1), (5, tk_b2), (6, tk_mx2), (7, tk_m2),
                  (8, tk_m12), (9, tk_wt), (10, tk_tp), (11, tk_ev)])
    for q in range(16):
        st.d("scalar", lambda e, q=q: e.dma_start(
            out=indT_d[:, 512 * q:512 * (q + 1)],
            in_=indT_s[:, 512 * q:512 * (q + 1)]))
    st.c("vector", lambda e: e.memset(vsc[:], 0.0))
    st.bar()

    def do_expert(e_i):
        st.d("scalar", lambda e: e.dma_start(out=w1s[:], in_=w1c[e_i]))
        st.d("scalar", lambda e: e.dma_start(out=w2s[:], in_=w2c[e_i]))
        st.d("scalar", lambda e: e.dma_start(out=w3s[:], in_=w3c[e_i]))
        st.bar()

        def do_chunk(c):
            c0 = c * CH
            for k in range(8):
                st.d("sync", lambda e, k=k: e.dma_start(
                    out=x3bc[:, 0, k, :], in_=x3b_d[128 * k:128 * (k + 1), c0:c0 + CH]))
            st.c("tensor", lambda e: e.matmul(
                ps_w[:], esel[:, e_i, :], wtT[:, c0:c0 + CH], start=True, stop=True))
            st.bar()

            def h_mm(e, src, wsrc, m, i):
                last = None
                for k in range(8):
                    last = e.matmul(pp[i % 6][:], wsrc[:, k, 128 * m:128 * (m + 1)],
                                    src[:, k, :], start=(k == 0), stop=(k == 7))
                return last

            def h1_mm(s):
                def f(e):
                    h_mm(e, x3bc[:, 0], w1s, 2 * s, 2 * s)
                    return h_mm(e, x3bc[:, 0], w1s, 2 * s + 1, 2 * s + 1)
                st.c("tensor", f)

            def h1_ev(s):
                for m in (2 * s, 2 * s + 1):
                    st.c("scalar", lambda e, m=m: e.activation(
                        h1b[:, m, :], pp[m % 6][:], AF.Relu, bias=b1c[:, e_i, m:m + 1]))

            pipe(st, 4, [(0, h1_mm), (1, h1_ev)])

            def h2_mm(s):
                def f(e):
                    h_mm(e, h1b, w2s, 2 * s, 2 * s)
                    return h_mm(e, h1b, w2s, 2 * s + 1, 2 * s + 1)
                st.c("tensor", f)

            def h2_ev(s):
                for m in (2 * s, 2 * s + 1):
                    st.c("scalar", lambda e, m=m: e.activation(
                        h2b[:, m, :], pp[m % 6][:], AF.Relu, bias=b2c[:, e_i, m:m + 1]))

            pipe(st, 4, [(0, h2_mm), (1, h2_ev)])

            def y_mm(s):
                def f(e):
                    h_mm(e, h2b, w3s, 2 * s, 2 * s)
                    return h_mm(e, h2b, w3s, 2 * s + 1, 2 * s + 1)
                st.c("tensor", f)

            def y_ev(s):
                for m in (2 * s, 2 * s + 1):
                    st.c("scalar", lambda e, m=m: e.activation(
                        ytb[:, m % 4, :], pp[m % 6][:], AF.Relu,
                        bias=b3c[:, e_i, m:m + 1]))

            def y_acc(s):
                for m in (2 * s, 2 * s + 1):
                    st.c("vector", lambda e, m=m: e.scalar_tensor_tensor(
                        ysc[:, m % 2, :], ytb[:, m % 4, :], 1.0, ps_w[:],
                        ALU.mult, ALU.mult,
                        accum_out=vsc[:, e_i, m, c:c + 1]))

            pipe(st, 4, [(0, y_mm), (1, y_ev), (2, y_acc)])

        for c in range(NCH):
            do_chunk(c)

    for e_i in range(8):
        do_expert(e_i)

    st.c("vector", lambda e: e.tensor_reduce(vred[:], vsc[:], AX.X, ALU.add))
    st.bar()
    st.d("sync", lambda e: e.dma_start(out=vpart_d, in_=vred[:]))
    st.bar()


def emit_l1f_b(nc, st, stk, pre=None, nvp=8):
    pre = pre or {}
    def dt(nm, shape, dtype, kind=None):
        if nm in pre:
            return pre[nm]
        if kind is None:
            return nc.dram_tensor("b_" + nm, shape, dtype).ap()
        return nc.dram_tensor("b_" + nm, shape, dtype, kind=kind).ap()
    sb = lambda n_, s_, d_=F32: stk.enter_context(nc.sbuf_tensor("b_" + n_, s_, d_))
    ps = lambda n_, s_, d_=F32: stk.enter_context(nc.psum_tensor("b_" + n_, s_, d_))

    x3f_d = dt("x3f", [D, T], F32, kind="ExternalInput")
    indT_d = dt("indT", [8, T], BF16, kind="ExternalInput")
    vp_d = [dt(f"vp{i}", [128, 8, 8], F32, kind="ExternalInput")
            for i in range(nvp)]
    ident_d = dt("ident", [128, 128], F32, kind="ExternalInput")
    onesr_d = dt("onesr", [1, 128], F32, kind="ExternalInput")
    onesc_d = dt("onesc", [128, 1], F32, kind="ExternalInput")
    cvec_d = dt("cvec", [128, 8], F32, kind="ExternalInput")
    x4_d = dt("x4", [T, D], F32, kind="ExternalOutput")

    vps = sb("vps", [128, nvp, 8, 8])
    vsum = sb("vsum", [128, 8, 8])
    vT = sb("vT", [8, 8, 128], BF16)
    indT_s = sb("indT_s", [8, T], BF16)
    ld = sb("ld", [128, 8, 512])
    r_f = sb("r_f", [128, 8, 512])
    o_tm = sb("o_tm", [128, 4, 1024])
    ident = sb("ident_s", [128, 128])
    onesr = sb("onesr_s", [1, 128]); onesc = sb("onesc_s", [128, 1])
    cvec = sb("cvec_s", [128, 8])
    rowm = sb("rowm", [1, 512]); rowv = sb("rowv", [1, 512])
    rowmm = sb("rowmm", [1, 512]); rowrs = sb("rowrs", [1, 512])
    rowmrs = sb("rowmrs", [1, 512])
    r2 = sb("r2", [128, 2, 512])

    pp = [ps(f"pp{i}", [128, 512]) for i in range(6)]
    ps_stat = ps("ps_stat", [1, 2, 512])

    for s_, d_ in [(ident, ident_d), (onesr, onesr_d), (onesc, onesc_d),
                   (cvec, cvec_d)]:
        st.d("sync", lambda e, s=s_, d=d_: e.dma_start(out=s[:], in_=d))
    for i in range(nvp):
        st.d("sync", lambda e, i=i: e.dma_start(out=vps[:, i, :, :], in_=vp_d[i]))
    st.d("scalar", lambda e: e.dma_start(out=indT_s[:], in_=indT_d))
    st.bar()
    stp = nvp // 2
    while stp >= 1:
        for i in range(stp):
            st.c("vector", lambda e, i=i, stp=stp: e.tensor_tensor(
                vps[:, i, :, :], vps[:, i, :, :], vps[:, i + stp, :, :], ALU.add))
        st.bar()
        stp //= 2
    st.c("vector", lambda e: e.tensor_copy(vsum[:], vps[:, 0, :, :]))
    st.bar()
    for m in range(8):
        st.c("tensor", lambda e, m=m: e.transpose(
            pp[m % 2][0:8, 0:128], vsum[:, :, m], ident[:]))
        st.bar()
        st.c("scalar", lambda e, m=m: e.copy(vT[:, m, :], pp[m % 2][0:8, 0:128]))
        st.bar()

    def do_chunk(c):
        c0 = c * CH
        for m in range(8):
            st.d("sync", lambda e, m=m: e.dma_start(
                out=ld[:, m, :], in_=x3f_d[128 * m:128 * (m + 1), c0:c0 + CH]))
        st.bar()

        def o_mm(m):
            st.c("tensor", lambda e, m=m: e.matmul(
                pp[m % 2][:], vT[:, m, :], indT_s[:, c0:c0 + CH],
                start=True, stop=True))

        def o_ev(m):
            st.c("vector", lambda e, m=m: e.tensor_tensor(
                r_f[:, m, :], pp[m % 2][:], ld[:, m, :], ALU.add))

        pipe(st, 8, [(0, o_mm), (1, o_ev)])
        emit_ln(nc, st, pp, (ps_stat[:, 0, :], ps_stat[:, 1, :]), r_f, r2, rowm, rowv, rowmm, rowrs,
                rowmrs, onesc, onesr, ld, None, cvec, outb_off=True)

        def t_mm(i):
            tt_, m = i // 8, i % 8
            st.c("tensor", lambda e, tt_=tt_, m=m, i=i: e.transpose(
                pp[2 + i % 2][:, 0:128], ld[:, m, 128 * tt_:128 * (tt_ + 1)],
                ident[:]))

        def t_ev(i):
            tt_, m = i // 8, i % 8
            st.c("vector", lambda e, tt_=tt_, m=m, i=i: e.tensor_copy(
                o_tm[:, tt_, 128 * m:128 * (m + 1)], pp[2 + i % 2][:, 0:128]))

        def t_st(i):
            tt_, m = i // 8, i % 8
            if m == 7:
                st.d("sync", lambda e, tt_=tt_: e.dma_start(
                    out=x4_d[c0 + 128 * tt_:c0 + 128 * (tt_ + 1), :],
                    in_=o_tm[:, tt_, :]))

        pipe(st, 32, [(0, t_mm), (1, t_ev), (2, t_st)])

    for c in range(NCH):
        do_chunk(c)


def build_l1f_a():
    nc = bass.Bass("TRN2", target_bir_lowering=False)
    with ExitStack() as stk:
        S = stk.enter_context(nc.semaphore("S"))
        S2 = stk.enter_context(nc.semaphore("S2"))
        S3 = stk.enter_context(nc.semaphore("S3"))
        block = stk.enter_context(nc.Block())
        st = Sched()
        emit_l1f_a(nc, st, stk)
        st.emit(block, S, S2, S3)
    return nc


def build_l1f_b():
    nc = bass.Bass("TRN2", target_bir_lowering=False)
    with ExitStack() as stk:
        S = stk.enter_context(nc.semaphore("S"))
        S2 = stk.enter_context(nc.semaphore("S2"))
        S3 = stk.enter_context(nc.semaphore("S3"))
        block = stk.enter_context(nc.Block())
        st = Sched()
        emit_l1f_b(nc, st, stk)
        st.emit(block, S, S2, S3)
    return nc


def l1f_host_inputs(g):
    out = common_consts()
    out["gwr2"] = np.ascontiguousarray(
        g["L1_ffn_gate_w"].reshape(8, 128, 8).transpose(1, 0, 2))
    out["gbb2"] = np.ascontiguousarray(
        np.broadcast_to(g["L1_ffn_gate_b"], (128, 8)))
    re3 = lambda w: np.ascontiguousarray(
        w.reshape(8, 8, 128, 1024).transpose(0, 2, 1, 3))
    out["w1r"] = re3(g["L1_ffn_w1"]); out["w2r"] = re3(g["L1_ffn_w2"])
    out["w3r"] = re3(g["L1_ffn_w3"])
    colc = lambda b: np.ascontiguousarray(
        b.reshape(8, 8, 128).transpose(2, 0, 1))
    out["b1c"] = colc(g["L1_ffn_b1"]); out["b2c"] = colc(g["L1_ffn_b2"])
    out["b3c"] = colc(g["L1_ffn_b3"])
    es = np.zeros((8, 8, 128), np.float32)
    for e in range(8):
        es[e, e, :] = 1.0
    out["esel"] = es
    return out


def build_l0a():
    nc = bass.Bass("TRN2", target_bir_lowering=False)
    with ExitStack() as stk:
        S = stk.enter_context(nc.semaphore("S"))
        S2 = stk.enter_context(nc.semaphore("S2"))
        S3 = stk.enter_context(nc.semaphore("S3"))
        block = stk.enter_context(nc.Block())
        st = Sched()
        emit_l0a(nc, st, stk)
        st.emit(block, S, S2, S3)
    return nc


def l0a_host_inputs(common):
    g = common
    out = {}
    re_w = lambda w: np.ascontiguousarray(
        w.reshape(8, 128, 1024).transpose(1, 0, 2))
    out["wq"] = re_w(g["L0_attn_Wq"]); out["wk"] = re_w(g["L0_attn_Wk"])
    out["wv"] = re_w(g["L0_attn_Wv"]); out["wo"] = re_w(g["L0_attn_Wo"])
    col = lambda b: np.ascontiguousarray(b.reshape(8, 128).T)
    out["bqc"] = col(g["L0_attn_bq"]); out["bkc"] = col(g["L0_attn_bk"])
    out["boc"] = col(g["L0_attn_bo"])
    out["bkb"] = np.ascontiguousarray(np.broadcast_to(g["L0_attn_bk"], (128, 1024)))
    out["bvb"] = np.ascontiguousarray(np.broadcast_to(g["L0_attn_bv"], (128, 1024)))
    ang = (np.pi / (2.0 * T)) * np.arange(T, dtype=np.float32)
    cosw = np.cos(ang).astype(np.float32); sinw = np.sin(ang).astype(np.float32)
    out["cosb"] = np.ascontiguousarray(np.broadcast_to(cosw, (128, T)))
    out["sinb"] = np.ascontiguousarray(np.broadcast_to(sinw, (128, T)))
    out["ccol"] = np.ascontiguousarray(cosw.reshape(64, 128).T)
    out["scol"] = np.ascontiguousarray(sinw.reshape(64, 128).T)
    u = np.arange(128)[:, None]
    tloc = np.arange(512)[None, :]
    msk = np.stack([(128 * ub + u <= tloc).astype(np.float32) for ub in range(4)], 1)
    out["mask"] = np.ascontiguousarray(msk.astype(BF))
    out["ident"] = np.eye(128, dtype=np.float32)
    out["ones64"] = np.ones((1, 64), np.float32)
    out["onesr"] = np.ones((1, 128), np.float32)
    out["onesc"] = np.ones((128, 1), np.float32)
    cv = np.zeros((128, 8), np.float32)
    cv[:, 0] = LN_EPS; cv[:, 1] = -0.5; cv[:, 2] = -1.0; cv[:, 3] = 1.0
    cv[:, 4] = 1.702
    out["cvec"] = cv
    return out


def common_consts():
    out = {}
    out["ident"] = np.eye(128, dtype=np.float32)
    out["onesr"] = np.ones((1, 128), np.float32)
    out["onesc"] = np.ones((128, 1), np.float32)
    cv = np.zeros((128, 8), np.float32)
    cv[:, 0] = LN_EPS; cv[:, 1] = -0.5; cv[:, 2] = -1.0; cv[:, 3] = 1.0
    cv[:, 4] = 1.702
    out["cvec"] = cv
    return out


_PERM = np.concatenate([np.arange(0, 4096, 2), np.arange(1, 4096, 2)])


def l0f_host_inputs(g):
    out = common_consts()
    w1p = g["L0_ffn_w1"][:, :, _PERM]
    w3p = g["L0_ffn_w3"][:, :, _PERM]
    out["w1r"] = np.ascontiguousarray(
        w1p.reshape(8, 8, 128, 4096).transpose(0, 2, 1, 3))
    out["w3r"] = np.ascontiguousarray(
        w3p.reshape(8, 8, 128, 4096).transpose(0, 2, 1, 3))
    out["w2r"] = np.ascontiguousarray(
        g["L0_ffn_w2"].reshape(8, 16, 128, 1024).transpose(0, 2, 1, 3))
    sw1p = g["L0_ffn_sw1"][:, _PERM]; sw3p = g["L0_ffn_sw3"][:, _PERM]
    out["sw1r"] = np.ascontiguousarray(sw1p.reshape(8, 128, 4096).transpose(1, 0, 2))
    out["sw3r"] = np.ascontiguousarray(sw3p.reshape(8, 128, 4096).transpose(1, 0, 2))
    out["sw2r"] = np.ascontiguousarray(
        g["L0_ffn_sw2"].reshape(16, 128, 1024).transpose(1, 0, 2))
    b1p = g["L0_ffn_b1"][:, _PERM]; b3p = g["L0_ffn_b3"][:, _PERM]
    b1c = b1p.reshape(8, 32, 128).transpose(2, 0, 1)
    b3c = b3p.reshape(8, 32, 128).transpose(2, 0, 1)
    sb1 = g["L0_ffn_sb1"][_PERM].reshape(32, 128).T[:, None, :]
    sb3 = g["L0_ffn_sb3"][_PERM].reshape(32, 128).T[:, None, :]
    out["b1c"] = np.ascontiguousarray(np.concatenate([b1c, sb1], 1))
    out["b3c"] = np.ascontiguousarray(np.concatenate([b3c, sb3], 1))
    b2c = g["L0_ffn_b2"].reshape(8, 8, 128).transpose(2, 0, 1)
    sb2 = g["L0_ffn_sb2"].reshape(8, 128).T[:, None, :]
    out["b2c"] = np.ascontiguousarray(np.concatenate([b2c, sb2], 1))
    out["gwr"] = np.ascontiguousarray(
        g["L0_ffn_gate_w"].T.reshape(8, 128, 8).transpose(1, 0, 2))
    out["gbb"] = np.ascontiguousarray(
        np.broadcast_to(g["L0_ffn_gate_b"], (128, 8)))
    es = np.zeros((8, 8, 128), np.float32)
    for e in range(8):
        es[e, e, :] = 1.0
    out["esel"] = es
    return out


def build_fused():
    nc = bass.Bass("TRN2", target_bir_lowering=False)
    x1f = nc.dram_tensor("x1f", [D, T], F32).ap()
    x1b = nc.dram_tensor("x1b", [D, T], BF16).ap()
    x2f = nc.dram_tensor("x2f", [D, T], F32).ap()
    x2b = nc.dram_tensor("x2b", [D, T], BF16).ap()
    x3f = nc.dram_tensor("x3f", [D, T], F32).ap()
    x3b = nc.dram_tensor("x3b", [D, T], BF16).ap()
    vpart = nc.dram_tensor("vpart", [128, 8, 8], F32).ap()
    vsum = nc.dram_tensor("vsum", [128, 64], F32, addr_space="Shared").ap()
    indT = nc.dram_tensor("indT", [8, T], BF16).ap()
    with nc.semaphore("S") as S, nc.semaphore("S2") as S2, \
            nc.semaphore("S3") as S3, nc.Block() as block:
        st = Sched()
        with ExitStack() as s1:
            emit_l0a(nc, st, s1, pre={"x1f": x1f, "x1b": x1b})
            st.bar()
        with ExitStack() as s2:
            emit_l0f(nc, st, s2, pre={"x1f": x1f, "x1b": x1b,
                                      "x2f": x2f, "x2b": x2b})
            st.bar()
        with ExitStack() as s3:
            emit_l1a(nc, st, s3, pre={"x2f": x2f, "x2b": x2b,
                                      "x3f": x3f, "x3b": x3b})
            st.bar()
        with ExitStack() as s4:
            emit_l1f_a(nc, st, s4, pre={"x3b": x3b, "vpart": vpart,
                                        "indT": indT})
            st.bar()
        st.c("gpsimd", lambda e: e.collective_compute(
            "AllReduce", ALU.add, replica_groups=[list(range(8))],
            ins=[vpart.rearrange("p a b -> p (a b)")], outs=[vsum]))
        st.bar()
        with ExitStack() as s5:
            emit_l1f_b(nc, st, s5, nvp=1,
                       pre={"x3f": x3f, "indT": indT,
                            "vp0": vsum.rearrange("p (a b) -> p a b", b=8)})
            st.bar()
        st.emit(block, S, S2, S3)
    return nc


def kernel(**inputs):
    n = 8
    host_a = l0a_host_inputs(inputs)
    host_f = l0f_host_inputs(inputs)
    host_m = l1a_host_inputs(inputs)
    host_m.pop("ident", None)
    host_p = l1f_host_inputs(inputs)
    hp_a = {k: host_p[k] for k in ["gwr2", "gbb2", "w1r", "w2r", "w3r",
                                   "b1c", "b2c", "b3c", "esel", "ident", "onesr"]}
    hp_b = {k: host_p[k] for k in ["ident", "onesr", "onesc", "cvec"]}
    base = {}
    for pfx, h in [("a", host_a), ("f", host_f), ("m", host_m),
                   ("p", hp_a), ("b", hp_b)]:
        for k, v in h.items():
            base[f"{pfx}_{k}"] = v
    nc = build_fused()
    in_maps = []
    for b in range(n):
        im = dict(base)
        im["a_x0"] = np.ascontiguousarray(inputs["L0_attn_x"][b])
        in_maps.append(im)
    res = run_bass_kernel_spmd(nc, in_maps, core_ids=list(range(n))).results
    return np.stack([np.asarray(res[b]["b_x4"]) for b in range(n)], axis=0)


def kernel_multi(**inputs):
    n = 8
    cores = list(range(n))
    x0s = [np.ascontiguousarray(inputs["L0_attn_x"][b]) for b in range(n)]

    host_a = l0a_host_inputs(inputs)
    nc1 = build_l0a()
    r1 = run_bass_kernel_spmd(
        nc1, [dict(host_a, x0=x0s[b]) for b in range(n)], core_ids=cores).results

    host_f = l0f_host_inputs(inputs)
    nc2 = build_l0f()
    r2 = run_bass_kernel_spmd(
        nc2, [dict(host_f, x1f=np.asarray(r1[b]["x1f"]),
                   x1b=np.asarray(r1[b]["x1b"])) for b in range(n)],
        core_ids=cores).results
    del r1

    host_m = l1a_host_inputs(inputs)
    nc3 = build_l1a()
    r3 = run_bass_kernel_spmd(
        nc3, [dict(host_m, x2f=np.asarray(r2[b]["x2f"]),
                   x2b=np.asarray(r2[b]["x2b"])) for b in range(n)],
        core_ids=cores).results
    del r2

    host_p = l1f_host_inputs(inputs)
    hp_a = {k: host_p[k] for k in ["gwr2", "gbb2", "w1r", "w2r", "w3r",
                                   "b1c", "b2c", "b3c", "esel", "ident", "onesr"]}
    nc4 = build_l1f_a()
    r4 = run_bass_kernel_spmd(
        nc4, [dict(hp_a, x3b=np.asarray(r3[b]["x3b"])) for b in range(n)],
        core_ids=cores).results

    vparts = [np.asarray(r4[b]["vpart"]) for b in range(n)]
    hp_b = {k: host_p[k] for k in ["ident", "onesr", "onesc", "cvec"]}
    nc5 = build_l1f_b()
    in5 = []
    for b in range(n):
        m = dict(hp_b, x3f=np.asarray(r3[b]["x3f"]),
                 indT=np.asarray(r4[b]["indT"]))
        for i in range(n):
            m[f"vp{i}"] = vparts[i]
        in5.append(m)
    r5 = run_bass_kernel_spmd(nc5, in5, core_ids=cores).results
    return np.stack([np.asarray(r5[b]["x4"]) for b in range(n)], axis=0)


# version 13 (final)
# speedup vs baseline: 1.0151x; 1.0115x over previous
import os
import numpy as np
import ml_dtypes
from contextlib import ExitStack

os.environ.setdefault("GAUGE_NTFF_IGNORE_DMA", "1")
import concourse.bass as bass
import concourse.mybir as mybir
from concourse.bass_utils import run_bass_kernel_spmd

F32 = mybir.dt.float32
BF16 = mybir.dt.bfloat16
AF = mybir.ActivationFunctionType
ALU = mybir.AluOpType
AX = mybir.AxisListType
BF = ml_dtypes.bfloat16

T, D, CH = 8192, 1024, 512
DBG_UPTO = None
DBG_CHUNKS = None
NCH = T // CH
H, DH = 16, 64
LN_EPS = 1e-5


class Sched:

    def __init__(self):
        self.stages = []
        self.cur = {}

    def c(self, eng, fn):
        self.cur.setdefault(eng, []).append((fn, "c", 0))

    def d(self, eng, fn, slack=0):
        self.cur.setdefault(eng, []).append((fn, "d", slack))

    def bar(self):
        if self.cur:
            self.stages.append(self.cur)
            self.cur = {}

    def emit(self, block, sem, sem2, sem3):
        self.bar()
        ns = len(self.stages)
        bucket = [0] * ns
        bucket2 = [0] * ns
        bucket3 = [0] * ns
        for s, stg in enumerate(self.stages):
            for eng, ops in stg.items():
                ncomp = 0
                for _, k, slack in ops:
                    if k == "d":
                        if slack:
                            bucket2[min(s + slack, ns - 1)] += 16
                        elif eng == "gpsimd":
                            bucket3[s] += 16
                        else:
                            bucket[s] += 16
                    else:
                        ncomp = 1
                bucket[s] += ncomp
        eff = [0] * (ns + 1)
        eff2 = [0] * (ns + 1)
        eff3 = [0] * (ns + 1)
        for s in range(ns):
            eff[s + 1] = eff[s] + bucket[s]
            eff2[s + 1] = eff2[s] + bucket2[s]
            eff3[s + 1] = eff3[s] + bucket3[s]
        total, total2, total3 = eff[ns], eff2[ns], eff3[ns]

        def emit_for(name, e):
            last2 = 0
            last3 = 0
            for s, stg in enumerate(self.stages):
                ops = stg.get(name)
                if not ops:
                    continue
                if eff[s]:
                    e.wait_ge(sem, eff[s])
                if eff2[s] > last2:
                    e.wait_ge(sem2, eff2[s])
                    last2 = eff2[s]
                if eff3[s] > last3:
                    e.wait_ge(sem3, eff3[s])
                    last3 = eff3[s]
                lastc = max(
                    (i for i, (_, k, _sl) in enumerate(ops) if k == "c"),
                    default=None,
                )
                for i, (fn, k, sl) in enumerate(ops):
                    inst = fn(e)
                    if k == "d":
                        inst.then_inc(
                            sem2 if sl else (sem3 if name == "gpsimd" else sem), 16)
                    elif i == lastc:
                        inst.then_inc(sem, 1)

        def make(nm):
            def body(e):
                emit_for(nm, e)
                if nm == "sync":
                    e.wait_ge(sem, total)
                    if total2:
                        e.wait_ge(sem2, total2)
                    if total3:
                        e.wait_ge(sem3, total3)
            body.__name__ = "_"
            return body

        for name in ["sync", "tensor", "vector", "scalar", "gpsimd"]:
            getattr(block, name)(make(name))
        return total


def pipe(st, n, phases, stride=1):
    moff = max(off for off, _ in phases)
    for s in range((n - 1) * stride + moff + 1):
        for off, fn in phases:
            q, r = divmod(s - off, stride)
            if r == 0 and 0 <= q < n:
                fn(q)
        st.bar()


def emit_l0a(nc, st, stk, pre=None):
    pre = pre or {}
    def dt(nm, shape, dtype, kind=None):
        if nm in pre:
            return pre[nm]
        if kind is None:
            return nc.dram_tensor("a_" + nm, shape, dtype).ap()
        return nc.dram_tensor("a_" + nm, shape, dtype, kind=kind).ap()
    sb = lambda n_, s_, d_=F32: stk.enter_context(nc.sbuf_tensor("a_" + n_, s_, d_))
    ps = lambda n_, s_, d_=F32: stk.enter_context(nc.psum_tensor("a_" + n_, s_, d_))

    x0_d = dt("x0", [T, D], F32, kind="ExternalInput")
    wq_d = dt("wq", [128, 8, 1024], F32, kind="ExternalInput")
    wk_d = dt("wk", [128, 8, 1024], F32, kind="ExternalInput")
    wv_d = dt("wv", [128, 8, 1024], F32, kind="ExternalInput")
    wo_d = dt("wo", [128, 8, 1024], F32, kind="ExternalInput")
    bqc_d = dt("bqc", [128, 8], F32, kind="ExternalInput")
    bkc_d = dt("bkc", [128, 8], F32, kind="ExternalInput")
    boc_d = dt("boc", [128, 8], F32, kind="ExternalInput")
    bkb_d = dt("bkb", [128, 1024], F32, kind="ExternalInput")
    bvb_d = dt("bvb", [128, 1024], F32, kind="ExternalInput")
    cosb_d = dt("cosb", [128, T], F32, kind="ExternalInput")
    sinb_d = dt("sinb", [128, T], F32, kind="ExternalInput")
    ccol_d = dt("ccol", [128, 64], F32, kind="ExternalInput")
    scol_d = dt("scol", [128, 64], F32, kind="ExternalInput")
    mask_d = dt("mask", [128, 4, 512], BF16, kind="ExternalInput")
    ident_d = dt("ident", [128, 128], F32, kind="ExternalInput")
    ones64_d = dt("ones64", [1, 64], F32, kind="ExternalInput")
    onesr_d = dt("onesr", [1, 128], F32, kind="ExternalInput")
    onesc_d = dt("onesc", [128, 1], F32, kind="ExternalInput")
    cvec_d = dt("cvec", [128, 8], F32, kind="ExternalInput")
    x1f_d = dt("x1f", [D, T], F32, kind="ExternalOutput")
    x1b_d = dt("x1b", [D, T], BF16, kind="ExternalOutput")

    w_q = sb("w_q", [128, 8, 1024], BF16)
    w_k = sb("w_k", [128, 8, 1024], BF16)
    w_v = sb("w_v", [128, 8, 1024], BF16)
    w_o = sb("w_o", [128, 8, 1024], BF16)
    bqc = sb("bqc_s", [128, 8]); bkc = sb("bkc_s", [128, 8]); boc = sb("boc_s", [128, 8])
    bkb = sb("bkb_s", [128, 1024]); bvb = sb("bvb_s", [128, 1024])
    ccol = sb("ccol_s", [128, 64]); scol = sb("scol_s", [128, 64])
    mask = sb("mask_s", [128, 4, 512], BF16)
    ident = sb("ident_s", [128, 128])
    ones64 = sb("ones64_s", [1, 64]); onesr = sb("onesr_s", [1, 128]); onesc = sb("onesc_s", [128, 1])
    cosb = sb("cosb_s", [128, 512]); sinb = sb("sinb_s", [128, 512])
    cvec = sb("cvec_s", [128, 8])
    big0 = sb("big0", [128, 4096])
    x_tm = big0[:].rearrange("p (a b) -> p a b", b=1024)
    r_f = big0[:].rearrange("p (a b) -> p a b", b=512)
    bufA = sb("bufA", [128, 8, 512])
    x_fm = bufA[:]
    x1f_t = bufA[:]
    bufB = sb("bufB", [128, 8, 512], BF16)
    x_fmb = bufB[:]
    y_b = bufB[:]
    x1b_t = bufB[:]
    qc = sb("qc", [128, 8, 512], BF16); qs = sb("qs", [128, 8, 512], BF16)
    kcf = sb("kcf", [128, 8, 512], BF16); ksf = sb("ksf", [128, 8, 512], BF16)
    kct = sb("kct", [128, 4, 1024], BF16); kst = sb("kst", [128, 4, 1024], BF16)
    v_tm = sb("v_tm", [128, 4, 16, 65], BF16)
    r2 = sb("r2", [128, 2, 512])
    tmp2 = r2
    am2 = big0[:, 0:2048].bitcast(BF16).rearrange("p (a u b) -> p a u b", a=2, b=512)
    den_r = sb("den_r", [1, 512]); rec_r = sb("rec_r", [1, 512])
    rowm = sb("rowm", [1, 512]); rowv = sb("rowv", [1, 512])
    rowmm = den_r
    rowrs = sb("rowrs", [1, 512]); rowmrs = sb("rowmrs", [1, 512])
    Sf = sb("Sf", [128, 8, 2, 65]); Sb = sb("Sb", [128, 8, 2, 65], BF16)

    pp = [ps(f"pp{i}", [128, 512]) for i in range(8)]

    st.d("gpsimd", lambda e: e.dma_start(out=w_q[:], in_=wq_d))
    st.d("gpsimd", lambda e: e.dma_start(out=w_k[:], in_=wk_d))
    st.d("gpsimd", lambda e: e.dma_start(out=w_v[:], in_=wv_d))
    st.d("gpsimd", lambda e: e.dma_start(out=w_o[:], in_=wo_d))
    for s_, d_ in [(bqc, bqc_d), (bkc, bkc_d), (boc, boc_d), (bkb, bkb_d),
                   (bvb, bvb_d), (ccol, ccol_d), (scol, scol_d), (mask, mask_d),
                   (ident, ident_d), (ones64, ones64_d), (onesr, onesr_d),
                   (onesc, onesc_d), (cvec, cvec_d)]:
        st.d("sync", lambda e, s=s_, d=d_: e.dma_start(out=s[:], in_=d))
    if DBG_UPTO != 'nomemset':
        st.c("vector", lambda e: e.memset(Sf[:], 0.0))
        st.c("vector", lambda e: e.memset(Sb[:], 0.0))
        st.c("vector", lambda e: e.memset(v_tm[:, :, :, 64:65], 1.0))
    st.bar()
    if DBG_UPTO in ('stage0', 'nomemset'):
        return

    def do_chunk(c):
        c0 = c * CH
        for tt in range(4):
            st.d("sync", lambda e, tt=tt: e.dma_start(
                out=x_tm[:, tt, :], in_=x0_d[c0 + 128 * tt: c0 + 128 * (tt + 1), :]))
        st.d("sync", lambda e: e.dma_start(out=cosb[:], in_=cosb_d[:, c0:c0 + CH]))
        st.d("sync", lambda e: e.dma_start(out=sinb[:], in_=sinb_d[:, c0:c0 + CH]))
        st.bar()
        if DBG_UPTO == 'ingdma':
            return

        def ing_mm(j):
            def f(e):
                last = None
                for tt in range(4):
                    last = e.transpose(pp[j % 2][:, 128 * tt:128 * (tt + 1)],
                                       x_tm[:, tt, 128 * j:128 * (j + 1)], ident[:])
                return last
            st.c("tensor", f)

        def ing_ev(j):
            st.c("vector", lambda e: e.tensor_copy(x_fm[:, j, :], pp[j % 2][:]))

        def ing_cp(j):
            st.c("scalar", lambda e: e.copy(x_fmb[:, j, :], x_fm[:, j, :]))

        if DBG_UPTO == 'ingmm':
            pipe(st, 8, [(0, ing_mm)])
            return
        if DBG_UPTO == 'ingev1':
            def ing_ev1(j):
                st.c("vector", lambda e: e.tensor_copy(x_fm[:, j, :], pp[j % 2][:]))
            pipe(st, 8, [(0, ing_mm), (1, ing_ev1)])
            return
        pipe(st, 8, [(0, ing_mm), (1, ing_ev), (2, ing_cp)])
        if DBG_UPTO == 'ingest':
            return

        for (w_, bc_, co_, so_) in [(w_q, bqc, qc, qs), (w_k, bkc, kcf, ksf)]:
            def p_mm(m, w_=w_):
                def f(e):
                    for k in range(8):
                        last = e.matmul(pp[2 + m % 2][:], w_[:, k, 128 * m:128 * (m + 1)],
                                 x_fmb[:, k, :], start=(k == 0), stop=(k == 7))
                    return last
                st.c("tensor", f)

            def p_act(m, bc_=bc_):
                st.c("scalar", lambda e: e.activation(
                    tmp2[:, m % 2, :], pp[2 + m % 2][:], AF.Relu, bias=bc_[:, m:m + 1]))

            def p_cs(m, co_=co_, so_=so_):
                st.c("vector", lambda e: e.tensor_tensor(
                    co_[:, m, :], tmp2[:, m % 2, :], cosb[:], ALU.mult))
                st.c("gpsimd", lambda e: e.tensor_tensor(
                    so_[:, m, :], tmp2[:, m % 2, :], sinb[:], ALU.mult))

            pipe(st, 8, [(0, p_mm), (1, p_act), (2, p_cs)])
        if DBG_UPTO == 'proj':
            return

        items = [(tt, hf) for tt in range(4) for hf in range(2)]

        def kt_mm(i):
            tt, hf = items[i]
            def f(e):
                for k in range(8):
                    last = e.matmul(pp[2 + i % 2][:], x_fmb[:, k, 128 * tt:128 * (tt + 1)],
                             w_k[:, k, 512 * hf:512 * (hf + 1)],
                             start=(k == 0), stop=(k == 7))
                return last
            st.c("tensor", f)

        def kt_add(i):
            tt, hf = items[i]
            st.c("vector", lambda e: e.tensor_tensor(
                tmp2[:, i % 2, :], pp[2 + i % 2][:], bkb[:, 512 * hf:512 * (hf + 1)], ALU.add))

        def kt_cs(i):
            tt, hf = items[i]
            cc = ccol[:, 4 * c + tt: 4 * c + tt + 1]
            sc = scol[:, 4 * c + tt: 4 * c + tt + 1]
            st.c("vector", lambda e: e.tensor_scalar(
                kct[:, tt, 512 * hf:512 * (hf + 1)], tmp2[:, i % 2, :], 0.0, cc,
                ALU.max, ALU.mult))
            st.c("gpsimd", lambda e: e.tensor_scalar(
                kst[:, tt, 512 * hf:512 * (hf + 1)], tmp2[:, i % 2, :], 0.0, sc,
                ALU.max, ALU.mult))

        pipe(st, 8, [(0, kt_mm), (1, kt_add), (2, kt_cs)])

        def vt_mm(i):
            tt, hf = items[i]
            def f(e):
                for k in range(8):
                    last = e.matmul(pp[2 + i % 2][:], x_fmb[:, k, 128 * tt:128 * (tt + 1)],
                             w_v[:, k, 512 * hf:512 * (hf + 1)],
                             start=(k == 0), stop=(k == 7))
                return last
            st.c("tensor", f)

        def vt_ev(i):
            tt, hf = items[i]
            st.c("vector", lambda e: e.tensor_tensor(
                v_tm[:, tt, 8 * hf:8 * (hf + 1), 0:64],
                pp[2 + i % 2][:].rearrange("p (h x) -> p h x", x=64),
                bvb[:, 512 * hf:512 * (hf + 1)].rearrange("p (h x) -> p h x", x=64),
                ALU.add))

        pipe(st, 8, [(0, vt_mm), (1, vt_ev)])
        if DBG_UPTO == 'ktvt':
            return

        def hA(h):
            ht, hr, he = h // 2, 64 * (h % 2), 64 * (h % 2) + 64
            q = 4 * (h % 2)
            def f(e):
                last = None
                for ub in range(4):
                    e.matmul(pp[q + ub][:], kcf[hr:he, ht, 128 * ub:128 * (ub + 1)],
                             qc[hr:he, ht, :], start=True, stop=False)
                    last = e.matmul(pp[q + ub][:],
                                    ksf[hr:he, ht, 128 * ub:128 * (ub + 1)],
                                    qs[hr:he, ht, :], start=False, stop=True)
                return last
            st.c("tensor", f)

        def hB(h):
            q = 4 * (h % 2)
            def f(e):
                last = None
                for ub in range(4):
                    last = e.tensor_tensor(am2[:, h % 2, ub, :], pp[q + ub][:],
                                           mask[:, ub, :], ALU.mult)
                return last
            st.c("vector", f)

        def hC(h):
            ht, hr, he = h // 2, 64 * (h % 2), 64 * (h % 2) + 64
            q = 4 * (h % 2)
            def f(e):
                e.matmul(pp[q][hr:he, :], Sb[hr:he, ht, 0, 0:64],
                         qc[hr:he, ht, :], start=True, stop=False)
                e.matmul(pp[q][hr:he, :], Sb[hr:he, ht, 1, 0:64],
                         qs[hr:he, ht, :], start=False, stop=False)
                e.matmul(pp[q + 1][0:1, :], Sb[hr:he, ht, 0, 64:65],
                         qc[hr:he, ht, :], start=True, stop=False)
                e.matmul(pp[q + 1][0:1, :], Sb[hr:he, ht, 1, 64:65],
                         qs[hr:he, ht, :], start=False, stop=False)
                for ub in range(4):
                    e.matmul(pp[q][hr:he, :], v_tm[:, ub, h, 0:64],
                             am2[:, h % 2, ub, :], start=False, stop=(ub == 3))
                for ub in range(4):
                    e.matmul(pp[q + 1][0:1, :], v_tm[:, ub, h, 64:65],
                             am2[:, h % 2, ub, :], start=False, stop=(ub == 3))
                for ub in range(4):
                    e.matmul(pp[q + 2][hr:he, 0:65], kct[:, ub, 64 * h:64 * (h + 1)],
                             v_tm[:, ub, h, :], start=(ub == 0), stop=(ub == 3))
                last = None
                for ub in range(4):
                    last = e.matmul(pp[q + 3][hr:he, 0:65],
                                    kst[:, ub, 64 * h:64 * (h + 1)],
                                    v_tm[:, ub, h, :], start=(ub == 0), stop=(ub == 3))
                return last
            st.c("tensor", f)

        def hD(h):
            q = 4 * (h % 2)
            st.c("vector", lambda e, q=q: e.tensor_scalar(
                den_r[:], pp[q + 1][0:1, :], 1e-6, None, ALU.max))

        def hE(h):
            st.c("vector", lambda e: e.reciprocal(rec_r[:], den_r[:]))

        def hF(h):
            ht, hr, he = h // 2, 64 * (h % 2), 64 * (h % 2) + 64
            q = 4 * (h % 2)
            numf_h = big0[hr:he, 2048:2560]
            st.c("tensor", lambda e, q=q, hr=hr, he=he: e.matmul(
                pp[q + 1][hr:he, :], ones64[:], rec_r[:], start=True, stop=True))
            def f(e):
                e.tensor_copy(numf_h[:], pp[q][hr:he, :])
                e.tensor_tensor(Sf[hr:he, ht, 0, :], Sf[hr:he, ht, 0, :],
                                pp[q + 2][hr:he, 0:65], ALU.add)
                return e.tensor_tensor(Sf[hr:he, ht, 1, :], Sf[hr:he, ht, 1, :],
                                       pp[q + 3][hr:he, 0:65], ALU.add)
            st.c("vector", f)

        def hG(h):
            ht, hr, he = h // 2, 64 * (h % 2), 64 * (h % 2) + 64
            q = 4 * (h % 2)
            numf_h = big0[hr:he, 2048:2560]
            st.c("vector", lambda e: e.tensor_tensor(
                y_b[hr:he, ht, :], numf_h[:], pp[q + 1][hr:he, :], ALU.mult))
            st.c("scalar", lambda e: e.copy(Sb[hr:he, ht, 0, :], Sf[hr:he, ht, 0, :]))
            st.c("scalar", lambda e: e.copy(Sb[hr:he, ht, 1, :], Sf[hr:he, ht, 1, :]))

        pipe(st, 16, [(0, hA), (1, hB), (2, hC), (3, hD), (4, hE),
                      (5, hF), (6, hG)], stride=4)
        def o_mm(m):
            def f(e):
                for k in range(8):
                    last = e.matmul(pp[2 + m % 2][:], w_o[:, k, 128 * m:128 * (m + 1)],
                             y_b[:, k, :], start=(k == 0), stop=(k == 7))
                return last
            st.c("tensor", f)

        def o_act(m):
            st.c("scalar", lambda e: e.activation(
                tmp2[:, m % 2, :], pp[2 + m % 2][:], AF.Identity, bias=boc[:, m:m + 1]))

        def o_res(m):
            st.c("vector", lambda e: e.tensor_tensor(
                r_f[:, m, :], tmp2[:, m % 2, :], x_fm[:, m, :], ALU.add))

        pipe(st, 8, [(0, o_mm), (1, o_act), (2, o_res)])
        if DBG_UPTO == 'omm':
            return

        emit_ln(nc, st, pp, (pp[6][0:1, :], pp[7][0:1, :]), r_f, r2, rowm, rowv, rowmm, rowrs, rowmrs,
                onesc, onesr, x1f_t, x1b_t, cvec)
        for m in range(8):
            st.d("sync", lambda e, m=m: e.dma_start(
                out=x1f_d[128 * m:128 * (m + 1), c0:c0 + CH], in_=x1f_t[:, m, :]))
            st.d("scalar", lambda e, m=m: e.dma_start(
                out=x1b_d[128 * m:128 * (m + 1), c0:c0 + CH], in_=x1b_t[:, m, :]))
        st.bar()

    for c in range(DBG_CHUNKS or NCH):
        do_chunk(c)


def emit_ln(nc, st, pp, ps_stat, r_f, r2, rowm, rowv, rowmm, rowrs, rowmrs,
            onesc, onesr, outf, outb, cvec, nt=8, outb_off=False):
    stat0, stat1 = ps_stat
    """LayerNorm over the partition(feature) axis of nt [128,512] fp32 tiles.
    Writes fp32 result to outf tiles and bf16 copy to outb."""
    def l_sq(m):
        st.c("gpsimd", lambda e: e.tensor_tensor(
            r2[:, m % 2, :], r_f[:, m, :], r_f[:, m, :], ALU.mult))

    def l_mm(m):
        def f(e):
            last = e.matmul(stat0, onesc[:], r_f[:, m, :],
                            start=(m == 0), stop=(m == nt - 1))
            last = e.matmul(stat1, onesc[:], r2[:, m % 2, :],
                            start=(m == 0), stop=(m == nt - 1))
            return last
        st.c("tensor", f)

    pipe(st, nt, [(0, l_sq), (1, l_mm)])
    inv = 1.0 / (128 * nt)

    def l_rows(e):
        e.tensor_scalar(rowm[:], stat0, inv, None, ALU.mult)
        return e.tensor_scalar(rowv[:], stat1, inv, None, ALU.mult)
    st.c("vector", l_rows)
    st.bar()
    st.c("vector", lambda e: e.tensor_tensor(rowmm[:], rowm[:], rowm[:], ALU.mult))
    st.bar()
    st.c("vector", lambda e: e.tensor_tensor(rowv[:], rowv[:], rowmm[:], ALU.subtract))
    st.bar()
    st.c("scalar", lambda e: e.activation(rowrs[:], rowv[:], AF.Ln, bias=cvec[0:1, 0:1]))
    st.bar()
    st.c("scalar", lambda e: e.activation(rowrs[:], rowrs[:], AF.Exp, scale=cvec[0:1, 1:2]))
    st.bar()
    st.c("vector", lambda e: e.tensor_tensor(rowmrs[:], rowm[:], rowrs[:], ALU.mult))
    st.bar()

    def l_bc(e):
        last = e.matmul(pp[0][:], onesr[:], rowrs[:], start=True, stop=True)
        last = e.matmul(pp[1][:], onesr[:], rowmrs[:], start=True, stop=True)
        return last
    st.c("tensor", l_bc)
    st.bar()

    def l_n1(m):
        st.c("vector", lambda e: e.tensor_tensor(
            r2[:, m % 2, :], r_f[:, m, :], pp[0][:], ALU.mult))

    def l_n2(m):
        st.c("vector", lambda e: e.tensor_tensor(
            outf[:, m, :], r2[:, m % 2, :], pp[1][:], ALU.subtract))

    def l_cp(m):
        st.c("scalar", lambda e: e.copy(outb[:, m, :], outf[:, m, :]))

    if outb_off:
        pipe(st, nt, [(0, l_n1), (1, l_n2)])
    else:
        pipe(st, nt, [(0, l_n1), (1, l_n2), (2, l_cp)])


NG = 1024
NGR = T // NG


def emit_l0f(nc, st, stk, pre=None):
    pre = pre or {}
    def dt(nm, shape, dtype, kind=None):
        if nm in pre:
            return pre[nm]
        if kind is None:
            return nc.dram_tensor("f_" + nm, shape, dtype).ap()
        return nc.dram_tensor("f_" + nm, shape, dtype, kind=kind).ap()
    sb = lambda n_, s_, d_=F32: stk.enter_context(nc.sbuf_tensor("f_" + n_, s_, d_))
    ps = lambda n_, s_, d_=F32: stk.enter_context(nc.psum_tensor("f_" + n_, s_, d_))

    x1f_d = dt("x1f", [D, T], F32, kind="ExternalInput")
    x1b_d = dt("x1b", [D, T], BF16, kind="ExternalInput")
    w1r_d = dt("w1r", [8, 128, 8, 4096], F32, kind="ExternalInput")
    w3r_d = dt("w3r", [8, 128, 8, 4096], F32, kind="ExternalInput")
    w2r_d = dt("w2r", [8, 128, 16, 1024], F32, kind="ExternalInput")
    sw1r_d = dt("sw1r", [128, 8, 4096], F32, kind="ExternalInput")
    sw3r_d = dt("sw3r", [128, 8, 4096], F32, kind="ExternalInput")
    sw2r_d = dt("sw2r", [128, 16, 1024], F32, kind="ExternalInput")
    b1c_d = dt("b1c", [128, 9, 32], F32, kind="ExternalInput")
    b3c_d = dt("b3c", [128, 9, 32], F32, kind="ExternalInput")
    b2c_d = dt("b2c", [128, 9, 8], F32, kind="ExternalInput")
    gwr_d = dt("gwr", [128, 8, 8], F32, kind="ExternalInput")
    gbb_d = dt("gbb", [128, 8], F32, kind="ExternalInput")
    esel_d = dt("esel", [8, 8, 128], F32, kind="ExternalInput")
    ident_d = dt("ident", [128, 128], F32, kind="ExternalInput")
    onesr_d = dt("onesr", [1, 128], F32, kind="ExternalInput")
    onesc_d = dt("onesc", [128, 1], F32, kind="ExternalInput")
    cvec_d = dt("cvec", [128, 8], F32, kind="ExternalInput")
    x2f_d = dt("x2f", [D, T], F32, kind="ExternalOutput")
    x2b_d = dt("x2b", [D, T], BF16, kind="ExternalOutput")

    w1c = dt("w1c", [8, 128, 8, 4096], BF16)
    w3c = dt("w3c", [8, 128, 8, 4096], BF16)
    w2c = dt("w2c", [8, 128, 16, 1024], BF16)
    sw1c = dt("sw1c", [128, 8, 4096], BF16)
    sw3c = dt("sw3c", [128, 8, 4096], BF16)
    sw2c = dt("sw2c", [128, 16, 1024], BF16)

    x1bg = sb("x1bg", [128, 8, NG], BF16)
    acc = sb("acc", [128, 8, NG])
    act = sb("act", [128, 16, NG], BF16)
    wbuf = sb("wbuf", [128, 2, 2, 2, 8, 128], BF16)
    w2s = sb("w2s", [128, 16, 1024], BF16)
    combT = sb("combT", [8, NG])
    comb_b = sb("comb_b", [128, NG], BF16)
    ld = sb("ld", [128, 8, 512])
    h1s = sb("h1s", [128, 2, 2, 512], BF16)
    swsc = sb("swsc", [128, 17, 512], BF16)
    tmp2 = sb("tmpf", [128, 2, 512])
    rt_s = sb("rt_s", [128, 8, 56])
    b1c = sb("b1c_s", [128, 9, 32]); b3c = sb("b3c_s", [128, 9, 32])
    b2c = sb("b2c_s", [128, 9, 8])
    gwr = sb("gwr_s", [128, 8, 8], BF16); gbb = sb("gbb_s", [128, 8])
    esel = sb("esel_s", [8, 8, 128])
    ident = sb("ident_s", [128, 128])
    onesr = sb("onesr_s", [1, 128]); onesc = sb("onesc_s", [128, 1])
    cvec = sb("cvec_s", [128, 8])
    rcol = sb("rcol", [128, 48])
    rowm = sb("rowm", [1, 512]); rowv = sb("rowv", [1, 512])
    rowmm = sb("rowmm", [1, 512]); rowrs = sb("rowrs", [1, 512])
    rowmrs = sb("rowmrs", [1, 512])
    r2 = sb("r2", [128, 2, 512])
    x2b_t = act[:, 0:8, 0:512]

    pp = [ps(f"pp{i}", [128, 512]) for i in range(8)]

    for s_, d_ in [(b1c, b1c_d), (b3c, b3c_d), (b2c, b2c_d),
                   (gbb, gbb_d), (esel, esel_d), (ident, ident_d),
                   (onesr, onesr_d), (onesc, onesc_d), (cvec, cvec_d)]:
        st.d("sync", lambda e, s=s_, d=d_: e.dma_start(out=s[:], in_=d))
    st.d("gpsimd", lambda e: e.dma_start(out=gwr[:], in_=gwr_d))
    st.bar()

    stg_8x2048 = [acc[:].bitcast(BF16),
                  act[:].rearrange("p (a x) b -> p a (x b)", x=2)]
    stg_16x1024 = [acc[:].bitcast(BF16).rearrange("p a (x b) -> p (a x) b", b=1024),
                   act[:]]
    pieces = []
    for e in range(8):
        for p in range(2):
            pieces.append((w1r_d[e, :, :, 2048 * p:2048 * (p + 1)],
                           w1c[e, :, :, 2048 * p:2048 * (p + 1)], (8, 2048)))
            pieces.append((w3r_d[e, :, :, 2048 * p:2048 * (p + 1)],
                           w3c[e, :, :, 2048 * p:2048 * (p + 1)], (8, 2048)))
        pieces.append((w2r_d[e], w2c[e], (16, 1024)))
    for p in range(2):
        pieces.append((sw1r_d[:, :, 2048 * p:2048 * (p + 1)],
                       sw1c[:, :, 2048 * p:2048 * (p + 1)], (8, 2048)))
        pieces.append((sw3r_d[:, :, 2048 * p:2048 * (p + 1)],
                       sw3c[:, :, 2048 * p:2048 * (p + 1)], (8, 2048)))
    pieces.append((sw2r_d, sw2c, (16, 1024)))

    def pc_buf(i):
        a_, b_ = pieces[i][2]
        return (stg_8x2048 if a_ == 8 else stg_16x1024)[i % 2]

    def pc_ld(i):
        a_, b_ = pieces[i][2]
        buf = pc_buf(i)
        for a in range(a_):
            st.d("gpsimd", lambda e, a=a, buf=buf: e.dma_start(
                out=buf[:, a, :], in_=pieces[i][0][:, a, :],
                max_dma_last_dim=4096))

    def pc_stn(i):
        a_, b_ = pieces[i][2]
        buf = pc_buf(i)
        for a in range(a_):
            st.d("sync", lambda e, a=a, buf=buf: e.dma_start(
                out=pieces[i][1][:, a, :], in_=buf[:, a, :]))

    pipe(st, len(pieces), [(0, pc_ld), (1, pc_stn)])

    def do_expert(g, e_i):
        shared = (e_i == 8)
        w1src = sw1c if shared else w1c[e_i]
        w3src = sw3c if shared else w3c[e_i]
        w2src = sw2c if shared else w2c[e_i]
        NI = NG // 512

        if not shared:
            def cb_mm(q):
                st.c("tensor", lambda e, q=q: e.matmul(
                    pp[q % 2][:], esel[:, e_i, :], combT[:, 512 * q:512 * (q + 1)],
                    start=True, stop=True))

            def cb_ev(q):
                st.c("vector", lambda e, q=q: e.tensor_copy(
                    comb_b[:, 512 * q:512 * (q + 1)], pp[q % 2][:]))

            pipe(st, NI, [(0, cb_mm), (1, cb_ev)])

        for w_ in range(2):
            for gl in range(2):
                src = (w1src if w_ == 0 else w3src)
                st.d("scalar", lambda e, w_=w_, gl=gl, src=src: e.dma_start(
                    out=wbuf[:, 0, w_, gl, :, :],
                    in_=src[:, :, 128 * (16 * gl):128 * (16 * gl) + 128]))
        st.bar()

        items = [(j, n4) for j in range(16) for n4 in range(NI)]

        def ph_ld(i):
            j, n4 = items[i]
            if i < 16:
                st.d("scalar", lambda e, i=i: e.dma_start(
                    out=w2s[:, i, :], in_=w2src[:, i, :]))
            if j + 1 < 16:
                jn = j + 1
                w_ = n4
                for gl in range(2):
                    src = (w1src if w_ == 0 else w3src)
                    st.d("sync", lambda e, w_=w_, gl=gl, src=src, jn=jn:
                         e.dma_start(
                             out=wbuf[:, jn % 2, w_, gl, :, :],
                             in_=src[:, :, 128 * (jn + 16 * gl):
                                     128 * (jn + 16 * gl) + 128]),
                         slack=0)

        def ph_mm(i):
            j, n4 = items[i]
            q = 4 * (i % 2)
            def f(e):
                last = None
                for w_ in range(2):
                    for gl in range(2):
                        for k in range(8):
                            last = e.matmul(pp[q + 2 * w_ + gl][:],
                                            wbuf[:, j % 2, w_, gl, k, :],
                                            x1bg[:, k, 512 * n4:512 * (n4 + 1)],
                                            start=(k == 0), stop=(k == 7))
                return last
            st.c("tensor", f)

        def ph_ev(i):
            j, n4 = items[i]
            q = 4 * (i % 2)
            st.c("scalar", lambda e: e.activation(
                h1s[:, i % 2, 0, :], pp[q][:], AF.Identity,
                bias=b1c[:, e_i, j:j + 1]))
            st.c("scalar", lambda e: e.activation(
                h1s[:, i % 2, 1, :], pp[q + 1][:], AF.Identity,
                bias=b1c[:, e_i, j + 16:j + 17]))
            st.c("vector", lambda e: e.tensor_scalar(
                swsc[:, i % 2, :], pp[q + 2][:],
                b3c[:, e_i, j:j + 1], None, ALU.add))
            st.c("vector", lambda e: e.tensor_scalar(
                swsc[:, 2 + i % 2, :], pp[q + 3][:],
                b3c[:, e_i, j + 16:j + 17], None, ALU.add))

        def ph_prod(i):
            def f(e):
                e.tensor_tensor(swsc[:, 4 + i % 4, :], h1s[:, i % 2, 0, :],
                                swsc[:, i % 2, :], ALU.mult)
                return e.tensor_tensor(swsc[:, 8 + i % 5, :], h1s[:, i % 2, 1, :],
                                       swsc[:, 2 + i % 2, :], ALU.mult)
            st.c("vector", f)

        def ph_min(i):
            def f(e):
                e.tensor_scalar(swsc[:, 4 + i % 4, :], swsc[:, 4 + i % 4, :],
                                7.0, None, ALU.min)
                return e.tensor_scalar(swsc[:, 8 + i % 5, :], swsc[:, 8 + i % 5, :],
                                       7.0, None, ALU.min)
            st.c("vector", f)

        def ph_lcm(i):
            st.c("vector", lambda e: e.tensor_scalar(
                swsc[:, 8 + i % 5, :], swsc[:, 8 + i % 5, :], -7.0, 1.0,
                ALU.max, ALU.add))
            st.c("scalar", lambda e: e.activation(
                swsc[:, 13 + i % 2, :], swsc[:, 4 + i % 4, :], AF.Sigmoid,
                scale=cvec[:, 4:5]))

        def ph_cmb(i):
            j, n4 = items[i]
            if not shared:
                st.c("gpsimd", lambda e, n4=n4: e.tensor_tensor(
                    swsc[:, 8 + i % 5, :], swsc[:, 8 + i % 5, :],
                    comb_b[:, 512 * n4:512 * (n4 + 1)], ALU.mult))
            st.c("vector", lambda e: e.tensor_tensor(
                swsc[:, 15 + i % 2, :], swsc[:, 4 + i % 4, :],
                swsc[:, 13 + i % 2, :], ALU.mult))

        def ph_fin(i):
            j, n4 = items[i]
            st.c("vector", lambda e, j=j, n4=n4: e.tensor_tensor(
                act[:, j, 512 * n4:512 * (n4 + 1)], swsc[:, 15 + i % 2, :],
                swsc[:, 8 + i % 5, :], ALU.mult))

        pipe(st, len(items), [(0, ph_ld), (0, ph_mm), (1, ph_ev),
                              (2, ph_prod), (3, ph_min), (4, ph_lcm),
                              (5, ph_cmb), (6, ph_fin)])

        w2items = [(mo, n4) for mo in range(8) for n4 in range(NI)]

        def w2_mm(i):
            mo, n4 = w2items[i]
            def f(e):
                last = None
                for k in range(16):
                    last = e.matmul(pp[i % 6][:], w2s[:, k, 128 * mo:128 * (mo + 1)],
                                    act[:, k, 512 * n4:512 * (n4 + 1)],
                                    start=(k == 0), stop=(k == 15))
                return last
            st.c("tensor", f)

        def w2_ev(i):
            mo, n4 = w2items[i]
            if shared:
                st.c("vector", lambda e: e.tensor_scalar(
                    tmp2[:, i % 2, :], pp[i % 6][:], b2c[:, e_i, mo:mo + 1],
                    None, ALU.add))
            else:
                st.c("vector", lambda e: e.scalar_tensor_tensor(
                    tmp2[:, i % 2, :], comb_b[:, 512 * n4:512 * (n4 + 1)],
                    b2c[:, e_i, mo:mo + 1], pp[i % 6][:], ALU.mult, ALU.add))

        def w2_acc(i):
            mo, n4 = w2items[i]
            st.c("gpsimd", lambda e: e.tensor_tensor(
                acc[:, mo, 512 * n4:512 * (n4 + 1)],
                acc[:, mo, 512 * n4:512 * (n4 + 1)], tmp2[:, i % 2, :], ALU.add))

        pipe(st, len(w2items), [(0, w2_mm), (1, w2_ev), (2, w2_acc)])

    def do_group(g):
        g0 = g * NG

        for k in range(8):
            st.d("sync", lambda e, k=k: e.dma_start(
                out=x1bg[:, k, :], in_=x1b_d[128 * k:128 * (k + 1), g0:g0 + NG]))
        st.c("vector", lambda e: e.memset(acc[:], 0.0))
        st.bar()

        def rt_mm(i):
            def f(e):
                last = None
                for k in range(8):
                    last = e.matmul(pp[i % 3][:, 0:8],
                                    x1bg[:, k, 128 * i:128 * (i + 1)],
                                    gwr[:, k, :], start=(k == 0), stop=(k == 7))
                return last
            st.c("tensor", f)

        def _sl(i, k):
            return rt_s[:, i % 8, 8 * k:8 * k + 8]

        def _rc(i, k):
            return rcol[:, 6 * (i % 8) + k:6 * (i % 8) + k + 1]

        def rt_max(i):
            st.c("vector", lambda e: e.tensor_reduce(
                _rc(i, 0), pp[i % 3][:, 0:8], AX.X, ALU.max, negate=True))

        def rt_exp(i):
            st.c("scalar", lambda e: e.activation(
                _sl(i, 0), pp[i % 3][:, 0:8], AF.Exp, bias=_rc(i, 0)))

        def rt_z(i):
            st.c("vector", lambda e: e.tensor_reduce(_rc(i, 1), _sl(i, 0), AX.X, ALU.add))

        def rt_rz(i):
            st.c("vector", lambda e: e.reciprocal(_rc(i, 2), _rc(i, 1)))

        def rt_pb(i):
            def f(e):
                e.tensor_scalar(_sl(i, 1), _sl(i, 0), _rc(i, 2), None, ALU.mult)
                return e.scalar_tensor_tensor(
                    _sl(i, 2), _sl(i, 0), _rc(i, 2), gbb[:], ALU.mult, ALU.add)
            st.c("vector", f)

        def rt_mx1(i):
            st.c("vector", lambda e: e.tensor_reduce(_rc(i, 3), _sl(i, 2), AX.X, ALU.max))

        def rt_m1(i):
            st.c("vector", lambda e: e.tensor_scalar(
                _sl(i, 4), _sl(i, 2), _rc(i, 3), None, ALU.is_equal))

        def rt_b2(i):
            st.c("vector", lambda e: e.scalar_tensor_tensor(
                _sl(i, 3), _sl(i, 4), -1e30, _sl(i, 2), ALU.mult, ALU.add))

        def rt_mx2(i):
            st.c("vector", lambda e: e.tensor_reduce(_rc(i, 4), _sl(i, 3), AX.X, ALU.max))

        def rt_m2(i):
            st.c("vector", lambda e: e.tensor_scalar(
                _sl(i, 5), _sl(i, 3), _rc(i, 4), None, ALU.is_equal))

        def rt_m12(i):
            st.c("vector", lambda e: e.tensor_tensor(
                _sl(i, 5), _sl(i, 4), _sl(i, 5), ALU.add))

        def rt_cmb(i):
            st.c("vector", lambda e: e.tensor_tensor(
                _sl(i, 6), _sl(i, 1), _sl(i, 5), ALU.mult))

        def rt_tp(i):
            st.c("tensor", lambda e: e.transpose(
                pp[3 + (i // 4) % 2][0:8, 128 * (i % 4):128 * (i % 4 + 1)],
                _sl(i, 6), ident[:]))

        def rt_ev(i):
            if i % 4 == 3:
                q = i // 4
                st.c("vector", lambda e, q=q: e.tensor_copy(
                    combT[:, 512 * q:512 * (q + 1)], pp[3 + q % 2][0:8, :]))

        pipe(st, NG // 128, [(0, rt_mm), (1, rt_max), (2, rt_exp), (3, rt_z),
                             (4, rt_rz), (5, rt_pb), (6, rt_mx1), (7, rt_m1),
                             (8, rt_b2), (9, rt_mx2), (10, rt_m2), (11, rt_m12),
                             (12, rt_cmb), (13, rt_tp), (14, rt_ev)])



        for e_i in range(9):
            do_expert(g, e_i)

        for cc in range(NG // CH):
            t0 = g0 + cc * CH
            for m in range(8):
                st.d("sync", lambda e, m=m, t0=t0: e.dma_start(
                    out=ld[:, m, :], in_=x1f_d[128 * m:128 * (m + 1), t0:t0 + CH]))
            st.bar()
            rr = acc[:, :, cc * CH:(cc + 1) * CH]
            for m in range(8):
                st.c("vector", lambda e, m=m, rr=rr: e.tensor_tensor(
                    rr[:, m, :], rr[:, m, :], ld[:, m, :], ALU.add))
            st.bar()
            emit_ln(nc, st, pp, (pp[4][0:1, :], pp[5][0:1, :]), rr, r2, rowm, rowv, rowmm, rowrs,
                    rowmrs, onesc, onesr, ld, x2b_t, cvec)
            for m in range(8):
                st.d("sync", lambda e, m=m, t0=t0: e.dma_start(
                    out=x2f_d[128 * m:128 * (m + 1), t0:t0 + CH], in_=ld[:, m, :]))
                st.d("scalar", lambda e, m=m, t0=t0: e.dma_start(
                    out=x2b_d[128 * m:128 * (m + 1), t0:t0 + CH], in_=x2b_t[:, m, :]))
            st.bar()

    for g in range(NGR):
        do_group(g)


def build_l0f():
    nc = bass.Bass("TRN2", target_bir_lowering=False)
    with ExitStack() as stk:
        S = stk.enter_context(nc.semaphore("S"))
        S2 = stk.enter_context(nc.semaphore("S2"))
        S3 = stk.enter_context(nc.semaphore("S3"))
        block = stk.enter_context(nc.Block())
        st = Sched()
        emit_l0f(nc, st, stk)
        st.emit(block, S, S2, S3)
    return nc


def emit_l1a(nc, st, stk, pre=None):
    pre = pre or {}
    def dt(nm, shape, dtype, kind=None):
        if nm in pre:
            return pre[nm]
        if kind is None:
            return nc.dram_tensor("m_" + nm, shape, dtype).ap()
        return nc.dram_tensor("m_" + nm, shape, dtype, kind=kind).ap()
    sb = lambda n_, s_, d_=F32: stk.enter_context(nc.sbuf_tensor("m_" + n_, s_, d_))
    ps = lambda n_, s_, d_=F32: stk.enter_context(nc.psum_tensor("m_" + n_, s_, d_))

    x2f_d = dt("x2f", [D, T], F32, kind="ExternalInput")
    x2b_d = dt("x2b", [D, T], BF16, kind="ExternalInput")
    winr_d = dt("winr", [128, 8, 4096], F32, kind="ExternalInput")
    woutr_d = dt("woutr", [128, 16, 1024], F32, kind="ExternalInput")
    wxp_d = dt("wxp", [128, 16, 80], F32, kind="ExternalInput")
    cwc_d = dt("cwc", [128, 16, 4], F32, kind="ExternalInput")
    cbc_d = dt("cbc", [128, 16], F32, kind="ExternalInput")
    arlog_d = dt("arlog", [128, 256], F32, kind="ExternalInput")
    dcol_d = dt("dcol", [128, 16], F32, kind="ExternalInput")
    repm_d = dt("repm", [128, 128], F32, kind="ExternalInput")
    sumsel_d = dt("sumsel", [128, 64], F32, kind="ExternalInput")
    selp_d = dt("selp", [48, 8, 128], F32, kind="ExternalInput")
    onesr_d = dt("onesr", [1, 128], F32, kind="ExternalInput")
    onesc_d = dt("onesc", [128, 1], F32, kind="ExternalInput")
    cvec_d = dt("cvec", [128, 8], F32, kind="ExternalInput")
    x3f_d = dt("x3f", [D, T], F32, kind="ExternalOutput")
    x3b_d = dt("x3b", [D, T], BF16, kind="ExternalOutput")

    winc = dt("winc", [128, 8, 4096], BF16)
    woutc = dt("woutc", [128, 16, 1024], BF16)

    xin = sb("xin", [128, 16, 515])
    tails = sb("tails", [128, 16, 3])
    xcb = sb("xcb", [128, 16, 512], BF16)
    szb = sb("szb", [128, 16, 512], BF16)
    yb = xin[:].rearrange("p a b -> p (a b)")[:, 4096:8192].bitcast(BF16).rearrange(
        "p (a b) -> p a b", b=512)
    yzb = yb
    x2bg = sb("x2bg", [128, 8, 512], BF16)
    ld = sb("ld", [128, 8, 512])
    r_f = xin[:].rearrange("p a b -> p (a b)")[:, 0:4096].rearrange(
        "p (a b) -> p a b", b=512)
    wbuf = sb("wbuf", [128, 3, 8, 128], BF16)
    wbuf2 = sb("wbuf2", [128, 2, 16, 128], BF16)
    wxs = sb("wxs", [128, 16, 80], BF16)
    cwc = sb("cwc_s", [128, 16, 4]); cbc = sb("cbc_s", [128, 16])
    ar = sb("ar_s", [128, 256]); arlog = sb("arlog_s", [128, 256])
    dcol = sb("dcol_s", [128, 16])
    repm = sb("repm_s", [128, 128], BF16)
    sumsel = sb("sumsel_s", [128, 64], BF16)
    selp = sb("selp_s", [48, 8, 128], BF16)
    onesr = sb("onesr_s", [1, 128]); onesc = sb("onesc_s", [128, 1])
    cvec = sb("cvec_s", [128, 8])
    ssm = sb("ssm_s", [80, 512])
    bpr = sb("bpr", [16, 512], BF16); cprt = sb("cprt", [48, 512], BF16)
    dtb = sb("dtb", [128, 512])
    Bb = sb("Bb", [128, 8, 512], BF16); Cb = sb("Cb", [128, 8, 512], BF16)
    dA = sb("dA", [128, 8, 512], BF16)
    bt = sb("bt", [128, 4, 512], BF16)
    ht = sb("ht", [128, 4, 512], BF16)
    hct = sb("hct", [128, 4, 512], BF16)
    xcrb = sb("xcrb", [128, 2, 512], BF16)
    stcol = sb("stcol", [128, 256])
    cv1 = sb("cv1", [128, 2, 1024]); cv2 = sb("cv2", [128, 2, 1024])
    cv3 = sb("cv3", [128, 2, 512])
    srow = sb("srow", [1, 6, 512])
    rowm = sb("rowm", [1, 512]); rowv = sb("rowv", [1, 512])
    rowmm = sb("rowmm", [1, 512]); rowrs = sb("rowrs", [1, 512])
    rowmrs = sb("rowmrs", [1, 512])
    r2 = sb("r2", [128, 2, 512])

    pp = [ps(f"pp{i}", [128, 512]) for i in range(4)]
    ps_y = ps("ps_y", [128, 2, 512])
    ps_stat = ps("ps_stat", [1, 2, 512])

    for s_, d_ in [(cwc, cwc_d), (cbc, cbc_d), (arlog, arlog_d), (dcol, dcol_d),
                   (onesr, onesr_d), (onesc, onesc_d), (cvec, cvec_d)]:
        st.d("sync", lambda e, s=s_, d=d_: e.dma_start(out=s[:], in_=d))
    for s_, d_ in [(wxs, wxp_d), (repm, repm_d), (sumsel, sumsel_d),
                   (selp, selp_d)]:
        st.d("gpsimd", lambda e, s=s_, d=d_: e.dma_start(out=s[:], in_=d))
    st.c("vector", lambda e: e.memset(stcol[:], 0.0))
    st.bar()
    st.c("scalar", lambda e: e.activation(ar[:], arlog[:], AF.Exp))
    st.bar()
    st.c("vector", lambda e: e.tensor_scalar(ar[:], ar[:], -1.0, None, ALU.mult))
    st.bar()

    stg = [xin[:].rearrange("p a b -> p (a b)")[:, 0:4096].bitcast(BF16)
           .rearrange("p (a b) -> p a b", b=1024),
           xin[:].rearrange("p a b -> p (a b)")[:, 4096:8192].bitcast(BF16)
           .rearrange("p (a b) -> p a b", b=1024)]
    pieces = []
    for q in range(4):
        pieces.append((winr_d[:, :, 1024 * q:1024 * (q + 1)],
                       winc[:, :, 1024 * q:1024 * (q + 1)], 8))
    pieces.append((woutr_d[:, 0:8, :], woutc[:, 0:8, :], 8))
    pieces.append((woutr_d[:, 8:16, :], woutc[:, 8:16, :], 8))

    def pc_ld(i):
        src, dst, na = pieces[i]
        for a in range(na):
            st.d("gpsimd", lambda e, a=a, src=src, i=i: e.dma_start(
                out=stg[i % 2][:, a, :], in_=src[:, a, :], max_dma_last_dim=4096))

    def pc_stn(i):
        src, dst, na = pieces[i]
        for a in range(na):
            st.d("sync", lambda e, a=a, dst=dst, i=i: e.dma_start(
                out=dst[:, a, :], in_=stg[i % 2][:, a, :]))

    pipe(st, len(pieces), [(0, pc_ld), (1, pc_stn)])
    st.bar()

    def do_chunk(c):
        c0 = c * CH
        for k in range(8):
            st.d("sync", lambda e, k=k: e.dma_start(
                out=x2bg[:, k, :], in_=x2b_d[128 * k:128 * (k + 1), c0:c0 + CH]))
        if c > 0:
            for j in range(16):
                st.c("gpsimd", lambda e, j=j: e.tensor_copy(
                    xin[:, j, 0:3], tails[:, j, :]))
        else:
            st.c("gpsimd", lambda e: e.memset(xin[:, :, 0:3], 0.0))
        for w_ in range(2):
            st.d("scalar", lambda e, w_=w_: e.dma_start(
                out=wbuf[:, w_, :, :],
                in_=winc[:, :, 128 * w_:128 * (w_ + 1)]))
        st.bar()

        def wi_ld(m):
            if m + 2 < 32:
                st.d("sync", lambda e, m=m: e.dma_start(
                    out=wbuf[:, (m + 2) % 3, :, :],
                    in_=winc[:, :, 128 * (m + 2):128 * (m + 3)]), slack=0)

        def wi_mm(m):
            def f(e):
                last = None
                for k in range(8):
                    last = e.matmul(pp[m % 2][:], wbuf[:, m % 3, k, :],
                                    x2bg[:, k, :], start=(k == 0), stop=(k == 7))
                return last
            st.c("tensor", f)

        def wi_ev(m):
            if m < 16:
                st.c("vector", lambda e, m=m: e.tensor_copy(
                    xin[:, m, 3:515], pp[m % 2][:]))
            else:
                st.c("scalar", lambda e, m=m: e.activation(
                    szb[:, m - 16, :], pp[m % 2][:], AF.Silu))

        pipe(st, 32, [(0, wi_ld), (0, wi_mm), (1, wi_ev)])

        def cv_a(j):
            def f(e):
                e.tensor_scalar(cv1[:, j % 2, 0:512], xin[:, j, 0:512],
                                cwc[:, j, 0:1], None, ALU.mult)
                e.tensor_scalar(cv1[:, j % 2, 512:1024], xin[:, j, 1:513],
                                cwc[:, j, 1:2], None, ALU.mult)
                e.tensor_scalar(cv2[:, j % 2, 0:512], xin[:, j, 2:514],
                                cwc[:, j, 2:3], None, ALU.mult)
                return e.tensor_scalar(cv2[:, j % 2, 512:1024], xin[:, j, 3:515],
                                       cwc[:, j, 3:4], None, ALU.mult)
            st.c("vector", f)

        def cv_b(j):
            def f(e):
                e.tensor_tensor(cv1[:, j % 2, 0:512], cv1[:, j % 2, 0:512],
                                cv1[:, j % 2, 512:1024], ALU.add)
                return e.tensor_tensor(cv2[:, j % 2, 0:512], cv2[:, j % 2, 0:512],
                                       cv2[:, j % 2, 512:1024], ALU.add)
            st.c("gpsimd", f)

        def cv_c(j):
            st.c("vector", lambda e, j=j: e.tensor_tensor(
                cv3[:, j % 2, :], cv1[:, j % 2, 0:512], cv2[:, j % 2, 0:512], ALU.add))

        def cv_d(j):
            st.c("scalar", lambda e, j=j: e.activation(
                xcb[:, j, :], cv3[:, j % 2, :], AF.Silu, bias=cbc[:, j:j + 1]))

        pipe(st, 16, [(0, cv_a), (1, cv_b), (2, cv_c), (3, cv_d)])
        for j in range(16):
            st.c("vector", lambda e, j=j: e.tensor_copy(
                tails[:, j, :], xin[:, j, 512:515]))
        st.bar()

        def wx_mm(e):
            last = None
            for k in range(16):
                last = e.matmul(pp[2][0:80, :], wxs[:, k, :], xcb[:, k, :],
                                start=(k == 0), stop=(k == 15))
            return last
        st.c("tensor", wx_mm)
        st.bar()
        st.c("vector", lambda e: e.tensor_copy(ssm[:], pp[2][0:80, :]))
        st.bar()

        st.d("scalar", lambda e: e.dma_start(out=srow[:, 0, :], in_=ssm[64:65, :]))
        st.bar()
        dtr = srow[:, 0, :]
        st.c("vector", lambda e: e.tensor_scalar(
            srow[:, 1, :], dtr, -1.0, None, ALU.mult))
        st.bar()
        st.c("vector", lambda e: e.tensor_tensor(
            srow[:, 1, :], srow[:, 1, :], dtr, ALU.max))
        st.bar()
        st.c("scalar", lambda e: e.activation(
            srow[:, 2, :], srow[:, 1, :], AF.Exp, scale=cvec[0:1, 2:3]))
        st.bar()
        st.c("scalar", lambda e: e.activation(
            srow[:, 3, :], srow[:, 2, :], AF.Ln, bias=cvec[0:1, 3:4]))
        st.c("vector", lambda e: e.tensor_scalar(
            srow[:, 4, :], dtr, 0.0, None, ALU.max))
        st.bar()
        st.c("vector", lambda e: e.tensor_tensor(
            srow[:, 5, :], srow[:, 3, :], srow[:, 4, :], ALU.add))
        st.bar()
        st.c("tensor", lambda e: e.matmul(pp[3][:], onesr[:], srow[:, 5, :],
                                          start=True, stop=True))
        st.bar()
        st.c("vector", lambda e: e.tensor_copy(dtb[:], pp[3][:]))
        st.bar()
        st.c("vector", lambda e: e.tensor_tensor(
            bpr[:], ssm[0:16, :], pp[3][0:16, :], ALU.mult))
        st.c("scalar", lambda e: e.copy(cprt[32:48, :], ssm[32:48, :]))
        st.bar()

        def bc_mm(i):
            sp, which = i // 2, i % 2
            if which == 0:
                st.c("tensor", lambda e, sp=sp: e.matmul(
                    pp[i % 2][:], selp[0:16, sp, :], bpr[:], start=True, stop=True))
            else:
                st.c("tensor", lambda e, sp=sp: e.matmul(
                    pp[i % 2][:], selp[32:48, sp, :], cprt[32:48, :],
                    start=True, stop=True))

        def bc_ev(i):
            sp, which = i // 2, i % 2
            dst = Bb if which == 0 else Cb
            st.c("vector", lambda e: e.tensor_copy(dst[:, sp, :], pp[i % 2][:]))

        pipe(st, 16, [(0, bc_mm), (1, bc_ev)])

        units = [(dg, sp) for dg in range(32) for sp in range(8)]

        def u_pe(i):
            dg, sp = units[i]
            if sp == 0:
                hb = 64 * (dg % 2)
                st.c("tensor", lambda e, dg=dg, hb=hb: e.matmul(
                    pp[2 + (dg % 2)][:], repm[hb:hb + 64, :],
                    xcb[hb:hb + 64, dg // 2, :], start=True, stop=True))

        def u_xev(i):
            dg, sp = units[i]
            if sp == 0:
                st.c("scalar", lambda e, dg=dg: e.copy(
                    xcrb[:, dg % 2, :], pp[2 + (dg % 2)][:]))

        def u_dA(i):
            dg, sp = units[i]
            st.c("scalar", lambda e, i=i: e.activation(
                dA[:, i % 8, :], dtb[:], AF.Exp, scale=ar[:, i:i + 1]))

        def u_b(i):
            dg, sp = units[i]
            st.c("vector", lambda e, i=i, dg=dg, sp=sp: e.tensor_tensor(
                bt[:, i % 4, :], xcrb[:, dg % 2, :], Bb[:, sp, :], ALU.mult))

        def u_scan(i):
            dg, sp = units[i]
            st.c("vector", lambda e, i=i: e.tensor_tensor_scan(
                ht[:, i % 4, :], dA[:, i % 8, :], bt[:, i % 4, :],
                stcol[:, i:i + 1], ALU.mult, ALU.add))

        def u_st(i):
            st.c("scalar", lambda e, i=i: e.copy(
                stcol[:, i:i + 1], ht[:, i % 4, 511:512]))

        def u_hc(i):
            dg, sp = units[i]
            st.c("gpsimd", lambda e, i=i, sp=sp: e.tensor_tensor(
                hct[:, i % 4, :], ht[:, i % 4, :], Cb[:, sp, :], ALU.mult))

        def u_ymm(i):
            dg, sp = units[i]
            hb = 64 * (dg % 2)
            st.c("tensor", lambda e, i=i, dg=dg, sp=sp, hb=hb: e.matmul(
                ps_y[hb:hb + 64, dg % 2, :], sumsel[:, :], hct[:, i % 4, :],
                start=(sp == 0), stop=(sp == 7)))

        def u_yev(i):
            dg, sp = units[i]
            if sp != 7:
                return
            hb = 64 * (dg % 2)
            st.c("vector", lambda e, dg=dg, hb=hb: e.scalar_tensor_tensor(
                yb[hb:hb + 64, dg // 2, :], xcb[hb:hb + 64, dg // 2, :],
                dcol[hb:hb + 64, dg // 2:dg // 2 + 1],
                ps_y[hb:hb + 64, dg % 2, :], ALU.mult, ALU.add))

        ph = [(0, u_pe), (0, u_dA), (1, u_xev), (2, u_b), (3, u_scan),
              (4, u_st), (4, u_hc), (5, u_ymm), (6, u_yev)]
        moff = 6
        nu = len(units)
        for s in range(nu // 2 + moff):
            for off, fn in ph:
                for half in range(2):
                    i = (s - off) * 2 + half
                    if 0 <= i < nu and (s - off) >= 0 and (s - off) < nu // 2:
                        fn(i)
            st.bar()

        def yz_f(j):
            st.c("vector", lambda e, j=j: e.tensor_tensor(
                yzb[:, j, :], yb[:, j, :], szb[:, j, :], ALU.mult))

        pipe(st, 16, [(0, yz_f)])

        for m in range(8):
            st.d("sync", lambda e, m=m: e.dma_start(
                out=ld[:, m, :], in_=x2f_d[128 * m:128 * (m + 1), c0:c0 + CH]))
        st.d("scalar", lambda e: e.dma_start(out=wbuf2[:, 0, :, :],
                                             in_=woutc[:, :, 0:128]))
        st.bar()

        def wo_ld(m):
            if m + 1 < 8:
                st.d("scalar", lambda e, m=m: e.dma_start(
                    out=wbuf2[:, (m + 1) % 2, :, :],
                    in_=woutc[:, :, 128 * (m + 1):128 * (m + 2)]))

        def wo_mm(m):
            def f(e):
                last = None
                for k in range(16):
                    last = e.matmul(pp[m % 2][:], wbuf2[:, m % 2, k, :],
                                    yzb[:, k, :], start=(k == 0), stop=(k == 15))
                return last
            st.c("tensor", f)

        def wo_ev(m):
            st.c("vector", lambda e, m=m: e.tensor_tensor(
                r_f[:, m, :], pp[m % 2][:], ld[:, m, :], ALU.add))

        pipe(st, 8, [(0, wo_ld), (0, wo_mm), (1, wo_ev)])

        emit_ln(nc, st, pp, (ps_stat[:, 0, :], ps_stat[:, 1, :]), r_f, r2, rowm, rowv, rowmm, rowrs,
                rowmrs, onesc, onesr, ld, xcb[:, 0:8, :], cvec)
        for m in range(8):
            st.d("sync", lambda e, m=m: e.dma_start(
                out=x3f_d[128 * m:128 * (m + 1), c0:c0 + CH], in_=ld[:, m, :]))
            st.d("scalar", lambda e, m=m: e.dma_start(
                out=x3b_d[128 * m:128 * (m + 1), c0:c0 + CH], in_=xcb[:, m, :]))
        st.bar()

    for c in range(NCH):
        do_chunk(c)


def build_l1a():
    nc = bass.Bass("TRN2", target_bir_lowering=False)
    with ExitStack() as stk:
        S = stk.enter_context(nc.semaphore("S"))
        S2 = stk.enter_context(nc.semaphore("S2"))
        S3 = stk.enter_context(nc.semaphore("S3"))
        block = stk.enter_context(nc.Block())
        st = Sched()
        emit_l1a(nc, st, stk)
        st.emit(block, S, S2, S3)
    return nc


def l1a_host_inputs(g):
    out = common_consts()
    winT = g["L1_attn_W_in"].T
    out["winr"] = np.ascontiguousarray(
        winT.reshape(8, 128, 4096).transpose(1, 0, 2))
    woutT = g["L1_attn_W_out"].T
    out["woutr"] = np.ascontiguousarray(
        woutT.reshape(16, 128, 1024).transpose(1, 0, 2))
    wx = g["L1_attn_W_x"]
    wxp = np.zeros((2048, 80), np.float32)
    wxp[:, 0:16] = wx[0:16].T
    wxp[:, 32:48] = wx[16:32].T
    wxp[:, 64] = wx[32]
    out["wxp"] = np.ascontiguousarray(
        wxp.reshape(16, 128, 80).transpose(1, 0, 2))
    cw = g["L1_attn_conv_w"][:, 0, :]
    out["cwc"] = np.ascontiguousarray(cw.reshape(16, 128, 4).transpose(1, 0, 2))
    out["cbc"] = np.ascontiguousarray(
        g["L1_attn_conv_b"].reshape(16, 128).T)
    A = g["L1_attn_A_log"]
    ar = np.zeros((128, 256), np.float32)
    for dg in range(32):
        for sp in range(8):
            u = dg * 8 + sp
            for sl in range(2):
                ar[64 * sl:64 * (sl + 1), u] = A[64 * dg:64 * (dg + 1), 2 * sp + sl]
    out["arlog"] = ar
    out["dcol"] = np.ascontiguousarray(
        g["L1_attn_D"].reshape(16, 128).T)
    repm = np.zeros((128, 128), np.float32)
    for a in range(2):
        for k in range(64):
            repm[64 * a + k, :][np.arange(128) % 64 == k] = 1.0
    out["repm"] = repm
    sumsel = np.zeros((128, 64), np.float32)
    for p in range(128):
        sumsel[p, p % 64] = 1.0
    out["sumsel"] = sumsel
    selp = np.zeros((48, 8, 128), np.float32)
    for sp in range(8):
        for p in range(128):
            selp[2 * sp + p // 64, sp, p] = 1.0
            selp[32 + 2 * sp + p // 64, sp, p] = 1.0
    out["selp"] = selp
    return out


def emit_l1f_a(nc, st, stk, pre=None):
    pre = pre or {}
    def dt(nm, shape, dtype, kind=None):
        if nm in pre:
            return pre[nm]
        if kind is None:
            return nc.dram_tensor("p_" + nm, shape, dtype).ap()
        return nc.dram_tensor("p_" + nm, shape, dtype, kind=kind).ap()
    sb = lambda n_, s_, d_=F32: stk.enter_context(nc.sbuf_tensor("p_" + n_, s_, d_))
    ps = lambda n_, s_, d_=F32: stk.enter_context(nc.psum_tensor("p_" + n_, s_, d_))

    x3b_d = dt("x3b", [D, T], BF16, kind="ExternalInput")
    gwr2_d = dt("gwr2", [128, 8, 8], F32, kind="ExternalInput")
    gbb2_d = dt("gbb2", [128, 8], F32, kind="ExternalInput")
    w1r_d = dt("w1r", [8, 128, 8, 1024], F32, kind="ExternalInput")
    w2r_d = dt("w2r", [8, 128, 8, 1024], F32, kind="ExternalInput")
    w3r_d = dt("w3r", [8, 128, 8, 1024], F32, kind="ExternalInput")
    b1c_d = dt("b1c", [128, 8, 8], F32, kind="ExternalInput")
    b2c_d = dt("b2c", [128, 8, 8], F32, kind="ExternalInput")
    b3c_d = dt("b3c", [128, 8, 8], F32, kind="ExternalInput")
    esel_d = dt("esel", [8, 8, 128], F32, kind="ExternalInput")
    ident_d = dt("ident", [128, 128], F32, kind="ExternalInput")
    onesr_d = dt("onesr", [1, 128], F32, kind="ExternalInput")
    vpart_d = dt("vpart", [128, 8, 8], F32, kind="ExternalOutput")
    indT_d = dt("indT", [8, T], BF16, kind="ExternalOutput")

    w1c = dt("w1c", [8, 128, 8, 1024], BF16)
    w2c = dt("w2c", [8, 128, 8, 1024], BF16)
    w3c = dt("w3c", [8, 128, 8, 1024], BF16)
    rowsc_d = dt("rowsc", [16], F32)

    lgT = sb("lgT", [8, T])
    lg_tm = sb("lg_tm", [128, 64, 8])
    indT_s = sb("indT_s", [8, T], BF16)
    wtT = sb("wtT", [8, T])
    w1s = sb("w1s", [128, 8, 1024], BF16)
    w2s = sb("w2s", [128, 8, 1024], BF16)
    w3s = sb("w3s", [128, 8, 1024], BF16)
    x3bc = sb("x3bc", [128, 2, 8, 512], BF16)
    h1b = sb("h1b", [128, 8, 512], BF16)
    h2b = sb("h2b", [128, 8, 512], BF16)
    ytb = sb("ytb", [128, 4, 512], BF16)
    ysc = sb("ysc", [128, 2, 512], BF16)
    vsc = sb("vsc", [128, 8, 8, 16]); vred = sb("vred", [128, 8, 8])
    tks = sb("tks", [128, 8, 64])
    tkc = sb("tkc", [128, 48])
    mzc = sb("mzc", [8, 2])
    mzr = sb("mzr", [1, 16])
    mzb = sb("mzb", [128, 16])
    gwr2 = sb("gwr2_s", [128, 8, 8], BF16)
    gbb2 = sb("gbb2_s", [128, 8])
    b1c = sb("b1c_s", [128, 8, 8]); b2c = sb("b2c_s", [128, 8, 8])
    b3c = sb("b3c_s", [128, 8, 8])
    esel = sb("esel_s", [8, 8, 128])
    ident = sb("ident_s", [128, 128])
    onesr = sb("onesr_s", [1, 128])

    pp = [ps(f"pp{i}", [128, 512]) for i in range(6)]
    ps_w = ps("ps_w", [128, 512])

    for s_, d_ in [(gbb2, gbb2_d), (b1c, b1c_d), (b2c, b2c_d), (b3c, b3c_d),
                   (esel, esel_d), (ident, ident_d), (onesr, onesr_d)]:
        st.d("sync", lambda e, s=s_, d=d_: e.dma_start(out=s[:], in_=d))
    st.d("gpsimd", lambda e: e.dma_start(out=gwr2[:], in_=gwr2_d))
    st.bar()

    pieces = []
    for e in range(8):
        pieces.append((w1r_d[e], w1c[e]))
        pieces.append((w2r_d[e], w2c[e]))
        pieces.append((w3r_d[e], w3c[e]))
    stg = [w1s[:], w2s[:]]

    def pc_ld(i):
        for a in range(8):
            st.d("gpsimd", lambda e, a=a, i=i: e.dma_start(
                out=stg[i % 2][:, a, :], in_=pieces[i][0][:, a, :],
                max_dma_last_dim=4096))

    def pc_stn(i):
        for a in range(8):
            st.d("sync", lambda e, a=a, i=i: e.dma_start(
                out=pieces[i][1][:, a, :], in_=stg[i % 2][:, a, :]))

    pipe(st, len(pieces), [(0, pc_ld), (1, pc_stn)])

    def lgdma(i):
        if i % 4 == 0:
            c = i // 4
            for k in range(8):
                st.d("sync", lambda e, k=k, c=c: e.dma_start(
                    out=x3bc[:, c % 2, k, :],
                    in_=x3b_d[128 * k:128 * (k + 1), 512 * c:512 * (c + 1)]),
                     slack=0)

    def lg_mm(i):
        def f(e):
            last = None
            for k in range(8):
                last = e.matmul(pp[i % 3][:, 0:8],
                                x3bc[:, (i // 4) % 2, k, 128 * (i % 4):128 * (i % 4 + 1)],
                                gwr2[:, k, :], start=(k == 0), stop=(k == 7))
            return last
        st.c("tensor", f)

    def lg_add(i):
        st.c("vector", lambda e, i=i: e.tensor_tensor(
            lg_tm[:, i, :], pp[i % 3][:, 0:8], gbb2[:], ALU.add))

    def lg_tp(i):
        st.c("tensor", lambda e, i=i: e.transpose(
            pp[3 + (i // 4) % 2][0:8, 128 * (i % 4):128 * (i % 4 + 1)],
            lg_tm[:, i, :], ident[:]))

    def lg_ev(i):
        if i % 4 == 3:
            q = i // 4
            st.c("vector", lambda e, q=q: e.tensor_copy(
                lgT[:, 512 * q:512 * (q + 1)], pp[3 + q % 2][0:8, :]))

    pipe(st, 64, [(0, lgdma), (2, lg_mm), (3, lg_add), (4, lg_tp), (5, lg_ev)])

    st.c("vector", lambda e: e.tensor_reduce(mzc[:, 0:1], lgT[:], AX.X, ALU.max,
                                             negate=True))
    st.bar()
    st.c("scalar", lambda e: e.activation(lgT[:], lgT[:], AF.Exp,
                                          bias=mzc[:, 0:1],
                                          accum_out=mzc[:, 1:2]))
    st.bar()
    st.d("scalar", lambda e: e.dma_start(out=rowsc_d[0:8], in_=mzc[:, 0]))
    st.d("scalar", lambda e: e.dma_start(out=rowsc_d[8:16], in_=mzc[:, 1]))
    st.bar()
    st.d("scalar", lambda e: e.dma_start(out=mzr[:], in_=rowsc_d))
    st.bar()
    st.c("vector", lambda e: e.reciprocal(mzr[:, 8:16], mzr[:, 8:16]))
    st.bar()
    st.c("tensor", lambda e: e.matmul(pp[0][:, 0:16], onesr[:], mzr[:],
                                      start=True, stop=True))
    st.bar()
    st.c("vector", lambda e: e.tensor_copy(mzb[:], pp[0][:, 0:16]))
    st.bar()

    def _tk(i, k):
        return tks[:, i % 8, 8 * k:8 * k + 8]

    def _tkc(i, k):
        return tkc[:, 6 * (i % 8) + k:6 * (i % 8) + k + 1]

    def tk_sub(i):
        st.c("vector", lambda e, i=i: e.tensor_tensor(
            _tk(i, 0), lg_tm[:, i, :], mzb[:, 0:8], ALU.add))

    def tk_exp(i):
        st.c("scalar", lambda e, i=i: e.activation(_tk(i, 1), _tk(i, 0), AF.Exp))

    def tk_pr(i):
        st.c("vector", lambda e, i=i: e.tensor_tensor(
            _tk(i, 2), _tk(i, 1), mzb[:, 8:16], ALU.mult))

    def tk_mx1(i):
        st.c("vector", lambda e, i=i: e.tensor_reduce(
            _tkc(i, 0), _tk(i, 2), AX.X, ALU.max))

    def tk_m1(i):
        st.c("vector", lambda e, i=i: e.tensor_scalar(
            _tk(i, 3), _tk(i, 2), _tkc(i, 0), None, ALU.is_equal))

    def tk_b2(i):
        st.c("vector", lambda e, i=i: e.scalar_tensor_tensor(
            _tk(i, 4), _tk(i, 3), -1e30, _tk(i, 2), ALU.mult, ALU.add))

    def tk_mx2(i):
        st.c("vector", lambda e, i=i: e.tensor_reduce(
            _tkc(i, 1), _tk(i, 4), AX.X, ALU.max))

    def tk_m2(i):
        st.c("vector", lambda e, i=i: e.tensor_scalar(
            _tk(i, 5), _tk(i, 4), _tkc(i, 1), None, ALU.is_equal))

    def tk_m12(i):
        st.c("vector", lambda e, i=i: e.tensor_tensor(
            _tk(i, 5), _tk(i, 3), _tk(i, 5), ALU.add))

    def tk_wt(i):
        st.c("vector", lambda e, i=i: e.tensor_tensor(
            _tk(i, 6), _tk(i, 2), _tk(i, 5), ALU.mult))

    def tk_tp(i):
        def f(e):
            e.transpose(pp[3 + (i // 4) % 2][0:8, 128 * (i % 4):128 * (i % 4 + 1)],
                        _tk(i, 5), ident[:])
            return e.transpose(
                pp[(i // 4) % 2][0:8, 128 * (i % 4):128 * (i % 4 + 1)],
                _tk(i, 6), ident[:])
        st.c("tensor", f)

    def tk_ev(i):
        if i % 4 == 3:
            q = i // 4
            st.c("vector", lambda e, q=q: e.tensor_copy(
                indT_s[:, 512 * q:512 * (q + 1)], pp[3 + q % 2][0:8, :]))
            st.c("scalar", lambda e, q=q: e.copy(
                wtT[:, 512 * q:512 * (q + 1)], pp[q % 2][0:8, :]))

    pipe(st, 64, [(0, tk_sub), (1, tk_exp), (2, tk_pr), (3, tk_mx1),
                  (4, tk_m1), (5, tk_b2), (6, tk_mx2), (7, tk_m2),
                  (8, tk_m12), (9, tk_wt), (10, tk_tp), (11, tk_ev)])
    for q in range(16):
        st.d("scalar", lambda e, q=q: e.dma_start(
            out=indT_d[:, 512 * q:512 * (q + 1)],
            in_=indT_s[:, 512 * q:512 * (q + 1)]))
    st.c("vector", lambda e: e.memset(vsc[:], 0.0))
    st.bar()

    def do_expert(e_i):
        st.d("scalar", lambda e: e.dma_start(out=w1s[:], in_=w1c[e_i]))
        st.d("scalar", lambda e: e.dma_start(out=w2s[:], in_=w2c[e_i]))
        st.d("scalar", lambda e: e.dma_start(out=w3s[:], in_=w3c[e_i]))
        st.bar()

        def do_chunk(c):
            c0 = c * CH
            for k in range(8):
                st.d("sync", lambda e, k=k: e.dma_start(
                    out=x3bc[:, 0, k, :], in_=x3b_d[128 * k:128 * (k + 1), c0:c0 + CH]))
            st.c("tensor", lambda e: e.matmul(
                ps_w[:], esel[:, e_i, :], wtT[:, c0:c0 + CH], start=True, stop=True))
            st.bar()

            def h_mm(e, src, wsrc, m, i):
                last = None
                for k in range(8):
                    last = e.matmul(pp[i % 6][:], wsrc[:, k, 128 * m:128 * (m + 1)],
                                    src[:, k, :], start=(k == 0), stop=(k == 7))
                return last

            def h1_mm(s):
                def f(e):
                    h_mm(e, x3bc[:, 0], w1s, 2 * s, 2 * s)
                    return h_mm(e, x3bc[:, 0], w1s, 2 * s + 1, 2 * s + 1)
                st.c("tensor", f)

            def h1_ev(s):
                for m in (2 * s, 2 * s + 1):
                    st.c("scalar", lambda e, m=m: e.activation(
                        h1b[:, m, :], pp[m % 6][:], AF.Relu, bias=b1c[:, e_i, m:m + 1]))

            pipe(st, 4, [(0, h1_mm), (1, h1_ev)])

            def h2_mm(s):
                def f(e):
                    h_mm(e, h1b, w2s, 2 * s, 2 * s)
                    return h_mm(e, h1b, w2s, 2 * s + 1, 2 * s + 1)
                st.c("tensor", f)

            def h2_ev(s):
                for m in (2 * s, 2 * s + 1):
                    st.c("scalar", lambda e, m=m: e.activation(
                        h2b[:, m, :], pp[m % 6][:], AF.Relu, bias=b2c[:, e_i, m:m + 1]))

            pipe(st, 4, [(0, h2_mm), (1, h2_ev)])

            def y_mm(s):
                def f(e):
                    h_mm(e, h2b, w3s, 2 * s, 2 * s)
                    return h_mm(e, h2b, w3s, 2 * s + 1, 2 * s + 1)
                st.c("tensor", f)

            def y_ev(s):
                for m in (2 * s, 2 * s + 1):
                    st.c("scalar", lambda e, m=m: e.activation(
                        ytb[:, m % 4, :], pp[m % 6][:], AF.Relu,
                        bias=b3c[:, e_i, m:m + 1]))

            def y_acc(s):
                for m in (2 * s, 2 * s + 1):
                    st.c("vector", lambda e, m=m: e.scalar_tensor_tensor(
                        ysc[:, m % 2, :], ytb[:, m % 4, :], 1.0, ps_w[:],
                        ALU.mult, ALU.mult,
                        accum_out=vsc[:, e_i, m, c:c + 1]))

            pipe(st, 4, [(0, y_mm), (1, y_ev), (2, y_acc)])

        for c in range(NCH):
            do_chunk(c)

    for e_i in range(8):
        do_expert(e_i)

    st.c("vector", lambda e: e.tensor_reduce(vred[:], vsc[:], AX.X, ALU.add))
    st.bar()
    st.d("sync", lambda e: e.dma_start(out=vpart_d, in_=vred[:]))
    st.bar()


def emit_l1f_b(nc, st, stk, pre=None, nvp=8):
    pre = pre or {}
    def dt(nm, shape, dtype, kind=None):
        if nm in pre:
            return pre[nm]
        if kind is None:
            return nc.dram_tensor("b_" + nm, shape, dtype).ap()
        return nc.dram_tensor("b_" + nm, shape, dtype, kind=kind).ap()
    sb = lambda n_, s_, d_=F32: stk.enter_context(nc.sbuf_tensor("b_" + n_, s_, d_))
    ps = lambda n_, s_, d_=F32: stk.enter_context(nc.psum_tensor("b_" + n_, s_, d_))

    x3f_d = dt("x3f", [D, T], F32, kind="ExternalInput")
    indT_d = dt("indT", [8, T], BF16, kind="ExternalInput")
    vp_d = [dt(f"vp{i}", [128, 8, 8], F32, kind="ExternalInput")
            for i in range(nvp)]
    ident_d = dt("ident", [128, 128], F32, kind="ExternalInput")
    onesr_d = dt("onesr", [1, 128], F32, kind="ExternalInput")
    onesc_d = dt("onesc", [128, 1], F32, kind="ExternalInput")
    cvec_d = dt("cvec", [128, 8], F32, kind="ExternalInput")
    x4_d = dt("x4", [T, D], F32, kind="ExternalOutput")

    vps = sb("vps", [128, nvp, 8, 8])
    vsum = sb("vsum", [128, 8, 8])
    vT = sb("vT", [8, 8, 128], BF16)
    indT_s = sb("indT_s", [8, T], BF16)
    ld = sb("ld", [128, 8, 512])
    r_f = sb("r_f", [128, 8, 512])
    o_tm = sb("o_tm", [128, 4, 1024])
    ident = sb("ident_s", [128, 128])
    onesr = sb("onesr_s", [1, 128]); onesc = sb("onesc_s", [128, 1])
    cvec = sb("cvec_s", [128, 8])
    rowm = sb("rowm", [1, 512]); rowv = sb("rowv", [1, 512])
    rowmm = sb("rowmm", [1, 512]); rowrs = sb("rowrs", [1, 512])
    rowmrs = sb("rowmrs", [1, 512])
    r2 = sb("r2", [128, 2, 512])

    pp = [ps(f"pp{i}", [128, 512]) for i in range(6)]
    ps_stat = ps("ps_stat", [1, 2, 512])

    for s_, d_ in [(ident, ident_d), (onesr, onesr_d), (onesc, onesc_d),
                   (cvec, cvec_d)]:
        st.d("sync", lambda e, s=s_, d=d_: e.dma_start(out=s[:], in_=d))
    for i in range(nvp):
        st.d("sync", lambda e, i=i: e.dma_start(out=vps[:, i, :, :], in_=vp_d[i]))
    st.d("scalar", lambda e: e.dma_start(out=indT_s[:], in_=indT_d))
    st.bar()
    stp = nvp // 2
    while stp >= 1:
        for i in range(stp):
            st.c("vector", lambda e, i=i, stp=stp: e.tensor_tensor(
                vps[:, i, :, :], vps[:, i, :, :], vps[:, i + stp, :, :], ALU.add))
        st.bar()
        stp //= 2
    st.c("vector", lambda e: e.tensor_copy(vsum[:], vps[:, 0, :, :]))
    st.bar()
    for m in range(8):
        st.c("tensor", lambda e, m=m: e.transpose(
            pp[m % 2][0:8, 0:128], vsum[:, :, m], ident[:]))
        st.bar()
        st.c("scalar", lambda e, m=m: e.copy(vT[:, m, :], pp[m % 2][0:8, 0:128]))
        st.bar()

    def do_chunk(c):
        c0 = c * CH
        for m in range(8):
            st.d("sync", lambda e, m=m: e.dma_start(
                out=ld[:, m, :], in_=x3f_d[128 * m:128 * (m + 1), c0:c0 + CH]))
        st.bar()

        def o_mm(m):
            st.c("tensor", lambda e, m=m: e.matmul(
                pp[m % 2][:], vT[:, m, :], indT_s[:, c0:c0 + CH],
                start=True, stop=True))

        def o_ev(m):
            st.c("vector", lambda e, m=m: e.tensor_tensor(
                r_f[:, m, :], pp[m % 2][:], ld[:, m, :], ALU.add))

        pipe(st, 8, [(0, o_mm), (1, o_ev)])
        emit_ln(nc, st, pp, (ps_stat[:, 0, :], ps_stat[:, 1, :]), r_f, r2, rowm, rowv, rowmm, rowrs,
                rowmrs, onesc, onesr, ld, None, cvec, outb_off=True)

        def t_mm(i):
            tt_, m = i // 8, i % 8
            st.c("tensor", lambda e, tt_=tt_, m=m, i=i: e.transpose(
                pp[2 + i % 2][:, 0:128], ld[:, m, 128 * tt_:128 * (tt_ + 1)],
                ident[:]))

        def t_ev(i):
            tt_, m = i // 8, i % 8
            st.c("vector", lambda e, tt_=tt_, m=m, i=i: e.tensor_copy(
                o_tm[:, tt_, 128 * m:128 * (m + 1)], pp[2 + i % 2][:, 0:128]))

        def t_st(i):
            tt_, m = i // 8, i % 8
            if m == 7:
                st.d("sync", lambda e, tt_=tt_: e.dma_start(
                    out=x4_d[c0 + 128 * tt_:c0 + 128 * (tt_ + 1), :],
                    in_=o_tm[:, tt_, :]))

        pipe(st, 32, [(0, t_mm), (1, t_ev), (2, t_st)])

    for c in range(NCH):
        do_chunk(c)


def build_l1f_a():
    nc = bass.Bass("TRN2", target_bir_lowering=False)
    with ExitStack() as stk:
        S = stk.enter_context(nc.semaphore("S"))
        S2 = stk.enter_context(nc.semaphore("S2"))
        S3 = stk.enter_context(nc.semaphore("S3"))
        block = stk.enter_context(nc.Block())
        st = Sched()
        emit_l1f_a(nc, st, stk)
        st.emit(block, S, S2, S3)
    return nc


def build_l1f_b():
    nc = bass.Bass("TRN2", target_bir_lowering=False)
    with ExitStack() as stk:
        S = stk.enter_context(nc.semaphore("S"))
        S2 = stk.enter_context(nc.semaphore("S2"))
        S3 = stk.enter_context(nc.semaphore("S3"))
        block = stk.enter_context(nc.Block())
        st = Sched()
        emit_l1f_b(nc, st, stk)
        st.emit(block, S, S2, S3)
    return nc


def l1f_host_inputs(g):
    out = common_consts()
    out["gwr2"] = np.ascontiguousarray(
        g["L1_ffn_gate_w"].reshape(8, 128, 8).transpose(1, 0, 2))
    out["gbb2"] = np.ascontiguousarray(
        np.broadcast_to(g["L1_ffn_gate_b"], (128, 8)))
    re3 = lambda w: np.ascontiguousarray(
        w.reshape(8, 8, 128, 1024).transpose(0, 2, 1, 3))
    out["w1r"] = re3(g["L1_ffn_w1"]); out["w2r"] = re3(g["L1_ffn_w2"])
    out["w3r"] = re3(g["L1_ffn_w3"])
    colc = lambda b: np.ascontiguousarray(
        b.reshape(8, 8, 128).transpose(2, 0, 1))
    out["b1c"] = colc(g["L1_ffn_b1"]); out["b2c"] = colc(g["L1_ffn_b2"])
    out["b3c"] = colc(g["L1_ffn_b3"])
    es = np.zeros((8, 8, 128), np.float32)
    for e in range(8):
        es[e, e, :] = 1.0
    out["esel"] = es
    return out


def build_l0a():
    nc = bass.Bass("TRN2", target_bir_lowering=False)
    with ExitStack() as stk:
        S = stk.enter_context(nc.semaphore("S"))
        S2 = stk.enter_context(nc.semaphore("S2"))
        S3 = stk.enter_context(nc.semaphore("S3"))
        block = stk.enter_context(nc.Block())
        st = Sched()
        emit_l0a(nc, st, stk)
        st.emit(block, S, S2, S3)
    return nc


def l0a_host_inputs(common):
    g = common
    out = {}
    re_w = lambda w: np.ascontiguousarray(
        w.reshape(8, 128, 1024).transpose(1, 0, 2))
    out["wq"] = re_w(g["L0_attn_Wq"]); out["wk"] = re_w(g["L0_attn_Wk"])
    out["wv"] = re_w(g["L0_attn_Wv"]); out["wo"] = re_w(g["L0_attn_Wo"])
    col = lambda b: np.ascontiguousarray(b.reshape(8, 128).T)
    out["bqc"] = col(g["L0_attn_bq"]); out["bkc"] = col(g["L0_attn_bk"])
    out["boc"] = col(g["L0_attn_bo"])
    out["bkb"] = np.ascontiguousarray(np.broadcast_to(g["L0_attn_bk"], (128, 1024)))
    out["bvb"] = np.ascontiguousarray(np.broadcast_to(g["L0_attn_bv"], (128, 1024)))
    ang = (np.pi / (2.0 * T)) * np.arange(T, dtype=np.float32)
    cosw = np.cos(ang).astype(np.float32); sinw = np.sin(ang).astype(np.float32)
    out["cosb"] = np.ascontiguousarray(np.broadcast_to(cosw, (128, T)))
    out["sinb"] = np.ascontiguousarray(np.broadcast_to(sinw, (128, T)))
    out["ccol"] = np.ascontiguousarray(cosw.reshape(64, 128).T)
    out["scol"] = np.ascontiguousarray(sinw.reshape(64, 128).T)
    u = np.arange(128)[:, None]
    tloc = np.arange(512)[None, :]
    msk = np.stack([(128 * ub + u <= tloc).astype(np.float32) for ub in range(4)], 1)
    out["mask"] = np.ascontiguousarray(msk.astype(BF))
    out["ident"] = np.eye(128, dtype=np.float32)
    out["ones64"] = np.ones((1, 64), np.float32)
    out["onesr"] = np.ones((1, 128), np.float32)
    out["onesc"] = np.ones((128, 1), np.float32)
    cv = np.zeros((128, 8), np.float32)
    cv[:, 0] = LN_EPS; cv[:, 1] = -0.5; cv[:, 2] = -1.0; cv[:, 3] = 1.0
    cv[:, 4] = 1.702
    out["cvec"] = cv
    return out


def common_consts():
    out = {}
    out["ident"] = np.eye(128, dtype=np.float32)
    out["onesr"] = np.ones((1, 128), np.float32)
    out["onesc"] = np.ones((128, 1), np.float32)
    cv = np.zeros((128, 8), np.float32)
    cv[:, 0] = LN_EPS; cv[:, 1] = -0.5; cv[:, 2] = -1.0; cv[:, 3] = 1.0
    cv[:, 4] = 1.702
    out["cvec"] = cv
    return out


_PERM = np.concatenate([np.arange(0, 4096, 2), np.arange(1, 4096, 2)])


def l0f_host_inputs(g):
    out = common_consts()
    w1p = g["L0_ffn_w1"][:, :, _PERM]
    w3p = g["L0_ffn_w3"][:, :, _PERM]
    out["w1r"] = np.ascontiguousarray(
        w1p.reshape(8, 8, 128, 4096).transpose(0, 2, 1, 3))
    out["w3r"] = np.ascontiguousarray(
        w3p.reshape(8, 8, 128, 4096).transpose(0, 2, 1, 3))
    out["w2r"] = np.ascontiguousarray(
        g["L0_ffn_w2"].reshape(8, 16, 128, 1024).transpose(0, 2, 1, 3))
    sw1p = g["L0_ffn_sw1"][:, _PERM]; sw3p = g["L0_ffn_sw3"][:, _PERM]
    out["sw1r"] = np.ascontiguousarray(sw1p.reshape(8, 128, 4096).transpose(1, 0, 2))
    out["sw3r"] = np.ascontiguousarray(sw3p.reshape(8, 128, 4096).transpose(1, 0, 2))
    out["sw2r"] = np.ascontiguousarray(
        g["L0_ffn_sw2"].reshape(16, 128, 1024).transpose(1, 0, 2))
    b1p = g["L0_ffn_b1"][:, _PERM]; b3p = g["L0_ffn_b3"][:, _PERM]
    b1c = b1p.reshape(8, 32, 128).transpose(2, 0, 1)
    b3c = b3p.reshape(8, 32, 128).transpose(2, 0, 1)
    sb1 = g["L0_ffn_sb1"][_PERM].reshape(32, 128).T[:, None, :]
    sb3 = g["L0_ffn_sb3"][_PERM].reshape(32, 128).T[:, None, :]
    out["b1c"] = np.ascontiguousarray(np.concatenate([b1c, sb1], 1))
    out["b3c"] = np.ascontiguousarray(np.concatenate([b3c, sb3], 1))
    b2c = g["L0_ffn_b2"].reshape(8, 8, 128).transpose(2, 0, 1)
    sb2 = g["L0_ffn_sb2"].reshape(8, 128).T[:, None, :]
    out["b2c"] = np.ascontiguousarray(np.concatenate([b2c, sb2], 1))
    out["gwr"] = np.ascontiguousarray(
        g["L0_ffn_gate_w"].T.reshape(8, 128, 8).transpose(1, 0, 2))
    out["gbb"] = np.ascontiguousarray(
        np.broadcast_to(g["L0_ffn_gate_b"], (128, 8)))
    es = np.zeros((8, 8, 128), np.float32)
    for e in range(8):
        es[e, e, :] = 1.0
    out["esel"] = es
    return out


def build_fused():
    nc = bass.Bass("TRN2", target_bir_lowering=False)
    x1f = nc.dram_tensor("x1f", [D, T], F32).ap()
    x1b = nc.dram_tensor("x1b", [D, T], BF16).ap()
    x2f = nc.dram_tensor("x2f", [D, T], F32).ap()
    x2b = nc.dram_tensor("x2b", [D, T], BF16).ap()
    x3f = nc.dram_tensor("x3f", [D, T], F32).ap()
    x3b = nc.dram_tensor("x3b", [D, T], BF16).ap()
    vpart = nc.dram_tensor("vpart", [128, 8, 8], F32).ap()
    vsum = nc.dram_tensor("vsum", [128, 64], F32, addr_space="Shared").ap()
    indT = nc.dram_tensor("indT", [8, T], BF16).ap()
    with nc.semaphore("S") as S, nc.semaphore("S2") as S2, \
            nc.semaphore("S3") as S3, nc.Block() as block:
        st = Sched()
        with ExitStack() as s1:
            emit_l0a(nc, st, s1, pre={"x1f": x1f, "x1b": x1b})
            st.bar()
        with ExitStack() as s2:
            emit_l0f(nc, st, s2, pre={"x1f": x1f, "x1b": x1b,
                                      "x2f": x2f, "x2b": x2b})
            st.bar()
        with ExitStack() as s3:
            emit_l1a(nc, st, s3, pre={"x2f": x2f, "x2b": x2b,
                                      "x3f": x3f, "x3b": x3b})
            st.bar()
        with ExitStack() as s4:
            emit_l1f_a(nc, st, s4, pre={"x3b": x3b, "vpart": vpart,
                                        "indT": indT})
            st.bar()
        st.c("gpsimd", lambda e: e.collective_compute(
            "AllReduce", ALU.add, replica_groups=[list(range(8))],
            ins=[vpart.rearrange("p a b -> p (a b)")], outs=[vsum]))
        st.bar()
        with ExitStack() as s5:
            emit_l1f_b(nc, st, s5, nvp=1,
                       pre={"x3f": x3f, "indT": indT,
                            "vp0": vsum.rearrange("p (a b) -> p a b", b=8)})
            st.bar()
        st.emit(block, S, S2, S3)
    return nc


def kernel(**inputs):
    n = 8
    host_a = l0a_host_inputs(inputs)
    host_f = l0f_host_inputs(inputs)
    host_m = l1a_host_inputs(inputs)
    host_m.pop("ident", None)
    host_p = l1f_host_inputs(inputs)
    hp_a = {k: host_p[k] for k in ["gwr2", "gbb2", "w1r", "w2r", "w3r",
                                   "b1c", "b2c", "b3c", "esel", "ident", "onesr"]}
    hp_b = {k: host_p[k] for k in ["ident", "onesr", "onesc", "cvec"]}
    base = {}
    for pfx, h in [("a", host_a), ("f", host_f), ("m", host_m),
                   ("p", hp_a), ("b", hp_b)]:
        for k, v in h.items():
            base[f"{pfx}_{k}"] = v
    nc = build_fused()
    in_maps = []
    for b in range(n):
        im = dict(base)
        im["a_x0"] = np.ascontiguousarray(inputs["L0_attn_x"][b])
        in_maps.append(im)
    res = run_bass_kernel_spmd(nc, in_maps, core_ids=list(range(n))).results
    return np.stack([np.asarray(res[b]["b_x4"]) for b in range(n)], axis=0)


def kernel_multi(**inputs):
    n = 8
    cores = list(range(n))
    x0s = [np.ascontiguousarray(inputs["L0_attn_x"][b]) for b in range(n)]

    host_a = l0a_host_inputs(inputs)
    nc1 = build_l0a()
    r1 = run_bass_kernel_spmd(
        nc1, [dict(host_a, x0=x0s[b]) for b in range(n)], core_ids=cores).results

    host_f = l0f_host_inputs(inputs)
    nc2 = build_l0f()
    r2 = run_bass_kernel_spmd(
        nc2, [dict(host_f, x1f=np.asarray(r1[b]["x1f"]),
                   x1b=np.asarray(r1[b]["x1b"])) for b in range(n)],
        core_ids=cores).results
    del r1

    host_m = l1a_host_inputs(inputs)
    nc3 = build_l1a()
    r3 = run_bass_kernel_spmd(
        nc3, [dict(host_m, x2f=np.asarray(r2[b]["x2f"]),
                   x2b=np.asarray(r2[b]["x2b"])) for b in range(n)],
        core_ids=cores).results
    del r2

    host_p = l1f_host_inputs(inputs)
    hp_a = {k: host_p[k] for k in ["gwr2", "gbb2", "w1r", "w2r", "w3r",
                                   "b1c", "b2c", "b3c", "esel", "ident", "onesr"]}
    nc4 = build_l1f_a()
    r4 = run_bass_kernel_spmd(
        nc4, [dict(hp_a, x3b=np.asarray(r3[b]["x3b"])) for b in range(n)],
        core_ids=cores).results

    vparts = [np.asarray(r4[b]["vpart"]) for b in range(n)]
    hp_b = {k: host_p[k] for k in ["ident", "onesr", "onesc", "cvec"]}
    nc5 = build_l1f_b()
    in5 = []
    for b in range(n):
        m = dict(hp_b, x3f=np.asarray(r3[b]["x3f"]),
                 indT=np.asarray(r4[b]["indT"]))
        for i in range(n):
            m[f"vp{i}"] = vparts[i]
        in5.append(m)
    r5 = run_bass_kernel_spmd(nc5, in5, core_ids=cores).results
    return np.stack([np.asarray(r5[b]["x4"]) for b in range(n)], axis=0)
